# version 01
# baseline (speedup 1.0000x reference)
import numpy as np
import ml_dtypes
import concourse.bass as bass
import concourse.mybir as mybir
from concourse.bass_utils import run_bass_kernel_spmd
from concourse.bass_core_rust import IndirectOffsetOnAxis
from contextlib import ExitStack

F32 = mybir.dt.float32
BF16 = mybir.dt.bfloat16
I32 = mybir.dt.int32
AF = mybir.ActivationFunctionType
ALU = mybir.AluOpType
AX = mybir.AxisListType

N_CORES = 8
B, T, D = 16, 2048, 1024
SEQ_PER_CORE = B // N_CORES
TOK = SEQ_PER_CORE * T
H, DH = 16, 64
E, TOPK, CAP = 64, 6, 512
I_EXP = 704
SI = 1408
EPS_LN = 1e-5
EPS_RMS = 1e-6
SHIFT = 12.0

BF = ml_dtypes.bfloat16


ENGINES = ("sync", "tensor", "vector", "scalar", "gpsimd")


class PySem:
    __slots__ = ("name", "v", "h")

    def __init__(self, name):
        self.name = name
        self.v = 0
        self.h = None


class Sched:

    def __init__(self, serialize=False, chained=False, ser_set=None):
        self.ser_set = set(ser_set) if ser_set else None
        self.items = {e: [] for e in ENGINES}
        self.sems = []
        self.fence_buf = None
        self.chained = chained
        self.chain_sems = {}
        self.serialize = serialize
        self.eng_sems = {}

    def sem(self, name):
        s = PySem(name)
        self.sems.append(s)
        return s

    def op(self, eng, fn, waits=(), incs=(), isdma=False):
        if self.chained:
            if eng not in self.chain_sems:
                self.chain_sems[eng] = self.sem(f"ch_{eng}")
                self.chain_last = getattr(self, "chain_last", {})
                self.chain_last[eng] = None
            last = self.chain_last[eng]
            if last is not None:
                waits = list(waits) + [last]
            if incs:
                s0, n0 = incs[0]
                self.chain_last[eng] = (s0, s0.v + n0)
            else:
                cs = self.chain_sems[eng]
                incs = list(incs) + [(cs, 16 if isdma else 1)]
                self.chain_last[eng] = (cs, cs.v + (16 if isdma else 1))
        if self.serialize:
            if not self.eng_sems:
                for e2 in ENGINES:
                    self.eng_sems[e2] = self.sem(f"ser_{e2}")
                self.last = {e2: None for e2 in ENGINES}
            if self.ser_set and "PEWAIT_ONLY" in self.ser_set:
                if eng in ("vector", "scalar"):
                    pesem = next((x for x in self.sems if x.name == "pe"), None)
                    if pesem is not None and pesem.v:
                        waits = list(waits) + [(pesem, pesem.v)]
                for (s_, n) in incs:
                    s_.v += n
                if eng in ("vector", "scalar") and incs and not isdma:
                    self.items[eng].append((fn, tuple(waits), ()))
                    self.items[eng].append((self._fence_fn(eng), (), tuple(incs)))
                else:
                    self.items[eng].append((fn, tuple(waits), tuple(incs)))
                return
            if self.ser_set and "DIR_PE_WAITS" in self.ser_set:
                extra = [w for e2, w in self.last.items()
                         if e2 != eng and w] if eng == "tensor" else []
            elif self.ser_set and "DIR_WAITS_PE" in self.ser_set:
                extra = [self.last["tensor"]] if (eng != "tensor" and self.last.get("tensor")) else []
            else:
                extra = [w for e2, w in self.last.items()
                         if e2 != eng and w and (self.ser_set is None or e2 in self.ser_set
                                                 or eng in self.ser_set)]
            waits = list(waits) + extra
        for (s, n) in incs:
            s.v += n
        if eng in ("vector", "scalar") and incs and not isdma:
            self.items[eng].append((fn, tuple(waits), ()))
            self.items[eng].append((self._fence_fn(eng), (), tuple(incs)))
        else:
            self.items[eng].append((fn, tuple(waits), tuple(incs)))
        if self.serialize:
            if isdma and incs:
                self.last[eng] = (incs[0][0], incs[0][0].v)
            else:
                ser = self.eng_sems[eng]
                ser.v += 1
                self.items[eng].append((self._ser_fence_fn(eng), (), ((ser, 1),)))
                self.last[eng] = (ser, ser.v)

    def _ser_fence_fn(self, eng):
        if eng in ("vector", "scalar"):
            return self._fence_fn(eng)
        if eng == "tensor":
            return lambda e: e.matmul(self.ser_junk, self.ser_ones1, self.ser_onest,
                                      start=True, stop=True)
        return lambda e: e.nop()

    def _fence_fn(self, eng):
        if self.fence_buf is None:
            raise RuntimeError("fence buffer not set")
        fb = self.fence_buf
        if eng == "vector":
            return lambda e: e.memset(fb[:, 0:1], 0.0)
        return lambda e: e.activation(out=fb[:, 1:2], in_=fb[:, 1:2], func=AF.Identity)

    def dma(self, eng, out, in_, waits=(), sem=None, **kw):
        incs = [(sem, 16)] if sem is not None else []
        self.op(eng, lambda e: e.dma_start(out=out, in_=in_, **kw), waits, incs, isdma=True)
        return sem.v if sem is not None else None

    def build(self, nc, ctx):
        for s in self.sems:
            s.h = ctx.enter_context(nc.semaphore(s.name))
        block = ctx.enter_context(nc.Block())
        for eng_name in ENGINES:
            items = self.items[eng_name]
            if not items:
                continue

            def make_body(items):
                def body(eng):
                    for fn, waits, incs in items:
                        for (s, v) in waits:
                            eng.wait_ge(s.h, v)
                        inst = fn(eng)
                        for (s, n) in incs:
                            if inst is None:
                                inst = eng.nop()
                            inst.then_inc(s.h, n)
                return body

            getattr(block, eng_name)(make_body(items))


def ceil_div(a, b):
    return (a + b - 1) // b



def emit_ln_chunk(S, bufs, X, tsl, ld_wait, out_ap, eps=EPS_LN, rms=False, nw=None):
    (pe, dv, sc) = bufs["pe"], bufs["dv"], bufs["sc"]
    ones1, ones128 = bufs["ones1"], bufs["ones128"]
    sq, row = bufs["sq"], bufs["row"]
    p_m = bufs["p_m"][0:1, :]
    p_sq = bufs["p_sq"][0:1, :]
    p_rb, p_mrb = bufs["p_rb"], bufs["p_mrb"]

    for k in range(8):
        pk = k % 2
        waits = list(ld_wait)
        if k >= 2:
            waits.append((pe, pe.v))
        S.op("scalar", (lambda k=k, pk=pk, tsl=tsl: lambda e: e.activation(
            out=sq[:, pk, :], in_=X[:, k, tsl], func=AF.Square))(),
            waits=waits, incs=[(sc, 1)])
        S.op("tensor", (lambda k=k, tsl=tsl: lambda e: e.matmul(
            p_m[:, :], ones1[:, :], X[:, k, tsl],
            start=(k == 0), stop=(k == 7)))(),
            waits=list(ld_wait) if k == 0 else [])
        S.op("tensor", (lambda k=k, pk=pk: lambda e: e.matmul(
            p_sq[:, :], ones1[:, :], sq[:, pk, :],
            start=(k == 0), stop=(k == 7)))(),
            waits=[(sc, sc.v)], incs=[(pe, 1)])
    pe_stats = emit_drain(S, bufs, bufs["ln_junk"])
    m_r = row[:, 0:512]; ms_r = row[:, 512:1024]; var_r = row[:, 1024:1536]
    r_r = row[:, 1536:2048]; mr_r = row[:, 512:1024]
    S.op("vector", lambda e: e.tensor_single_scalar(m_r, p_m[:, :], 1.0 / D, ALU.mult),
         waits=[(pe, pe_stats)])
    S.op("vector", lambda e: e.tensor_single_scalar(ms_r, p_sq[:, :], 1.0 / D, ALU.mult))
    S.op("vector", lambda e: e.tensor_mul(var_r, m_r, m_r))
    S.op("vector", lambda e: e.tensor_sub(var_r, ms_r, var_r), incs=[(dv, 1)])
    eps_ap = bufs["cb"][0:1, 0:1] if eps == EPS_LN else bufs["cb"][0:1, 2:3]
    S.op("scalar", (lambda var_r=var_r, eps_ap=eps_ap: lambda e: e.activation(
        out=var_r, in_=var_r, func=AF.Ln, bias=eps_ap))(),
        waits=[(dv, dv.v)])
    S.op("scalar", lambda e: e.activation(out=r_r, in_=var_r, func=AF.Exp, scale=-0.5),
         incs=[(sc, 1)])
    if rms:
        S.op("scalar", lambda e: e.activation(out=bufs["tmprow"][0:1, 0:512], in_=var_r,
                                              func=AF.Exp, scale=-1.0), incs=[(sc, 1)])
        S.op("vector", lambda e: e.tensor_single_scalar(
            bufs["tmprow"][0:1, 512:1024], bufs["tmprow"][0:1, 0:512], -EPS_LN, ALU.mult),
            waits=[(sc, sc.v)])
        S.op("vector", lambda e: e.tensor_single_scalar(
            bufs["tmprow"][0:1, 512:1024], bufs["tmprow"][0:1, 512:1024], 1.0, ALU.add),
            incs=[(dv, 1)])
        S.op("scalar", lambda e: e.activation(out=bufs["tmprow"][0:1, 512:1024],
                                              in_=bufs["tmprow"][0:1, 512:1024],
                                              func=AF.Ln, bias=bufs["cb"][0:1, 2:3]),
             waits=[(dv, dv.v)])
        S.op("scalar", lambda e: e.activation(out=bufs["tmprow"][0:1, 512:1024],
                                              in_=bufs["tmprow"][0:1, 512:1024],
                                              func=AF.Exp, scale=-0.5), incs=[(sc, 1)])
        S.op("vector", lambda e: e.tensor_mul(r_r, r_r, bufs["tmprow"][0:1, 512:1024]),
             waits=[(sc, sc.v)])
    S.op("vector", lambda e: e.tensor_mul(mr_r, m_r, r_r),
         waits=[(sc, sc.v)], incs=[(dv, 1)])
    S.op("tensor", lambda e: e.matmul(p_rb[:, :], ones128[:, :], r_r, start=True, stop=True),
         waits=[(dv, dv.v)])
    S.op("tensor", lambda e: e.matmul(p_mrb[:, :], ones128[:, :], mr_r, start=True, stop=True))
    pe_bc = emit_drain(S, bufs, bufs["ln_junk"])
    for k in range(8):
        S.op("vector", (lambda k=k, tsl=tsl: lambda e: e.tensor_mul(
            bufs["tmp8"][:, :], X[:, k, tsl], p_rb[:, :]))(),
            waits=[(pe, pe_bc)] if k == 0 else [])
        if nw is None:
            S.op("vector", (lambda k=k: lambda e: e.tensor_sub(
                out_ap(k), bufs["tmp8"][:, :], p_mrb[:, :]))(),
                incs=[(dv, 1)] if k == 7 else [])
        else:
            S.op("vector", (lambda k=k: lambda e: e.tensor_sub(
                bufs["tmp8"][:, :], bufs["tmp8"][:, :], p_mrb[:, :]))())
            S.op("vector", (lambda k=k: lambda e: e.tensor_scalar(
                out_ap(k), bufs["tmp8"][:, :], nw[:, k:k + 1], None, ALU.mult))(),
                incs=[(dv, 1)] if k == 7 else [])
    S.op("tensor", lambda e: None, waits=[(dv, dv.v)])
    return dv.v


def emit_layernorm(S, nc, bufs, X, xn, seq_len, ld_wait, eps=EPS_LN):
    for tc in range(seq_len // 512):
        tsl = slice(tc * 512, (tc + 1) * 512)
        emit_ln_chunk(S, bufs, X, tsl, ld_wait,
                      (lambda tsl=tsl: lambda k: xn[:, k, tsl])(), eps=eps)
    return bufs["dv"].v


def alloc_common(S, nc, ctx):
    sb = lambda name, shape, dt=F32: ctx.enter_context(nc.sbuf_tensor(name, shape, dt))
    ps = lambda name, shape, dt=F32: ctx.enter_context(nc.psum_tensor(name, shape, dt))
    bufs = {
        "pe": S.sem("pe"), "dv": S.sem("dv"), "sc": S.sem("sc"),
        "ld": S.sem("ld"), "gq": S.sem("gq"), "st": S.sem("st"),
        "ones1": sb("ones1", [128, 1]),
        "ones128": sb("ones128", [1, 128]),
        "onest": sb("onest", [128, 256]),
        "cb": sb("cb", [128, 8]),
        "sq": sb("sq", [128, 2, 512]),
        "row": sb("row", [1, 4 * 512]),
        "tmprow": sb("tmprow", [1, 2 * 512]),
        "tmp8": sb("tmp8", [128, 512]),
    }
    S.fence_buf = sb("fence", [128, 2])
    dv = bufs["dv"]
    S.op("vector", lambda e: e.memset(bufs["ones1"][:, :], 1.0), incs=[(dv, 1)])
    S.op("vector", lambda e: e.memset(bufs["ones128"][:, :], 1.0), incs=[(dv, 1)])
    S.op("vector", lambda e: e.memset(bufs["onest"][:, :], 1.0), incs=[(dv, 1)])
    S.op("vector", lambda e: e.memset(bufs["cb"][:, 0:1], EPS_LN), incs=[(dv, 1)])
    S.op("vector", lambda e: e.memset(bufs["cb"][:, 1:2], -SHIFT), incs=[(dv, 1)])
    S.op("vector", lambda e: e.memset(bufs["cb"][:, 2:3], EPS_RMS), incs=[(dv, 1)])
    bufs["init_dv"] = dv.v
    return bufs, sb, ps



def emit_drain(S, bufs, junk_view):
    pe = bufs["pe"]
    S.op("tensor", lambda e: e.matmul(junk_view, bufs["ones1"][0:1, 0:1],
                                      bufs["onest"][0:1, 0:256], start=True, stop=True),
         incs=[(pe, 1)])
    return pe.v


def emit_attn_head(S, bufs, *, qS, kS, vaug_v, X_add, h, qsl, kt_lo, kt_hi,
                   pt, p1buf, p_scA, p_scB, p_sums, p_o, p_rb2, p_sh,
                   osb, rro, t1, id64, att_junk, mask_sb=None, mask_tiles=(),
                   scale=0.125, c_pass1=8.0, nq=512):
    pe, dv, sc = bufs["pe"], bufs["dv"], bufs["sc"]
    nkt = kt_hi - kt_lo
    for i, kt in enumerate(range(kt_lo, kt_hi)):
        bank = p_scA if i % 2 == 0 else p_scB
        w = [(sc, sc.v)] if i >= 2 else [(dv, dv.v)]
        S.op("tensor", (lambda kt=kt, bank=bank, i=i: lambda e: e.matmul(
            bank[:, 0:nq], kS[0:64, kt * 128:(kt + 1) * 128], qS[0:64, qsl],
            start=True, stop=True))(), waits=w, incs=[(pe, 1)])
        if kt in mask_tiles:
            S.op("vector", (lambda kt=kt, bank=bank: lambda e: e.tensor_add(
                bank[:, 0:nq], bank[:, 0:nq], mask_tiles[kt]))(),
                waits=[(pe, pe.v)], incs=[(dv, 1)])
            S.op("scalar", (lambda i=i, bank=bank: lambda e: e.activation(
                out=p1buf[:, (i % 2) * nq:(i % 2) * nq + nq], in_=bank[:, 0:nq],
                func=AF.Exp, scale=scale / c_pass1))(),
                waits=[(dv, dv.v)], incs=[(sc, 1)])
        else:
            S.op("scalar", (lambda i=i, bank=bank: lambda e: e.activation(
                out=p1buf[:, (i % 2) * nq:(i % 2) * nq + nq], in_=bank[:, 0:nq],
                func=AF.Exp, scale=scale / c_pass1))(),
                waits=[(pe, pe.v)], incs=[(sc, 1)])
        S.op("tensor", (lambda i=i, nkt=nkt: lambda e: e.matmul(
            p_sums[0:1, 0:nq], bufs["onesc_bf"][:, 0:1],
            p1buf[:, (i % 2) * nq:(i % 2) * nq + nq],
            start=(i == 0), stop=(i == nkt - 1)))(),
            waits=[(sc, sc.v)], incs=[(pe, 1)])
    pe_s1 = emit_drain(S, bufs, att_junk)
    S.op("scalar", lambda e: e.activation(out=rro[0:1, 0:nq], in_=p_sums[0:1, 0:nq],
                                          func=AF.Ln),
         waits=[(pe, pe_s1)], incs=[(sc, 1)])
    S.op("vector", (lambda: lambda e: e.tensor_single_scalar(
        rro[0:1, 0:nq], rro[0:1, 0:nq], -c_pass1 / scale, ALU.mult))(),
        waits=[(sc, sc.v)], incs=[(dv, 1)])
    S.op("tensor", lambda e: e.matmul(p_sh[64:65, 0:nq], bufs["ones1"][0:1, 0:1],
                                      rro[0:1, 0:nq], start=True, stop=True),
         waits=[(dv, dv.v)])
    pe_m = emit_drain(S, bufs, att_junk)
    S.op("vector", (lambda: lambda e: e.tensor_copy(qS[64:65, qsl], p_sh[64:65, 0:nq]))(),
         waits=[(pe, pe_m)], incs=[(dv, 1)])
    dv_m = dv.v
    for i, kt in enumerate(range(kt_lo, kt_hi)):
        bank = p_scA if i % 2 == 0 else p_scB
        w = [(dv, dv_m)] if i < 2 else [(sc, sc.v)]
        S.op("tensor", (lambda kt=kt, bank=bank: lambda e: e.matmul(
            bank[:, 0:nq], kS[0:65, kt * 128:(kt + 1) * 128], qS[0:65, qsl],
            start=True, stop=True))(), waits=w, incs=[(pe, 1)])
        if kt in mask_tiles:
            S.op("vector", (lambda kt=kt, bank=bank: lambda e: e.tensor_add(
                bank[:, 0:nq], bank[:, 0:nq], mask_tiles[kt]))(),
                waits=[(pe, pe.v)], incs=[(dv, 1)])
            S.op("scalar", (lambda i=i, bank=bank: lambda e: e.activation(
                out=pt[:, i * nq:(i + 1) * nq], in_=bank[:, 0:nq],
                func=AF.Exp, scale=scale))(),
                waits=[(dv, dv.v)], incs=[(sc, 1)])
        else:
            S.op("scalar", (lambda i=i, bank=bank: lambda e: e.activation(
                out=pt[:, i * nq:(i + 1) * nq], in_=bank[:, 0:nq],
                func=AF.Exp, scale=scale))(),
                waits=[(pe, pe.v)], incs=[(sc, 1)])
    sc_exp = sc.v
    for i, kt in enumerate(range(kt_lo, kt_hi)):
        S.op("tensor", (lambda kt=kt, i=i, nkt=nkt: lambda e: e.matmul(
            p_o[0:65, 0:nq], vaug_v[:, kt, :], pt[:, i * nq:(i + 1) * nq],
            start=(i == 0), stop=(i == nkt - 1)))(),
            waits=[(sc, sc_exp)] if i == 0 else [])
    pe_av = emit_drain(S, bufs, att_junk)
    S.op("vector", lambda e: e.tensor_copy(osb[:, 0:nq], p_o[0:65, 0:nq]),
         waits=[(pe, pe_av)])
    S.op("vector", lambda e: e.tensor_copy(rro[64:65, 0:nq], osb[64:65, 0:nq]))
    S.op("vector", lambda e: e.reciprocal(rro[64:65, 0:nq], rro[64:65, 0:nq]),
         incs=[(dv, 1)])
    S.op("tensor", lambda e: e.matmul(p_rb2[0:64, 0:nq], bufs["onest"][64:65, 0:64],
                                      rro[64:65, 0:nq], start=True, stop=True),
         waits=[(dv, dv.v)])
    pe_rb2 = emit_drain(S, bufs, att_junk)
    S.op("vector", lambda e: e.tensor_mul(t1[:, 0:nq], osb[0:64, 0:nq], p_rb2[0:64, 0:nq]),
         waits=[(pe, pe_rb2)], incs=[(dv, 1)])
    X_add(t1, p_sh, id64)
    S.op("tensor", lambda e: None, waits=[(dv, dv.v)])


def build_stage1(serialize=False, chained=False):
    nc = bass.Bass("TRN2", target_bir_lowering=False, debug=False)
    x0T = nc.dram_tensor("x0T", [D, TOK], F32, kind="ExternalInput").ap()
    wqkT = nc.dram_tensor("wqkT", [D, 2 * D], F32, kind="ExternalInput").ap()
    wvT = nc.dram_tensor("wvT", [D, D], F32, kind="ExternalInput").ap()
    idhi = nc.dram_tensor("idhi", [128, 64], F32, kind="ExternalInput").ap()
    x1T = nc.dram_tensor("x1T", [D, TOK], F32, kind="ExternalOutput").ap()

    S = Sched(serialize=serialize, chained=chained)
    ctx = ExitStack()
    bufs, sb, ps = alloc_common(S, nc, ctx)
    pe, dv, sc = bufs["pe"], bufs["dv"], bufs["sc"]
    ld, gq, st = bufs["ld"], bufs["gq"], bufs["st"]

    X = sb("X", [128, 8, T])
    xn = sb("xn", [128, 8, T], BF16)
    vaug = sb("vaug", [128, H * 16 * 65], BF16)
    qS = sb("qS", [65, 2 * T], BF16)
    kS = sb("kS", [65, 2 * T], BF16)
    pt = sb("pt", [128, 16 * 512], BF16)
    p1buf = sb("p1buf", [128, 2 * 512], BF16)
    wv = sb("wv", [128, 8, 512], BF16)
    wqk = sb("wqk", [128, 2, 8, 256], BF16)
    id64 = sb("id64", [128, 64])
    osb = sb("osb", [65, 512])
    rro = sb("rro", [128, 512])
    bufs["onesc_bf"] = sb("onesc_bf", [128, 1], BF16)
    t1 = bufs["tmp8"][0:64, :]

    P = [ps(f"P{i}", [128, 512]) for i in range(8)]
    bufs["p_m"], bufs["p_sq"], bufs["p_rb"], bufs["p_mrb"] = P[0], P[1], P[2], P[3]
    bufs["ln_junk"] = P[5][0:1, 0:256]
    p_qk = P[4]
    p_scA, p_scB = P[2], P[3]
    p_sums = P[1]
    p_o = P[4]
    p_rb2 = P[6]
    p_sh = P[5]
    att_junk = P[0][0:1, 0:256]

    vaug_vh = vaug.rearrange("p (h t c) -> p h t c", h=H, c=65)
    S.op("vector", lambda e: e.memset(vaug_vh[:, :, :, 64:65], 1.0), incs=[(dv, 1)])
    S.op("vector", lambda e: e.memset(kS[64:65, :], 1.0), incs=[(dv, 1)])
    S.op("vector", lambda e: e.memset(bufs["onesc_bf"][:, :], 1.0), incs=[(dv, 1)])
    init_dv = dv.v

    S.dma("sync", id64[:, :], idhi[:, :], sem=ld)
    ld_id = ld.v


    for s in range(SEQ_PER_CORE):
        t0 = s * T
        x_src = x0T[:, t0:t0 + T].rearrange("(c p) t -> p c t", p=128)
        S.dma("sync", X[:, :, :], x_src, waits=[(st, st.v)], sem=ld)
        ld_x = ld.v

        dv_xn = emit_layernorm(S, nc, bufs, X, xn, T,
                               [(ld, ld_x)] + ([(bufs["dv"], bufs["init_dv"])] if s == 0 else []))

        for half in range(2):
            S.op("gpsimd", lambda e: None, waits=[(pe, pe.v)])
            ld_wv = S.dma("gpsimd", wv[:, :, :],
                          wvT[:, half * 512:(half + 1) * 512].rearrange("(c p) f -> p c f", p=128),
                          sem=gq)
            for tt in range(16):
                for k in range(8):
                    S.op("tensor", (lambda tt=tt, k=k: lambda e: e.matmul(
                        p_qk[:, :], xn[:, k, tt * 128:(tt + 1) * 128],
                        wv[:, k, :],
                        start=(k == 0), stop=(k == 7)))(),
                        waits=([(dv, dv_xn), (gq, ld_wv)] if (tt == 0 and k == 0) else []))
                pe_v = emit_drain(S, bufs, att_junk)
                for hh in range(8):
                    h = half * 8 + hh
                    S.op("vector", (lambda tt=tt, h=h, hh=hh: lambda e: e.tensor_copy(
                        vaug_vh[:, h, tt, 0:64], p_qk[:, hh * 64:(hh + 1) * 64]))(),
                        waits=[(pe, pe_v)] if hh == 0 else [],
                        incs=[(dv, 1)] if hh == 7 else [])
                S.op("tensor", lambda e: None, waits=[(dv, dv.v)])

        for g in range(H // 2):
            gb = g % 2
            wsrc = wqkT[:, g * 256:(g + 1) * 256].rearrange("(c p) f -> p c f", p=128)
            if s > 0 or g >= 2:
                S.op("gpsimd", lambda e: None, waits=[(pe, pe.v)])
            ld_w = S.dma("gpsimd", wqk[:, gb, :, :], wsrc, sem=gq)

            for dst, c0 in ((qS, 0), (kS, 128)):
                for ncx in range(4):
                    for k in range(8):
                        S.op("tensor", (lambda gb=gb, ncx=ncx, k=k, c0=c0: lambda e: e.matmul(
                            p_qk[:, :], wqk[:, gb, k, c0:c0 + 128],
                            xn[:, k, ncx * 512:(ncx + 1) * 512],
                            start=(k == 0), stop=(k == 7)))(),
                            waits=[(gq, ld_w), (ld, ld_id)] if (k == 0 and ncx == 0 and c0 == 0) else [])
                    pe_qk = emit_drain(S, bufs, att_junk)
                    S.op("vector", (lambda dst=dst, ncx=ncx: lambda e: e.tensor_copy(
                        dst[0:64, ncx * 512:(ncx + 1) * 512], p_qk[0:64, :]))(),
                        waits=[(pe, pe_qk)])
                    S.op("vector", (lambda ncx=ncx: lambda e: e.tensor_copy(
                        bufs["tmp8"][64:128, :], p_qk[64:128, :]))(),
                        incs=[(dv, 1)])
                    S.op("tensor", lambda e: e.matmul(
                        p_sh[0:64, :], id64[64:128, :], bufs["tmp8"][64:128, :],
                        start=True, stop=True),
                        waits=[(dv, dv.v)])
                    pe_sh2 = emit_drain(S, bufs, att_junk)
                    S.op("vector", (lambda dst=dst, ncx=ncx: lambda e: e.tensor_copy(
                        dst[0:64, T + ncx * 512:T + (ncx + 1) * 512], p_sh[0:64, :]))(),
                        waits=[(pe, pe_sh2)], incs=[(dv, 1)])
                    S.op("tensor", lambda e: None, waits=[(dv, dv.v)])
            dv_qk = dv.v

            for sub in range(2):
                h = 2 * g + sub
                qSh = qS[:, sub * T:(sub + 1) * T]
                kSh = kS[:, sub * T:(sub + 1) * T]
                c = h // 2
                for qc in range(4):
                    qsl = slice(qc * 512, (qc + 1) * 512)

                    def X_add(t1v, p_shv, id64v, c=c, qsl=qsl, sub=sub):
                        if sub == 0:
                            S.op("vector", (lambda: lambda e: e.tensor_add(
                                X[0:64, c, qsl], X[0:64, c, qsl], t1v[:, 0:512]))(),
                                incs=[(dv, 1)])
                        else:
                            S.op("tensor", (lambda: lambda e: e.matmul(
                                p_shv[64:128, :], id64v[0:64, :],
                                t1v[:, 0:512], start=True, stop=True))(),
                                waits=[(dv, dv.v)])
                            pe_sh = emit_drain(S, bufs, att_junk)
                            S.op("vector", (lambda: lambda e: e.tensor_add(
                                X[64:128, c, qsl], X[64:128, c, qsl], p_shv[64:128, :]))(),
                                waits=[(pe, pe_sh)], incs=[(dv, 1)])

                    emit_attn_head(S, bufs, qS=qSh, kS=kSh, vaug_v=vaug_vh[:, h, :, :],
                                   X_add=X_add, h=h, qsl=qsl, kt_lo=0, kt_hi=16,
                                   pt=pt, p1buf=p1buf, p_scA=p_scA, p_scB=p_scB,
                                   p_sums=p_sums, p_o=p_o, p_rb2=p_rb2, p_sh=p_sh,
                                   osb=osb, rro=rro, t1=t1, id64=id64,
                                   att_junk=att_junk)

        x_dst = x1T[:, t0:t0 + T].rearrange("(c p) t -> p c t", p=128)
        S.dma("sync", x_dst, X[:, :, :], waits=[(dv, dv.v)], sem=st)

    S.op("sync", lambda e: None, waits=[(st, st.v)])
    S.build(nc, ctx)
    ctx.close()
    return nc


def shard_tokens_fm(x_full):
    out = []
    for c in range(N_CORES):
        sl = x_full[c * SEQ_PER_CORE:(c + 1) * SEQ_PER_CORE].reshape(TOK, D)
        out.append(np.ascontiguousarray(np.asarray(sl, dtype=np.float32).T))
    return out


def prep_wqkv(W_qkv):
    W = np.asarray(W_qkv, dtype=np.float32).reshape(H, 3, DH, D)
    pairs = []
    for g in range(H // 2):
        pairs.append(W[2 * g, 0]); pairs.append(W[2 * g + 1, 0])
        pairs.append(W[2 * g, 1]); pairs.append(W[2 * g + 1, 1])
    qk = np.concatenate(pairs, 0)
    v = W[:, 2].reshape(H * DH, D)
    return np.ascontiguousarray(qk.T), np.ascontiguousarray(v.T)






def build_stage2(serialize=False, chained=False):
    nc = bass.Bass("TRN2", target_bir_lowering=False, debug=False)
    x1T = nc.dram_tensor("x1T", [D, TOK], F32, kind="ExternalInput").ap()
    wqTb = nc.dram_tensor("wqTb", [D, D], BF16, kind="ExternalInput").ap()
    wscale = nc.dram_tensor("wscale", [128, 8], F32, kind="ExternalInput").ap()
    fbias = nc.dram_tensor("fbias", [128, 8], F32, kind="ExternalInput").ap()
    idf_in = nc.dram_tensor("idf", [128, 128], F32, kind="ExternalInput").ap()
    x2T = nc.dram_tensor("x2T", [D, TOK], F32, kind="ExternalOutput").ap()

    S = Sched(serialize=serialize, chained=chained)
    ctx = ExitStack()
    bufs, sb, ps = alloc_common(S, nc, ctx)
    pe, dv, sc = bufs["pe"], bufs["dv"], bufs["sc"]
    ld, gq, st = bufs["ld"], bufs["gq"], bufs["st"]

    X = sb("X", [128, 8, T])
    xnf = sb("xnf", [128, 8, 512])
    xq = sb("xq", [128, 8, 512], BF16)
    wq = sb("wq", [128, 8, D], BF16)
    wsc = sb("wsc", [128, 8])
    fbc = sb("fbc", [128, 8])
    bufs["idf"] = sb("idf_sb", [128, 128])
    amaxc = sb("amaxc", [128, 2])
    rrow = sb("rrow", [1, 3 * 512])
    ti = sb("ti", [128, 512], I32)
    tf = sb("tf", [128, 512])
    td = sb("td", [128, 512])
    tg = sb("tg", [128, 512])
    t2 = sb("t2", [128, 512])

    P = [ps(f"P{i}", [128, 512]) for i in range(8)]
    bufs["p_m"], bufs["p_sq"], bufs["p_rb"], bufs["p_mrb"] = P[0], P[1], P[2], P[3]
    bufs["ln_junk"] = P[5][0:1, 0:256]
    p_tr = P[4]
    p_out = P[5]
    p_xs = P[6]
    p_junk = P[7][0:1, 0:256]

    ld_w = S.dma("gpsimd", wq[:, :, :], wqTb.rearrange("(c p) f -> p c f", p=128), sem=gq)
    S.dma("sync", wsc[:, :], wscale[:, :], sem=ld)
    S.dma("sync", fbc[:, :], fbias[:, :], sem=ld)
    S.dma("sync", bufs["idf"][:, :], idf_in[:, :], sem=ld)
    ld_const = ld.v

    for s in range(SEQ_PER_CORE):
        t0 = s * T
        x_src = x1T[:, t0:t0 + T].rearrange("(c p) t -> p c t", p=128)
        S.dma("sync", X[:, :, :], x_src, waits=[(st, st.v)], sem=ld)
        ld_x = ld.v

        for tc in range(4):
            tsl = slice(tc * 512, (tc + 1) * 512)
            dv_ln = emit_ln_chunk(
                S, bufs, X, tsl,
                [(ld, ld_x)] + ([(dv, bufs["init_dv"]), (ld, ld_const)] if (s == 0 and tc == 0) else []),
                lambda k: xnf[:, k, :])
            for tt in range(4):
                for k in range(8):
                    S.op("tensor", (lambda k=k, tt=tt: lambda e: e.transpose(
                        p_tr[:, 0:128], xnf[:, k, tt * 128:(tt + 1) * 128], bufs["idf"][:, :]))(),
                        waits=[(dv, dv_ln)] if (tt == 0 and k == 0) else [])
                    pe_t = emit_drain(S, bufs, p_junk)
                    S.op("vector", (lambda k=k: lambda e: e.tensor_reduce(
                        amaxc[:, 1:2], p_tr[:, 0:128], AX.X, ALU.max,
                        apply_absolute_value=True))(),
                        waits=[(pe, pe_t)])
                    if k == 0:
                        S.op("vector", lambda e: e.tensor_copy(amaxc[:, 0:1], amaxc[:, 1:2]),
                             waits=[(pe, pe.v)], incs=[(dv, 1)])
                    else:
                        S.op("vector", lambda e: e.tensor_tensor(
                            amaxc[:, 0:1], amaxc[:, 0:1], amaxc[:, 1:2], ALU.max),
                            incs=[(dv, 1)])
                    S.op("tensor", lambda e: None, waits=[(dv, dv.v)])
                S.op("tensor", (lambda tt=tt: lambda e: e.transpose(
                    p_tr[0:1, 0:128], amaxc[:, 0:1], bufs["idf"][:, :]))())
                pe_t2 = emit_drain(S, bufs, p_junk)
                S.op("vector", (lambda tt=tt: lambda e: e.tensor_copy(
                    rrow[0:1, tt * 128:(tt + 1) * 128], p_tr[0:1, 0:128]))(),
                    waits=[(pe, pe_t2)], incs=[(dv, 1)])
                S.op("tensor", lambda e: None, waits=[(dv, dv.v)])
            S.op("vector", lambda e: e.tensor_single_scalar(
                rrow[0:1, 0:512], rrow[0:1, 0:512], 1e-6, ALU.max))
            S.op("vector", lambda e: e.tensor_single_scalar(
                rrow[0:1, 512:1024], rrow[0:1, 0:512], 1.0 / 127.0, ALU.mult))
            S.op("vector", lambda e: e.reciprocal(rrow[0:1, 1024:1536], rrow[0:1, 512:1024]),
                 incs=[(dv, 1)])
            S.op("tensor", lambda e: e.matmul(p_tr[:, :], bufs["ones128"][:, :],
                                              rrow[0:1, 1024:1536], start=True, stop=True),
                 waits=[(dv, dv.v)])
            S.op("tensor", lambda e: e.matmul(p_xs[:, :], bufs["ones128"][:, :],
                                              rrow[0:1, 512:1024], start=True, stop=True))
            pe_bc = emit_drain(S, bufs, p_junk)
            for k in range(8):
                S.op("vector", (lambda k=k: lambda e: e.tensor_mul(
                    tf[:, :], xnf[:, k, :], p_tr[:, :]))(),
                    waits=[(pe, pe_bc)] if k == 0 else [])
                S.op("vector", lambda e: e.tensor_copy(ti[:, :], tf[:, :]))
                S.op("vector", lambda e: e.tensor_copy(td[:, :], ti[:, :]))
                S.op("vector", lambda e: e.tensor_sub(tg[:, :], tf[:, :], td[:, :]))
                S.op("vector", lambda e: e.scalar_tensor_tensor(
                    t2[:, :], tg[:, :], 0.5, td[:, :], ALU.is_ge, ALU.add))
                S.op("vector", lambda e: e.tensor_single_scalar(
                    tg[:, :], tg[:, :], -0.5, ALU.is_le))
                S.op("vector", (lambda k=k: lambda e: e.tensor_sub(
                    xq[:, k, :], t2[:, :], tg[:, :]))(),
                    incs=[(dv, 1)] if k == 7 else [])
            dv_q = dv.v
            for o in range(8):
                for k in range(8):
                    S.op("tensor", (lambda o=o, k=k: lambda e: e.matmul(
                        p_out[:, :], wq[:, k, o * 128:(o + 1) * 128], xq[:, k, :],
                        start=(k == 0), stop=(k == 7)))(),
                        waits=[(dv, dv_q), (gq, ld_w)] if (o == 0 and k == 0) else
                              ([(dv, dv.v)] if k == 0 else []))
                pe_o = emit_drain(S, bufs, p_junk)
                S.op("scalar", (lambda o=o: lambda e: e.activation(
                    out=t2[:, :], in_=p_out[:, :], func=AF.Copy, scale=wsc[:, o:o + 1]))(),
                    waits=[(pe, pe_o)], incs=[(sc, 1)])
                S.op("vector", lambda e: e.tensor_mul(t2[:, :], t2[:, :], p_xs[:, :]),
                     waits=[(sc, sc.v)])
                S.op("vector", (lambda o=o: lambda e: e.tensor_scalar(
                    t2[:, :], t2[:, :], fbc[:, o:o + 1], None, ALU.add))(),)
                S.op("vector", (lambda o=o, tsl=tsl: lambda e: e.tensor_add(
                    X[:, o, tsl], X[:, o, tsl], t2[:, :]))(),
                    incs=[(dv, 1)])
            S.op("tensor", lambda e: None, waits=[(dv, dv.v)])

        x_dst = x2T[:, t0:t0 + T].rearrange("(c p) t -> p c t", p=128)
        S.dma("sync", x_dst, X[:, :, :], waits=[(dv, dv.v)], sem=st)

    S.op("sync", lambda e: None, waits=[(st, st.v)])
    S.build(nc, ctx)
    ctx.close()
    return nc



def build_stage3(serialize=False, chained=False):
    nc = bass.Bass("TRN2", target_bir_lowering=False, debug=False)
    x2T = nc.dram_tensor("x2T", [D, TOK], F32, kind="ExternalInput").ap()
    wqkT3 = nc.dram_tensor("wqkT3", [D, 2 * D], F32, kind="ExternalInput").ap()
    wvT3 = nc.dram_tensor("wvT3", [D, D], F32, kind="ExternalInput").ap()
    woT = nc.dram_tensor("woT", [D, D], F32, kind="ExternalInput").ap()
    nw_in = nc.dram_tensor("nw", [128, 8], F32, kind="ExternalInput").ap()
    bqk_in = nc.dram_tensor("bqk", [1, 2 * D], F32, kind="ExternalInput").ap()
    bv_in = nc.dram_tensor("bv", [1, D], F32, kind="ExternalInput").ap()
    bo_in = nc.dram_tensor("bo", [1, D], F32, kind="ExternalInput").ap()
    idhi = nc.dram_tensor("idhi", [128, 64], F32, kind="ExternalInput").ap()
    cmask_in = nc.dram_tensor("cmask", [128, 896], F32, kind="ExternalInput").ap()
    x3T = nc.dram_tensor("x3T", [D, TOK], F32, kind="ExternalOutput").ap()

    S = Sched(serialize=serialize, chained=chained)
    ctx = ExitStack()
    bufs, sb, ps = alloc_common(S, nc, ctx)
    pe, dv, sc = bufs["pe"], bufs["dv"], bufs["sc"]
    ld, gq, st = bufs["ld"], bufs["gq"], bufs["st"]

    X = sb("X", [128, 8, T])
    xn = sb("xn", [128, 8, T], BF16)
    vaug = sb("vaug", [128, 8 * 16 * 65], BF16)
    qS = sb("qS", [65, 2 * T], BF16)
    kS = sb("kS", [65, 2 * T], BF16)
    Ob = sb("Ob", [128, 4, T], BF16)
    pt = sb("pt", [128, 16 * 512], BF16)
    p1buf = pt[:, 0:2 * 512]
    wshare = sb("wshare", [128, 4096], BF16)
    wv = wshare.rearrange("p (a b) -> p a b", a=8)
    wo = wshare.rearrange("p (a b) -> p a b", a=4)
    wqk3 = sb("wqk", [128, 1, 8, 256], BF16)
    id64 = sb("id64", [128, 64])
    osb = sb("osb", [65, 512])
    rro = sb("rro", [128, 512])
    nw = sb("nw_sb", [128, 8])
    brow = sb("brow", [1, 2 * D], BF16)
    bvrow = sb("bvrow", [1, D], BF16)
    borow = sb("borow", [1, D], BF16)
    onesr = sb("onesr", [1, 512], BF16)
    cmask = sb("cmask_sb", [128, 896], BF16)
    bufs["onesc_bf"] = sb("onesc_bf", [128, 1], BF16)
    t1 = bufs["tmp8"][0:64, :]

    P = [ps(f"P{i}", [128, 512]) for i in range(8)]
    bufs["p_m"], bufs["p_sq"], bufs["p_rb"], bufs["p_mrb"] = P[0], P[1], P[2], P[3]
    bufs["ln_junk"] = P[5][0:1, 0:256]
    p_qk = P[4]
    p_scA, p_scB = P[2], P[3]
    p_sums = P[1]
    p_o = P[4]
    p_rb2 = P[6]
    p_sh = P[5]
    att_junk = P[0][0:1, 0:256]
    p_wo = P[7]

    vaug_vh = vaug.rearrange("p (h t c) -> p h t c", h=8, c=65)
    S.op("vector", lambda e: e.memset(vaug_vh[:, :, :, 64:65], 1.0), incs=[(dv, 1)])
    S.op("vector", lambda e: e.memset(kS[64:65, :], 1.0), incs=[(dv, 1)])
    S.op("vector", lambda e: e.memset(bufs["onesc_bf"][:, :], 1.0), incs=[(dv, 1)])
    S.op("vector", lambda e: e.memset(onesr[:, :], 1.0), incs=[(dv, 1)])
    init_dv = dv.v

    S.dma("sync", id64[:, :], idhi[:, :], sem=ld)
    S.dma("sync", nw[:, :], nw_in[:, :], sem=ld)
    S.dma("gpsimd", cmask[:, :], cmask_in[:, :], sem=gq)
    S.dma("gpsimd", brow[:, :], bqk_in[:, :], sem=gq)
    S.dma("gpsimd", bvrow[:, :], bv_in[:, :], sem=gq)
    S.dma("gpsimd", borow[:, :], bo_in[:, :], sem=gq)
    ld_const = ld.v
    gq_const = gq.v

    for s in range(SEQ_PER_CORE):
        t0 = s * T
        x_src = x2T[:, t0:t0 + T].rearrange("(c p) t -> p c t", p=128)
        S.dma("sync", X[:, :, :], x_src, waits=[(st, st.v)], sem=ld)
        ld_x = ld.v

        dv_xn = 0
        for tc in range(4):
            tsl = slice(tc * 512, (tc + 1) * 512)
            w0 = [(ld, ld_x)]
            if s == 0 and tc == 0:
                w0 += [(dv, init_dv), (ld, ld_const), (gq, gq_const)]
            dv_xn = emit_ln_chunk(S, bufs, X, tsl, w0,
                                  (lambda tsl=tsl: lambda k: xn[:, k, tsl])(),
                                  rms=True, nw=nw)

        for half in range(2):
            S.op("gpsimd", lambda e: None, waits=[(pe, pe.v)])
            ld_wv = S.dma("gpsimd", wv[:, :, :],
                          wvT3[:, half * 512:(half + 1) * 512].rearrange("(c p) f -> p c f", p=128),
                          sem=gq)
            for tt in range(16):
                for k in range(8):
                    S.op("tensor", (lambda tt=tt, k=k: lambda e: e.matmul(
                        p_qk[:, :], xn[:, k, tt * 128:(tt + 1) * 128], wv[:, k, :],
                        start=(k == 0), stop=False))(),
                        waits=([(dv, dv_xn), (gq, ld_wv)] if (tt == 0 and k == 0) else []))
                S.op("tensor", (lambda tt=tt, half=half: lambda e: e.matmul(
                    p_qk[:, :], onesr[0:1, 0:128], bvrow[0:1, half * 512:(half + 1) * 512],
                    start=False, stop=True))())
                pe_v = emit_drain(S, bufs, att_junk)
                for hh in range(8):
                    S.op("vector", (lambda tt=tt, hh=hh: lambda e: e.tensor_copy(
                        vaug_vh[:, hh, tt, 0:64], p_qk[:, hh * 64:(hh + 1) * 64]))(),
                        waits=[(pe, pe_v)] if hh == 0 else [],
                        incs=[(dv, 1)] if hh == 7 else [])
                S.op("tensor", lambda e: None, waits=[(dv, dv.v)])

            for gl in range(4):
                g = half * 4 + gl
                gb = 0
                wsrc = wqkT3[:, g * 256:(g + 1) * 256].rearrange("(c p) f -> p c f", p=128)
                S.op("gpsimd", lambda e: None, waits=[(pe, pe.v)])
                ld_w = S.dma("gpsimd", wqk3[:, 0, :, :], wsrc, sem=gq)

                for dst, c0 in ((qS, 0), (kS, 128)):
                    for ncx in range(4):
                        for k in range(8):
                            S.op("tensor", (lambda ncx=ncx, k=k, c0=c0: lambda e: e.matmul(
                                p_qk[:, :], wqk3[:, 0, k, c0:c0 + 128],
                                xn[:, k, ncx * 512:(ncx + 1) * 512],
                                start=(k == 0), stop=False))(),
                                waits=[(gq, ld_w)] if (k == 0 and ncx == 0 and c0 == 0) else [])
                        S.op("tensor", (lambda g=g, ncx=ncx, c0=c0: lambda e: e.matmul(
                            p_qk[:, :], brow[0:1, g * 256 + c0:g * 256 + c0 + 128],
                            onesr[0:1, :],
                            start=False, stop=True))())
                        pe_qk = emit_drain(S, bufs, att_junk)
                        S.op("vector", (lambda dst=dst, ncx=ncx: lambda e: e.tensor_copy(
                            dst[0:64, ncx * 512:(ncx + 1) * 512], p_qk[0:64, :]))(),
                            waits=[(pe, pe_qk)])
                        S.op("vector", (lambda ncx=ncx: lambda e: e.tensor_copy(
                            bufs["tmp8"][64:128, :], p_qk[64:128, :]))(),
                            incs=[(dv, 1)])
                        S.op("tensor", lambda e: e.matmul(
                            p_sh[0:64, :], id64[64:128, :], bufs["tmp8"][64:128, :],
                            start=True, stop=True),
                            waits=[(dv, dv.v)])
                        pe_sh2 = emit_drain(S, bufs, att_junk)
                        S.op("vector", (lambda dst=dst, ncx=ncx: lambda e: e.tensor_copy(
                            dst[0:64, T + ncx * 512:T + (ncx + 1) * 512], p_sh[0:64, :]))(),
                            waits=[(pe, pe_sh2)], incs=[(dv, 1)])
                        S.op("tensor", lambda e: None, waits=[(dv, dv.v)])

                for sub in range(2):
                    hl = 2 * gl + sub
                    qSh = qS[:, sub * T:(sub + 1) * T]
                    kSh = kS[:, sub * T:(sub + 1) * T]
                    c = hl // 2
                    for qc in range(4):
                        qsl = slice(qc * 512, (qc + 1) * 512)
                        kt_hi = 4 * qc + 4
                        mt = {}
                        for j in range(4):
                            kt = 4 * qc + j
                            cc = kt * 128 - qc * 512
                            mt[kt] = cmask[:, 384 - cc:384 - cc + 512]

                        def X_add(t1v, p_shv, id64v, c=c, qsl=qsl, sub=sub):
                            if sub == 0:
                                S.op("vector", (lambda: lambda e: e.tensor_copy(
                                    Ob[0:64, c, qsl], t1v[:, 0:512]))(),
                                    incs=[(dv, 1)])
                            else:
                                S.op("tensor", (lambda: lambda e: e.matmul(
                                    p_shv[64:128, :], id64v[0:64, :],
                                    t1v[:, 0:512], start=True, stop=True))(),
                                    waits=[(dv, dv.v)])
                                pe_sh = emit_drain(S, bufs, att_junk)
                                S.op("vector", (lambda: lambda e: e.tensor_copy(
                                    Ob[64:128, c, qsl], p_shv[64:128, :]))(),
                                    waits=[(pe, pe_sh)], incs=[(dv, 1)])

                        emit_attn_head(S, bufs, qS=qSh, kS=kSh,
                                       vaug_v=vaug_vh[:, hl, :, :],
                                       X_add=X_add, h=hl, qsl=qsl, kt_lo=0, kt_hi=kt_hi,
                                       pt=pt, p1buf=p1buf, p_scA=p_scA, p_scB=p_scB,
                                       p_sums=p_sums, p_o=p_o, p_rb2=p_rb2, p_sh=p_sh,
                                       osb=osb, rro=rro, t1=t1, id64=id64,
                                       att_junk=att_junk, mask_tiles=mt)

            S.op("gpsimd", lambda e: None, waits=[(pe, pe.v)])
            ld_wo = S.dma("gpsimd", wo[:, :, :],
                          woT[half * 512:(half + 1) * 512, :].rearrange("(c p) f -> p c f", p=128),
                          sem=gq)
            dv_ob = dv.v
            for qc in range(4):
                qsl = slice(qc * 512, (qc + 1) * 512)
                for o in range(8):
                    for k in range(4):
                        S.op("tensor", (lambda o=o, k=k, qsl=qsl: lambda e: e.matmul(
                            p_wo[:, :], wo[:, k, o * 128:(o + 1) * 128], Ob[:, k, qsl],
                            start=(k == 0), stop=(half == 1 or k < 3)))(),
                            waits=[(gq, ld_wo), (dv, dv_ob)] if (qc == 0 and o == 0 and k == 0) else [])
                    if half == 0:
                        S.op("tensor", (lambda o=o: lambda e: e.matmul(
                            p_wo[:, :], borow[0:1, o * 128:(o + 1) * 128], onesr[0:1, :],
                            start=False, stop=True))())
                    pe_wo = emit_drain(S, bufs, att_junk)
                    if half == 0:
                        S.op("vector", (lambda o=o, qsl=qsl: lambda e: e.tensor_add(
                            bufs["tmp8"][:, :], p_wo[:, :], xn[:, o, qsl]))(),
                            waits=[(pe, pe_wo)])
                    else:
                        S.op("vector", (lambda o=o, qsl=qsl: lambda e: e.tensor_copy(
                            bufs["tmp8"][:, :], p_wo[:, :]))(),
                            waits=[(pe, pe_wo)])
                    S.op("vector", (lambda o=o, qsl=qsl: lambda e: e.tensor_add(
                        X[:, o, qsl], X[:, o, qsl], bufs["tmp8"][:, :]))(),
                        incs=[(dv, 1)])
                    S.op("tensor", lambda e: None, waits=[(dv, dv.v)])

        x_dst = x3T[:, t0:t0 + T].rearrange("(c p) t -> p c t", p=128)
        S.dma("sync", x_dst, X[:, :, :], waits=[(dv, dv.v)], sem=st)

    S.op("sync", lambda e: None, waits=[(st, st.v)])
    S.build(nc, ctx)
    ctx.close()
    return nc



def build_stage4(serialize=True, chained=False, ser_set=None):
    nc = bass.Bass("TRN2", target_bir_lowering=False, debug=False)
    x3T = nc.dram_tensor("x3T", [D, TOK], F32, kind="ExternalInput").ap()
    gwT = nc.dram_tensor("gwT", [D, E], F32, kind="ExternalInput").ap()
    gbrow = nc.dram_tensor("gbrow", [1, E], F32, kind="ExternalInput").ap()
    ic64_in = nc.dram_tensor("ic64", [128, E], F32, kind="ExternalInput").ap()
    ut128_in = nc.dram_tensor("ut128", [128, 128], F32, kind="ExternalInput").ap()
    idb_in = nc.dram_tensor("idb", [128, 128], BF16, kind="ExternalInput").ap()
    xfT = nc.dram_tensor("xfT", [D, TOK], BF16, kind="ExternalOutput").ap()
    xfr = nc.dram_tensor("xfr", [TOK, D], BF16, kind="ExternalOutput").ap()
    cnt = nc.dram_tensor("cnt", [1, E], F32, kind="ExternalOutput").ap()
    rinfo = nc.dram_tensor("rinfo", [128, 32, 18], F32, kind="ExternalOutput").ap()

    S = Sched(serialize=serialize, chained=chained, ser_set=ser_set)
    ctx = ExitStack()
    bufs, sb, ps = alloc_common(S, nc, ctx)
    pe, dv, sc = bufs["pe"], bufs["dv"], bufs["sc"]
    ld, gq, st = bufs["ld"], bufs["gq"], bufs["st"]

    X = sb("X", [128, 8, T])
    xf = sb("xf", [128, 8, T], BF16)
    gw = sb("gw", [128, 8, E], BF16)
    gbb = sb("gbb", [128, E])
    ic64 = sb("ic64_sb", [128, E])
    ut128 = sb("ut128_sb", [128, 128])
    idb = sb("idb_sb", [128, 128], BF16)
    bufs["tmp8b"] = sb("tmp8b", [128, 1024], BF16)
    sco = sb("sco", [128, E])
    sp = sb("sp", [128, E])
    oh = sb("oh", [128, E])
    cmat = sb("cmat", [128, E])
    pplus = sb("pplus", [128, E])
    top8 = sb("top8", [128, 8])
    run = sb("run", [1, E])
    info = sb("info", [128, 32, 18])
    junkrow = sb("junkrow", [128, E])
    rowmax = sb("rowmax", [128, 2])

    P = [ps(f"P{i}", [128, 512]) for i in range(7)]
    bufs["p_m"], bufs["p_sq"], bufs["p_rb"], bufs["p_mrb"] = P[0], P[1], P[2], P[3]
    bufs["ln_junk"] = P[5][0:1, 0:256]
    p_g = P[4]
    p_off = P[5]
    p_tot = P[6]
    att_junk = P[0][0:1, 0:256]
    S.ser_junk = P[6][0:1, 256:512]
    S.ser_ones1 = bufs["ones1"][0:1, 0:1]
    S.ser_onest = bufs["onest"][0:1, 0:256]
    p_trb = ctx.enter_context(nc.psum_tensor("p_trb", [128, 1024], BF16))

    S.op("vector", lambda e: e.memset(run[:, :], 0.0), incs=[(dv, 1)])
    init_dv0 = dv.v

    S.dma("gpsimd", gw[:, :, :], gwT.rearrange("(c p) f -> p c f", p=128), sem=gq)
    S.dma("sync", ic64[:, :], ic64_in[:, :], sem=ld)
    S.dma("sync", ut128[:, :], ut128_in[:, :], sem=ld)
    S.dma("sync", idb[:, :], idb_in[:, :], sem=ld)
    ld_const = ld.v
    gq_const = gq.v
    gbr = sb("gbr", [1, E])
    S.dma("sync", gbr[:, :], gbrow[:, :], sem=ld)
    ld_gb = ld.v
    S.op("tensor", lambda e: e.matmul(p_g[:, 0:E], bufs["ones128"][:, :], gbr[:, :],
                                      start=True, stop=True),
         waits=[(ld, ld_gb), (dv, init_dv0)])
    pe_gb = emit_drain(S, bufs, att_junk)
    S.op("vector", lambda e: e.tensor_copy(gbb[:, :], p_g[:, 0:E]),
         waits=[(pe, pe_gb)], incs=[(dv, 1)])
    init_dv = dv.v

    for s in range(SEQ_PER_CORE):
        t0 = s * T
        x_src = x3T[:, t0:t0 + T].rearrange("(c p) t -> p c t", p=128)
        S.dma("sync", X[:, :, :], x_src, waits=[(st, st.v)], sem=ld)
        ld_x = ld.v

        dv_xf = emit_layernorm(S, nc, bufs, X, xf, T,
                               [(ld, ld_x)] + ([(dv, init_dv), (ld, ld_const)] if s == 0 else []))
        S.dma("sync", xfT[:, t0:t0 + T].rearrange("(c p) t -> p c t", p=128),
              xf[:, :, :], waits=[(dv, dv_xf)], sem=st)

        for tt in range(16):
            gt = s * 16 + tt
            for k in range(8):
                S.op("tensor", (lambda k=k, tt=tt: lambda e: e.transpose(
                    p_trb[:, k * 128:(k + 1) * 128], xf[:, k, tt * 128:(tt + 1) * 128],
                    idb[:, :]))(),
                    waits=[(dv, dv_xf)] if (k == 0) else [])
            pe_tr = emit_drain(S, bufs, att_junk)
            S.op("vector", lambda e: e.tensor_copy(bufs["tmp8b"][:, :], p_trb[:, :]),
                 waits=[(pe, pe_tr), (st, st.v)], incs=[(dv, 1)])
            S.dma("sync", xfr[t0 + tt * 128:t0 + (tt + 1) * 128, :],
                  bufs["tmp8b"][:, :], waits=[(dv, dv.v)], sem=st)
            dv_tr = dv.v

            for k in range(8):
                S.op("tensor", (lambda k=k, tt=tt: lambda e: e.matmul(
                    p_g[:, 0:E], xf[:, k, tt * 128:(tt + 1) * 128], gw[:, k, :],
                    start=(k == 0), stop=(k == 7)))(),
                    waits=([(dv, dv.v)] + ([(gq, gq_const)] if (s == 0 and tt == 0) else [])) if k == 0 else [])
            pe_g = emit_drain(S, bufs, att_junk)
            S.op("vector", lambda e: e.tensor_copy(junkrow[:, 0:8], p_g[:, 0:8]),
                 waits=[(pe, pe_g)])
            S.op("vector", lambda e: e.tensor_reduce(rowmax[:, 0:1], p_g[:, 0:E],
                                                     AX.X, ALU.max))
            S.op("vector", lambda e: e.tensor_single_scalar(rowmax[:, 1:2], rowmax[:, 0:1],
                                                            -1.0, ALU.mult),
                 incs=[(dv, 1)])
            S.op("vector", lambda e: e.tensor_copy(junkrow[:, 0:8], junkrow[:, 8:16]),
                 incs=[(dv, 1)])
            S.op("vector", lambda e: e.tensor_copy(junkrow[:, 0:8], junkrow[:, 8:16]),
                 incs=[(dv, 1)])
            S.op("scalar", lambda e: e.activation(out=sco[:, :], in_=p_g[:, 0:E],
                                                  func=AF.Exp, bias=rowmax[:, 1:2],
                                                  accum_out=rowmax[:, 0:1]),
                 waits=[(dv, dv.v)], incs=[(sc, 1)])
            S.op("vector", lambda e: e.reciprocal(rowmax[:, 1:2], rowmax[:, 0:1]),
                 waits=[(sc, sc.v)])
            S.op("vector", lambda e: e.tensor_scalar(sco[:, :], sco[:, :],
                                                     rowmax[:, 1:2], None, ALU.mult))
            S.op("vector", lambda e: e.tensor_add(sp[:, :], sco[:, :], gbb[:, :]),
                 incs=[(dv, 1)])
            S.op("vector", lambda e: e.max(out=top8[:, :], in_=sp[:, :]))
            S.op("vector", lambda e: e.tensor_copy(junkrow[:, 0:8], oh[:, 0:8]))
            S.op("vector", lambda e: e.tensor_copy(junkrow[:, 8:16], oh[:, 0:8]))
            S.op("vector", lambda e: e.tensor_copy(junkrow[:, 16:24], oh[:, 0:8]))
            for k6 in range(TOPK):
                S.op("vector", (lambda k6=k6: lambda e: e.tensor_scalar(
                    oh[:, :], sp[:, :], top8[:, k6:k6 + 1], None, ALU.is_equal))())
                S.op("vector", (lambda k6=k6, gt=gt: lambda e: e.scalar_tensor_tensor(
                    junkrow[:, :], sco[:, :], 1.0, oh[:, :], ALU.mult, ALU.mult,
                    accum_out=info[:, gt, 6 + k6:7 + k6]))())
                S.op("vector", (lambda k6=k6, gt=gt: lambda e: e.scalar_tensor_tensor(
                    junkrow[:, :], ic64[:, :], 1.0, oh[:, :], ALU.mult, ALU.mult,
                    accum_out=info[:, gt, k6:k6 + 1]))())
                if k6 == 0:
                    S.op("vector", lambda e: e.tensor_copy(cmat[:, :], oh[:, :]))
                else:
                    S.op("vector", lambda e: e.tensor_add(cmat[:, :], cmat[:, :], oh[:, :]),
                         incs=[(dv, 1)] if k6 == TOPK - 1 else [])
            dv_c = dv.v
            S.op("tensor", lambda e: e.matmul(p_g[:, 0:E], ut128[:, :], cmat[:, :],
                                              start=True, stop=True),
                 waits=[(dv, dv_c)])
            S.op("tensor", lambda e: e.matmul(p_tot[0:1, 0:E], bufs["ones1"][:, :],
                                              cmat[:, :], start=True, stop=True))
            S.op("tensor", lambda e: e.matmul(p_off[:, 0:E], bufs["ones128"][:, :],
                                              run[:, :], start=True, stop=True))
            pe_pr = emit_drain(S, bufs, att_junk)
            S.op("vector", lambda e: e.tensor_copy(pplus[:, :], p_g[:, 0:E]),
                 waits=[(pe, pe_pr)])
            S.op("vector", lambda e: e.tensor_add(pplus[:, :], pplus[:, :], p_off[:, 0:E]))
            S.op("vector", lambda e: e.tensor_add(run[:, :], run[:, :], p_tot[0:1, 0:E]),
                 incs=[(dv, 1)])
            for k6 in range(TOPK):
                S.op("vector", (lambda k6=k6: lambda e: e.tensor_scalar(
                    oh[:, :], sp[:, :], top8[:, k6:k6 + 1], None, ALU.is_equal))())
                S.op("vector", (lambda k6=k6, gt=gt: lambda e: e.scalar_tensor_tensor(
                    junkrow[:, :], pplus[:, :], 1.0, oh[:, :], ALU.mult, ALU.mult,
                    accum_out=info[:, gt, 12 + k6:13 + k6]))(),
                    incs=[(dv, 1)] if k6 == TOPK - 1 else [])
            S.op("tensor", lambda e: None, waits=[(dv, dv.v)])

    S.dma("sync", cnt[:, :], run[:, :], waits=[(bufs["dv"], bufs["dv"].v)], sem=st)
    S.dma("sync", rinfo[:, :, :], info[:, :, :], sem=st)
    S.op("sync", lambda e: None, waits=[(st, st.v)])
    S.build(nc, ctx)
    ctx.close()
    return nc



def build_stage4b(serialize=True, chained=False):
    nc = bass.Bass("TRN2", target_bir_lowering=False, debug=False)
    cnts = nc.dram_tensor("cnts", [8, E], F32, kind="ExternalInput").ap()
    rinfo = nc.dram_tensor("rinfo", [128, 32, 18], F32, kind="ExternalInput").ap()
    csel = nc.dram_tensor("csel", [8, 1], F32, kind="ExternalInput").ap()
    i512_in = nc.dram_tensor("i512", [128, 512], F32, kind="ExternalInput").ap()
    tbase_in = nc.dram_tensor("tbase", [128, 1], F32, kind="ExternalInput").ap()
    ic64_in = nc.dram_tensor("ic64", [128, E], F32, kind="ExternalInput").ap()
    mpart = nc.dram_tensor("mpart", [E, 512], F32, kind="ExternalOutput").ap()
    sinfo = nc.dram_tensor("sinfo", [128, 32, 12], F32, kind="ExternalOutput").ap()

    S = Sched(serialize=serialize, chained=chained)
    ctx = ExitStack()
    bufs, sb, ps = alloc_common(S, nc, ctx)
    pe, dv, sc = bufs["pe"], bufs["dv"], bufs["sc"]
    ld, gq, st = bufs["ld"], bufs["gq"], bufs["st"]

    cntb = sb("cntb", [8, E])
    cselb = sb("cselb", [8, 1])
    i512 = sb("i512_sb", [128, 512])
    tbase = sb("tbase_sb", [128, 1])
    ic64 = sb("ic64_sb", [128, E])
    info = sb("info", [128, 32, 18])
    sout = sb("sout", [128, 32, 12])
    offb = sb("offb", [128, E])
    oh = sb("oh", [128, E])
    ohp = sb("ohp", [128, 512])
    oht = sb("oht", [128, E])
    tg = sb("tg", [128, 1])
    posg = sb("posg", [128, 1])
    lt = sb("lt", [128, 1])
    junkrow = sb("junkrow", [128, E])
    msb = sb("msb", [64, 512])

    P = [ps(f"P{i}", [128, 512]) for i in range(4)]
    p_off = P[0]
    p_m = P[1]
    att_junk = P[2][0:1, 0:256]
    S.ser_junk = P[3][0:1, 0:256]
    S.ser_ones1 = bufs["ones1"][0:1, 0:1]
    S.ser_onest = bufs["onest"][0:1, 0:256]

    S.dma("sync", cntb[:, :], cnts[:, :], sem=ld)
    S.dma("sync", cselb[:, :], csel[:, :], sem=ld)
    S.dma("sync", i512[:, :], i512_in[:, :], sem=ld)
    S.dma("sync", tbase[:, :], tbase_in[:, :], sem=ld)
    S.dma("sync", ic64[:, :], ic64_in[:, :], sem=ld)
    S.dma("sync", info[:, :, :], rinfo[:, :, :], sem=ld)
    ld_c = ld.v

    S.op("tensor", lambda e: e.matmul(p_off[0:1, 0:E], cselb[:, :], cntb[:, :],
                                      start=True, stop=True),
         waits=[(ld, ld_c)])
    pe1 = emit_drain(S, bufs, att_junk)
    S.op("vector", lambda e: e.tensor_copy(bufs["tmp8"][0:1, 0:E], p_off[0:1, 0:E]),
         waits=[(pe, pe1)], incs=[(dv, 1)])
    S.op("tensor", lambda e: e.matmul(p_off[:, 0:E], bufs["ones128"][:, :],
                                      bufs["tmp8"][0:1, 0:E], start=True, stop=True),
         waits=[(dv, dv.v)])
    pe2 = emit_drain(S, bufs, att_junk)
    S.op("vector", lambda e: e.tensor_copy(offb[:, :], p_off[:, 0:E]),
         waits=[(pe, pe2)], incs=[(dv, 1)])
    dv_off = dv.v

    first = True
    for gt in range(32):
        S.op("vector", (lambda gt=gt: lambda e: e.tensor_single_scalar(
            tg[:, :], tbase[:, :], float(gt * 128), ALU.add))(),
            waits=[(dv, dv_off)] if first else [])
        for k6 in range(TOPK):
            S.op("vector", (lambda gt=gt, k6=k6: lambda e: e.tensor_scalar(
                oh[:, :], ic64[:, :], info[:, gt, k6:k6 + 1], None, ALU.is_equal))())
            S.op("vector", (lambda: lambda e: e.scalar_tensor_tensor(
                junkrow[:, :], offb[:, :], 1.0, oh[:, :], ALU.mult, ALU.mult,
                accum_out=posg[:, :]))())
            S.op("vector", (lambda gt=gt, k6=k6: lambda e: e.tensor_scalar(
                posg[:, :], posg[:, :], info[:, gt, 12 + k6:13 + k6], None, ALU.add))())
            S.op("vector", lambda e: e.tensor_single_scalar(lt[:, :], posg[:, :], 512.0, ALU.is_lt))
            S.op("vector", (lambda gt=gt, k6=k6: lambda e: e.tensor_scalar(
                sout[:, gt, k6:k6 + 1], lt[:, :], info[:, gt, 6 + k6:7 + k6], None, ALU.mult))())
            S.op("vector", (lambda gt=gt, k6=k6: lambda e: e.tensor_single_scalar(
                bufs["tmp8"][:, 0:1], info[:, gt, k6:k6 + 1], 512.0, ALU.mult))())
            S.op("vector", lambda e: e.tensor_add(bufs["tmp8"][:, 0:1], bufs["tmp8"][:, 0:1], posg[:, :]))
            S.op("vector", lambda e: e.tensor_single_scalar(lt[:, :], lt[:, :], -1.0, ALU.add))
            S.op("vector", lambda e: e.tensor_single_scalar(lt[:, :], lt[:, :], -1e7, ALU.mult))
            S.op("vector", (lambda gt=gt, k6=k6: lambda e: e.tensor_add(
                sout[:, gt, 6 + k6:7 + k6], bufs["tmp8"][:, 0:1], lt[:, :]))())
            S.op("vector", (lambda: lambda e: e.tensor_scalar(
                oht[:, :], oh[:, :], tg[:, :], None, ALU.mult))())
            S.op("vector", (lambda: lambda e: e.tensor_scalar(
                ohp[:, :], i512[:, :], posg[:, :], None, ALU.is_equal))(),
                incs=[(dv, 1)])
            S.op("tensor", (lambda gt=gt, k6=k6: lambda e: e.matmul(
                p_m[0:64, :], oht[:, :], ohp[:, :],
                start=(gt == 0 and k6 == 0), stop=(gt == 31 and k6 == TOPK - 1)))(),
                waits=[(dv, dv.v)])
            pe_mm = emit_drain(S, bufs, att_junk)
            S.op("vector", lambda e: None, waits=[(pe, pe_mm)])
        first = False
    pe_m = emit_drain(S, bufs, att_junk)
    S.op("vector", lambda e: e.tensor_copy(msb[:, :], p_m[0:64, :]),
         waits=[(pe, pe_m)], incs=[(dv, 1)])
    S.dma("sync", mpart[:, :], msb[:, :], waits=[(dv, dv.v)], sem=st)
    S.dma("sync", sinfo[:, :, :], sout[:, :, :], waits=[(dv, dv.v)], sem=st)
    S.op("sync", lambda e: None, waits=[(st, st.v)])
    S.build(nc, ctx)
    ctx.close()
    return nc


I_PAD = 768


def build_stage5(serialize=False, chained=False):
    nc = bass.Bass("TRN2", target_bir_lowering=False, debug=False)
    mloc = nc.dram_tensor("mloc", [8, 512, 8], F32, kind="ExternalInput").ap()
    xf_full = nc.dram_tensor("xf_full", [B * T, D], BF16, kind="ExternalInput").ap()
    w1T = nc.dram_tensor("w1T", [8, D, I_PAD], F32, kind="ExternalInput").ap()
    w3T = nc.dram_tensor("w3T", [8, D, I_PAD], F32, kind="ExternalInput").ap()
    w2T = nc.dram_tensor("w2T", [8, I_PAD, D], F32, kind="ExternalInput").ap()
    idb_in = nc.dram_tensor("idb", [128, 128], BF16, kind="ExternalInput").ap()
    yb = nc.dram_tensor("yb", [8 * 512, D], BF16, kind="ExternalOutput").ap()

    S = Sched(serialize=serialize, chained=chained)
    ctx = ExitStack()
    bufs, sb, ps = alloc_common(S, nc, ctx)
    pe, dv, sc = bufs["pe"], bufs["dv"], bufs["sc"]
    ld, gq, st = bufs["ld"], bufs["gq"], bufs["st"]

    NI = I_PAD // 128
    marr = sb("marr", [128, 4, 8])
    midx = sb("midx", [128, 4], I32)
    mfl = sb("mfl", [128, 4])
    xb = sb("xb", [128, 4, D], BF16)
    xbT = sb("xbT", [128, 8, 512], BF16)
    w1b = sb("w1b", [128, 2, 8, I_PAD], BF16)
    w3b = sb("w3b", [128, 2, 8, I_PAD], BF16)
    w2b = sb("w2b", [128, 2, NI, D], BF16)
    us = sb("us", [128, NI, 512], BF16)
    vs = sb("vs", [128, NI, 512], BF16)
    idb = sb("idb_sb", [128, 128], BF16)
    ybt = sb("ybt", [128, 2, 512], BF16)
    jnk5 = sb("jnk5", [128, 256])

    P = [ps(f"P{i}", [128, 512]) for i in range(6)]
    p_u = P[0]
    p_v = P[1]
    p_y = P[2]
    att_junk = P[3][0:1, 0:256]
    p_trb = ctx.enter_context(nc.psum_tensor("p_trb", [128, 1024], BF16))

    S.dma("sync", idb[:, :], idb_in[:, :], sem=ld)
    ld_id = ld.v

    prev_gather = 0
    for e_i in range(8):
        eb = e_i % 2
        if e_i >= 2:
            S.op("gpsimd", lambda e: None, waits=[(pe, pe.v)])
        l1 = S.dma("gpsimd", w1b[:, eb, :, :],
                   w1T[e_i, :, :].rearrange("(c p) f -> p c f", p=128), sem=gq)
        l3 = S.dma("gpsimd", w3b[:, eb, :, :],
                   w3T[e_i, :, :].rearrange("(c p) f -> p c f", p=128), sem=gq)
        l2 = S.dma("gpsimd", w2b[:, eb, :, :],
                   w2T[e_i, :, :].rearrange("(c p) f -> p c f", p=128), sem=gq)

        S.dma("sync", marr[:, :, :],
              mloc[e_i, :, :].rearrange("(c p) k -> p c k", p=128),
              waits=[(dv, dv.v)], sem=ld)
        def _sp():
            S.op("vector", lambda e: e.tensor_copy(jnk5[:, 0:128], jnk5[:, 128:256]))
        S.op("vector", lambda e: e.tensor_reduce(mfl[:, :], marr[:, :, :], AX.X, ALU.add),
             waits=[(ld, ld.v)])
        _sp(); _sp()
        S.op("vector", lambda e: e.tensor_single_scalar(mfl[:, :], mfl[:, :], -1.0, ALU.add))
        _sp(); _sp()
        S.op("vector", lambda e: e.tensor_scalar_max(mfl[:, :], mfl[:, :], 0.0))
        _sp(); _sp()
        S.op("vector", lambda e: e.tensor_copy(midx[:, :], mfl[:, :]),
             waits=[(gq, prev_gather)] if prev_gather else [])
        _sp(); _sp()
        S.op("vector", lambda e: e.tensor_copy(jnk5[:, 0:128], jnk5[:, 128:256]),
             incs=[(dv, 1)])
        dv_idx = dv.v
        for c in range(4):
            S.op("gpsimd", (lambda c=c: lambda e: e.indirect_dma_start(
                out=xb[:, c, :], out_offset=None,
                in_=xf_full[:, :], in_offset=IndirectOffsetOnAxis(midx[:, c:c + 1], 0),
                bounds_check=B * T - 1, oob_is_err=False))(),
                waits=[(dv, dv_idx)] if c == 0 else [], incs=[(gq, 16)])
        gq_gather = gq.v
        prev_gather = gq_gather
        for c in range(4):
            for k in range(8):
                S.op("tensor", (lambda c=c, k=k: lambda e: e.transpose(
                    p_trb[:, k * 128:(k + 1) * 128], xb[:, c, k * 128:(k + 1) * 128],
                    idb[:, :]))(),
                    waits=[(gq, gq_gather), (ld, ld_id)] if (c == 0 and k == 0) else [])
            pe_tr = emit_drain(S, bufs, att_junk)
            S.op("vector", (lambda c=c: lambda e: e.tensor_copy(
                xbT[:, :, c * 128:(c + 1) * 128],
                p_trb[:, :].rearrange("p (a b) -> p a b", a=8)))(),
                waits=[(pe, pe_tr)], incs=[(dv, 1)])
            S.op("tensor", lambda e: None, waits=[(dv, dv.v)])
        dv_xbT = dv.v

        for i in range(NI):
            for k in range(8):
                S.op("tensor", (lambda i=i, k=k, eb=eb: lambda e: e.matmul(
                    p_u[:, :], w1b[:, eb, k, i * 128:(i + 1) * 128], xbT[:, k, :],
                    start=(k == 0), stop=(k == 7)))(),
                    waits=[(dv, dv_xbT), (gq, l3)] if (i == 0 and k == 0) else [])
                S.op("tensor", (lambda i=i, k=k, eb=eb: lambda e: e.matmul(
                    p_v[:, :], w3b[:, eb, k, i * 128:(i + 1) * 128], xbT[:, k, :],
                    start=(k == 0), stop=(k == 7)))())
            pe_uv = emit_drain(S, bufs, att_junk)
            S.op("scalar", (lambda i=i: lambda e: e.activation(
                out=us[:, i, :], in_=p_u[:, :], func=AF.Silu))(),
                waits=[(pe, pe_uv)], incs=[(sc, 1)])
            S.op("vector", (lambda i=i: lambda e: e.tensor_copy(vs[:, i, :], p_v[:, :]))(),
                 waits=[(pe, pe_uv)])
            S.op("vector", (lambda i=i: lambda e: e.tensor_mul(
                us[:, i, :], us[:, i, :], vs[:, i, :]))(),
                waits=[(sc, sc.v)], incs=[(dv, 1)])
            S.op("tensor", lambda e: None, waits=[(dv, dv.v)])
        dv_h = dv.v

        for c in range(4):
            for dh in range(2):
                for i in range(NI):
                    S.op("tensor", (lambda c=c, dh=dh, i=i, eb=eb: lambda e: e.matmul(
                        p_y[:, :], us[:, i, c * 128:(c + 1) * 128],
                        w2b[:, eb, i, dh * 512:(dh + 1) * 512],
                        start=(i == 0), stop=(i == NI - 1)))(),
                        waits=[(dv, dv_h), (gq, l2)] if (c == 0 and dh == 0 and i == 0) else [])
                pe_y = emit_drain(S, bufs, att_junk)
                S.op("vector", (lambda dh=dh: lambda e: e.tensor_copy(
                    ybt[:, dh, :], p_y[:, :]))(),
                    waits=[(pe, pe_y), (st, st.v)], incs=[(dv, 1)])
                S.op("tensor", lambda e: None, waits=[(dv, dv.v)])
            S.dma("sync", yb[e_i * 512 + c * 128:e_i * 512 + (c + 1) * 128, :],
                  ybt[:, :, :].rearrange("p a b -> p (a b)"), waits=[(dv, dv.v)], sem=st)

    S.op("sync", lambda e: None, waits=[(st, st.v)])
    S.build(nc, ctx)
    ctx.close()
    return nc



def build_stage6(serialize=False, chained=False):
    nc = bass.Bass("TRN2", target_bir_lowering=False, debug=False)
    x3T = nc.dram_tensor("x3T", [D, TOK], F32, kind="ExternalInput").ap()
    xfT_in = nc.dram_tensor("xfT", [D, TOK], BF16, kind="ExternalInput").ap()
    yb_full = nc.dram_tensor("yb_full", [B * T, D], BF16, kind="ExternalInput").ap()
    sinfo_in = nc.dram_tensor("sinfo", [128, 32, 12], F32, kind="ExternalInput").ap()
    ws1T = nc.dram_tensor("ws1T", [D, SI], F32, kind="ExternalInput").ap()
    ws3T = nc.dram_tensor("ws3T", [D, SI], F32, kind="ExternalInput").ap()
    ws2T = nc.dram_tensor("ws2T", [SI, D], F32, kind="ExternalInput").ap()
    idf_in = nc.dram_tensor("idf", [128, 128], F32, kind="ExternalInput").ap()
    outr = nc.dram_tensor("outr", [TOK, D], F32, kind="ExternalOutput").ap()

    S = Sched(serialize=serialize, chained=chained)
    ctx = ExitStack()
    bufs, sb, ps = alloc_common(S, nc, ctx)
    pe, dv, sc = bufs["pe"], bufs["dv"], bufs["sc"]
    ld, gq, st = bufs["ld"], bufs["gq"], bufs["st"]

    NS = SI // 128
    X = sb("X", [128, 8, T])
    xf = sb("xf", [128, 8, T], BF16)
    w1b = sb("w1b", [128, 8, SI], BF16)
    w3b = sb("w3b", [128, 8, SI], BF16)
    w2b = sb("w2b", [128, NS, D], BF16)
    us = sb("us", [128, NS, 512], BF16)
    sinfo = sb("sinfo_sb", [128, 32, 12])
    sidx = sb("sidx", [128, 1], I32)
    gbuf = sb("gbuf", [128, 1, D], BF16)
    acc = sb("acc", [128, D])
    idf = sb("idf_sb", [128, 128])
    jnk = sb("jnk", [128, 256])

    P = [ps(f"P{i}", [128, 512]) for i in range(6)]
    p_u = P[0]
    p_v = P[1]
    p_z = P[2]
    p_x = P[3]
    att_junk = P[4][0:1, 0:256]

    S.dma("sync", sinfo[:, :, :], sinfo_in[:, :, :], sem=ld)
    S.dma("sync", idf[:, :], idf_in[:, :], sem=ld)
    l1 = S.dma("gpsimd", w1b[:, :, :], ws1T.rearrange("(c p) f -> p c f", p=128), sem=gq)
    l3 = S.dma("gpsimd", w3b[:, :, :], ws3T.rearrange("(c p) f -> p c f", p=128), sem=gq)
    l2 = S.dma("gpsimd", w2b[:, :, :], ws2T.rearrange("(c p) f -> p c f", p=128), sem=gq)
    ld_c = ld.v
    S.op("vector", lambda e: e.memset(gbuf[:, 0, :], 0.0), incs=[(dv, 1)])
    init_dv = dv.v
    _bh = {}

    def _breg(e):
        if "r" not in _bh:
            _bh["r"] = e.alloc_register("bnd")
            e.reg_mov(_bh["r"], B * T - 1)
        return _bh["r"]


    for s in range(SEQ_PER_CORE):
        t0 = s * T
        S.dma("sync", X[:, :, :], x3T[:, t0:t0 + T].rearrange("(c p) t -> p c t", p=128),
              waits=[(st, st.v)], sem=ld)
        S.dma("sync", xf[:, :, :], xfT_in[:, t0:t0 + T].rearrange("(c p) t -> p c t", p=128),
              sem=ld)
        ld_x = ld.v

        for tc in range(4):
            tsl = slice(tc * 512, (tc + 1) * 512)
            for i in range(NS):
                for k in range(8):
                    S.op("tensor", (lambda i=i, k=k, tsl=tsl: lambda e: e.matmul(
                        p_u[:, :], w1b[:, k, i * 128:(i + 1) * 128], xf[:, k, tsl],
                        start=(k == 0), stop=(k == 7)))(),
                        waits=[(ld, ld_x), (gq, l3), (dv, init_dv)] if (i == 0 and k == 0 and tc == 0 and s == 0) else ([(ld, ld_x)] if (i == 0 and k == 0) else []))
                    S.op("tensor", (lambda i=i, k=k, tsl=tsl: lambda e: e.matmul(
                        p_v[:, :], w3b[:, k, i * 128:(i + 1) * 128], xf[:, k, tsl],
                        start=(k == 0), stop=(k == 7)))())
                pe_uv = emit_drain(S, bufs, att_junk)
                S.op("scalar", (lambda i=i: lambda e: e.activation(
                    out=us[:, i, :], in_=p_u[:, :], func=AF.Silu))(),
                    waits=[(pe, pe_uv)], incs=[(sc, 1)])
                S.op("vector", (lambda i=i: lambda e: e.tensor_mul(
                    us[:, i, :], us[:, i, :], p_v[:, :]))(),
                    waits=[(sc, sc.v)], incs=[(dv, 1)])
                S.op("tensor", lambda e: None, waits=[(dv, dv.v)])
            dv_h = dv.v

            for ttl in range(4):
                gt = s * 16 + tc * 4 + ttl
                for dh in range(2):
                    for i in range(NS):
                        S.op("tensor", (lambda ttl=ttl, dh=dh, i=i: lambda e: e.matmul(
                            p_z[:, :], us[:, i, ttl * 128:(ttl + 1) * 128],
                            w2b[:, i, dh * 512:(dh + 1) * 512],
                            start=(i == 0), stop=(i == NS - 1)))(),
                            waits=[(dv, dv_h), (gq, l2)] if (ttl == 0 and dh == 0 and i == 0) else [])
                    pe_z = emit_drain(S, bufs, att_junk)
                    S.op("vector", (lambda dh=dh: lambda e: e.tensor_copy(
                        acc[:, dh * 512:(dh + 1) * 512], p_z[:, :]))(),
                        waits=[(pe, pe_z)], incs=[(dv, 1)])
                    S.op("tensor", lambda e: None, waits=[(dv, dv.v)])
                for k in range(8):
                    S.op("tensor", (lambda k=k, tc=tc, ttl=ttl: lambda e: e.transpose(
                        p_x[:, 0:128], X[:, k, tc * 512 + ttl * 128:tc * 512 + (ttl + 1) * 128],
                        idf[:, :]))(),
                        waits=[(ld, ld_c)] if k == 0 else [])
                    pe_x = emit_drain(S, bufs, att_junk)
                    S.op("vector", (lambda k=k: lambda e: e.tensor_add(
                        acc[:, k * 128:(k + 1) * 128], acc[:, k * 128:(k + 1) * 128],
                        p_x[:, 0:128]))(),
                        waits=[(pe, pe_x)], incs=[(dv, 1)])
                    S.op("tensor", lambda e: None, waits=[(dv, dv.v)])
                for k6 in range(TOPK):
                    S.op("vector", (lambda gt=gt, k6=k6: lambda e: e.tensor_copy(
                        sidx[:, :], sinfo[:, gt, 6 + k6:7 + k6]))())
                    S.op("vector", lambda e: e.tensor_copy(jnk[:, 0:128], jnk[:, 128:256]))
                    S.op("vector", lambda e: e.tensor_copy(jnk[:, 0:128], jnk[:, 128:256]))
                    S.op("vector", lambda e: e.tensor_copy(jnk[:, 0:128], jnk[:, 128:256]),
                         incs=[(dv, 1)])
                    S.op("gpsimd", (lambda: lambda e: e.indirect_dma_start(
                        out=gbuf[:, 0, :], out_offset=None,
                        in_=yb_full[:, :], in_offset=IndirectOffsetOnAxis(sidx[:, :], 0),
                        bounds_check=_breg(e), oob_is_err=False))(),
                        waits=[(dv, dv.v)], incs=[(gq, 16)])
                    S.op("vector", (lambda gt=gt, k6=k6: lambda e: e.scalar_tensor_tensor(
                        acc[:, :], gbuf[:, 0, :], sinfo[:, gt, k6:k6 + 1], acc[:, :],
                        ALU.mult, ALU.add))(),
                        waits=[(gq, gq.v)], incs=[(dv, 1)])
                S.dma("sync", outr[t0 + (tc * 4 + ttl) * 128:t0 + (tc * 4 + ttl + 1) * 128, :],
                      acc[:, :], waits=[(dv, dv.v)], sem=st)
                S.op("vector", lambda e: None, waits=[(st, st.v)])

    S.op("sync", lambda e: None, waits=[(st, st.v)])
    S.build(nc, ctx)
    ctx.close()
    return nc



def host_consts():
    idhi = np.concatenate([np.eye(64, dtype=np.float32)] * 2, 0)
    idf = np.eye(128, dtype=np.float32)
    idb = np.eye(128).astype(BF)
    ic64 = np.broadcast_to(np.arange(E, dtype=np.float32), (128, E)).copy()
    i512 = np.broadcast_to(np.arange(512, dtype=np.float32), (128, 512)).copy()
    ut = (np.arange(128)[:, None] < np.arange(128)[None, :]).astype(np.float32)
    kk = np.arange(128)[:, None]
    m = np.arange(896)[None, :]
    cmask = np.where(m - 384 >= kk, 0.0, -1e9).astype(np.float32)
    return dict(idhi=idhi, idf=idf, idb=idb, ic64=ic64, i512=i512, ut128=ut, cmask=cmask)


def prep_stage3_weights(Wq, Wk, Wv, Wo, nw, bq, bk, bv, bo):
    qk, bqk = [], []
    for g in range(H // 2):
        for h in (2 * g, 2 * g + 1):
            qk.append(Wq[h * DH:(h + 1) * DH, :])
        for h in (2 * g, 2 * g + 1):
            qk.append(Wk[h * DH:(h + 1) * DH, :])
        bqk.extend([bq[2 * g * DH:(2 * g + 1) * DH], bq[(2 * g + 1) * DH:(2 * g + 2) * DH],
                    bk[2 * g * DH:(2 * g + 1) * DH], bk[(2 * g + 1) * DH:(2 * g + 2) * DH]])
    return dict(
        wqkT3=np.ascontiguousarray(np.concatenate(qk, 0).T),
        wvT3=np.ascontiguousarray(Wv.T),
        woT=np.ascontiguousarray(Wo.T),
        nw=np.ascontiguousarray(nw.reshape(8, 128).T),
        bqk=np.ascontiguousarray(np.concatenate(bqk)[None, :]),
        bv=np.ascontiguousarray(bv[None, :]),
        bo=np.ascontiguousarray(bo[None, :]),
    )


_BUILT = {}


def _get(name, builder):
    if name not in _BUILT:
        _BUILT[name] = builder()
    return _BUILT[name]


def kernel(L0_attn_z, L0_attn_W_qkv, L0_ffn_weight_q, L0_ffn_weight_scale,
           L0_ffn_bias, L1_attn_norm_w, L1_attn_Wq, L1_attn_bq, L1_attn_Wk,
           L1_attn_bk, L1_attn_Wv, L1_attn_bv, L1_attn_Wo, L1_attn_bo,
           L1_ffn_gate_w, L1_ffn_gate_b, L1_ffn_w1, L1_ffn_w2, L1_ffn_w3,
           L1_ffn_ws1, L1_ffn_ws2, L1_ffn_ws3):
    f32 = lambda a: np.asarray(a, dtype=np.float32)
    C = host_consts()
    cores = list(range(N_CORES))

    shards = shard_tokens_fm(f32(L0_attn_z))
    wqkT, wvT = prep_wqkv(f32(L0_attn_W_qkv))
    nc1 = _get("s1", build_stage1)
    r1 = run_bass_kernel_spmd(nc1, [
        {"x0T": shards[c], "wqkT": wqkT, "wvT": wvT, "idhi": C["idhi"]}
        for c in cores], core_ids=cores)
    x1 = [r1.results[c]["x1T"] for c in cores]

    wqTb = np.ascontiguousarray(np.asarray(L0_ffn_weight_q).T.astype(BF))
    wsc = np.ascontiguousarray(f32(L0_ffn_weight_scale).reshape(8, 128).T)
    fbc = np.ascontiguousarray(f32(L0_ffn_bias).reshape(8, 128).T)
    nc2 = _get("s2", build_stage2)
    r2 = run_bass_kernel_spmd(nc2, [
        {"x1T": x1[c], "wqTb": wqTb, "wscale": wsc, "fbias": fbc, "idf": C["idf"]}
        for c in cores], core_ids=cores)
    x2 = [r2.results[c]["x2T"] for c in cores]

    w3d = prep_stage3_weights(f32(L1_attn_Wq), f32(L1_attn_Wk), f32(L1_attn_Wv),
                              f32(L1_attn_Wo), f32(L1_attn_norm_w), f32(L1_attn_bq),
                              f32(L1_attn_bk), f32(L1_attn_bv), f32(L1_attn_bo))
    w3d["idhi"] = C["idhi"]
    w3d["cmask"] = C["cmask"]
    nc3 = _get("s3", build_stage3)
    r3 = run_bass_kernel_spmd(nc3, [dict(w3d, x2T=x2[c]) for c in cores], core_ids=cores)
    x3 = [r3.results[c]["x3T"] for c in cores]

    gwT = np.ascontiguousarray(f32(L1_ffn_gate_w).T)
    gbrow = np.ascontiguousarray(f32(L1_ffn_gate_b)[None, :])
    nc4 = _get("s4", build_stage4)
    r4 = run_bass_kernel_spmd(nc4, [
        {"x3T": x3[c], "gwT": gwT, "gbrow": gbrow, "ic64": C["ic64"],
         "ut128": C["ut128"], "idb": C["idb"]} for c in cores], core_ids=cores)
    xfT = [r4.results[c]["xfT"] for c in cores]
    xfr = [r4.results[c]["xfr"] for c in cores]
    cnts = np.concatenate([r4.results[c]["cnt"] for c in cores], 0)
    rinfo = [r4.results[c]["rinfo"] for c in cores]

    nc4b = _get("s4b", build_stage4b)
    in4b = []
    for c in cores:
        csel = (np.arange(8) < c).astype(np.float32)[:, None]
        tbase = (c * TOK + np.arange(128) + 1).astype(np.float32)[:, None]
        in4b.append({"cnts": cnts, "rinfo": rinfo[c], "csel": np.ascontiguousarray(csel),
                     "i512": C["i512"], "tbase": np.ascontiguousarray(tbase),
                     "ic64": C["ic64"]})
    r4b = run_bass_kernel_spmd(nc4b, in4b, core_ids=cores)
    mparts = [r4b.results[c]["mpart"] for c in cores]
    sinfo = [r4b.results[c]["sinfo"] for c in cores]

    xf_full = np.ascontiguousarray(np.concatenate(xfr, 0))
    w1 = f32(L1_ffn_w1); w2 = f32(L1_ffn_w2); w3 = f32(L1_ffn_w3)
    nc5 = _get("s5", build_stage5)
    in5 = []
    for c in cores:
        mloc = np.ascontiguousarray(
            np.stack([mp[c * 8:(c + 1) * 8] for mp in mparts], -1))
        w1T = np.zeros((8, D, I_PAD), np.float32)
        w1T[:, :, :I_EXP] = w1[c * 8:(c + 1) * 8].transpose(0, 2, 1)
        w3T = np.zeros((8, D, I_PAD), np.float32)
        w3T[:, :, :I_EXP] = w3[c * 8:(c + 1) * 8].transpose(0, 2, 1)
        w2T = np.zeros((8, I_PAD, D), np.float32)
        w2T[:, :I_EXP, :] = w2[c * 8:(c + 1) * 8].transpose(0, 2, 1)
        in5.append({"mloc": mloc, "xf_full": xf_full, "w1T": np.ascontiguousarray(w1T),
                    "w3T": np.ascontiguousarray(w3T), "w2T": np.ascontiguousarray(w2T),
                    "idb": C["idb"]})
    r5 = run_bass_kernel_spmd(nc5, in5, core_ids=cores)
    yb_full = np.ascontiguousarray(np.concatenate([r5.results[c]["yb"] for c in cores], 0))

    ws1T = np.ascontiguousarray(f32(L1_ffn_ws1).T)
    ws3T = np.ascontiguousarray(f32(L1_ffn_ws3).T)
    ws2T = np.ascontiguousarray(f32(L1_ffn_ws2).T)
    nc6 = _get("s6", build_stage6)
    r6 = run_bass_kernel_spmd(nc6, [
        {"x3T": x3[c], "xfT": xfT[c], "yb_full": yb_full, "sinfo": sinfo[c],
         "ws1T": ws1T, "ws3T": ws3T, "ws2T": ws2T, "idf": C["idf"]}
        for c in cores], core_ids=cores)
    out = np.concatenate([r6.results[c]["outr"].reshape(SEQ_PER_CORE, T, D)
                          for c in cores], 0)
    return out


# version 02
# speedup vs baseline: 1.1666x; 1.1666x over previous
import numpy as np
import ml_dtypes
import concourse.bass as bass
import concourse.mybir as mybir
from concourse.bass_utils import run_bass_kernel_spmd
from concourse.bass_core_rust import IndirectOffsetOnAxis
from contextlib import ExitStack

F32 = mybir.dt.float32
BF16 = mybir.dt.bfloat16
I32 = mybir.dt.int32
AF = mybir.ActivationFunctionType
ALU = mybir.AluOpType
AX = mybir.AxisListType

N_CORES = 8
B, T, D = 16, 2048, 1024
SEQ_PER_CORE = B // N_CORES
TOK = SEQ_PER_CORE * T
H, DH = 16, 64
E, TOPK, CAP = 64, 6, 512
I_EXP = 704
SI = 1408
EPS_LN = 1e-5
EPS_RMS = 1e-6
SHIFT = 12.0

BF = ml_dtypes.bfloat16


ENGINES = ("sync", "tensor", "vector", "scalar", "gpsimd")


class PySem:
    __slots__ = ("name", "v", "h")

    def __init__(self, name):
        self.name = name
        self.v = 0
        self.h = None


class Sched:

    def __init__(self, serialize=False, chained=False, ser_set=None):
        self.ser_set = set(ser_set) if ser_set else None
        self.items = {e: [] for e in ENGINES}
        self.sems = []
        self.fence_buf = None
        self.chained = chained
        self.chain_sems = {}
        self.serialize = serialize
        self.eng_sems = {}

    def sem(self, name):
        s = PySem(name)
        self.sems.append(s)
        return s

    def op(self, eng, fn, waits=(), incs=(), isdma=False):
        if self.chained:
            if eng not in self.chain_sems:
                self.chain_sems[eng] = self.sem(f"ch_{eng}")
                self.chain_last = getattr(self, "chain_last", {})
                self.chain_last[eng] = None
            last = self.chain_last[eng]
            if last is not None:
                waits = list(waits) + [last]
            if incs:
                s0, n0 = incs[0]
                self.chain_last[eng] = (s0, s0.v + n0)
            else:
                cs = self.chain_sems[eng]
                incs = list(incs) + [(cs, 16 if isdma else 1)]
                self.chain_last[eng] = (cs, cs.v + (16 if isdma else 1))
        if self.serialize:
            if not self.eng_sems:
                for e2 in ENGINES:
                    self.eng_sems[e2] = self.sem(f"ser_{e2}")
                self.last = {e2: None for e2 in ENGINES}
            if self.ser_set and "PEWAIT_ONLY" in self.ser_set:
                if eng in ("vector", "scalar"):
                    pesem = next((x for x in self.sems if x.name == "pe"), None)
                    if pesem is not None and pesem.v:
                        waits = list(waits) + [(pesem, pesem.v)]
                for (s_, n) in incs:
                    s_.v += n
                if eng in ("vector", "scalar") and incs and not isdma:
                    self.items[eng].append((fn, tuple(waits), ()))
                    self.items[eng].append((self._fence_fn(eng), (), tuple(incs)))
                else:
                    self.items[eng].append((fn, tuple(waits), tuple(incs)))
                return
            if self.ser_set and "DIR_PE_WAITS" in self.ser_set:
                extra = [w for e2, w in self.last.items()
                         if e2 != eng and w] if eng == "tensor" else []
            elif self.ser_set and "DIR_WAITS_PE" in self.ser_set:
                extra = [self.last["tensor"]] if (eng != "tensor" and self.last.get("tensor")) else []
            else:
                extra = [w for e2, w in self.last.items()
                         if e2 != eng and w and (self.ser_set is None or e2 in self.ser_set
                                                 or eng in self.ser_set)]
            waits = list(waits) + extra
        for (s, n) in incs:
            s.v += n
        if eng in ("vector", "scalar") and incs and not isdma:
            self.items[eng].append((fn, tuple(waits), ()))
            self.items[eng].append((self._fence_fn(eng), (), tuple(incs)))
        else:
            self.items[eng].append((fn, tuple(waits), tuple(incs)))
        if self.serialize:
            if isdma and incs:
                self.last[eng] = (incs[0][0], incs[0][0].v)
            else:
                ser = self.eng_sems[eng]
                ser.v += 1
                self.items[eng].append((self._ser_fence_fn(eng), (), ((ser, 1),)))
                self.last[eng] = (ser, ser.v)

    def _ser_fence_fn(self, eng):
        if eng in ("vector", "scalar"):
            return self._fence_fn(eng)
        if eng == "tensor":
            return lambda e: e.matmul(self.ser_junk, self.ser_ones1, self.ser_onest,
                                      start=True, stop=True)
        return lambda e: e.nop()

    def _fence_fn(self, eng):
        if self.fence_buf is None:
            raise RuntimeError("fence buffer not set")
        fb = self.fence_buf
        if eng == "vector":
            return lambda e: e.memset(fb[:, 0:1], 0.0)
        return lambda e: e.activation(out=fb[:, 1:2], in_=fb[:, 1:2], func=AF.Identity)

    def dma(self, eng, out, in_, waits=(), sem=None, **kw):
        incs = [(sem, 16)] if sem is not None else []
        self.op(eng, lambda e: e.dma_start(out=out, in_=in_, **kw), waits, incs, isdma=True)
        return sem.v if sem is not None else None

    def build(self, nc, ctx):
        for s in self.sems:
            s.h = ctx.enter_context(nc.semaphore(s.name))
        block = ctx.enter_context(nc.Block())
        for eng_name in ENGINES:
            items = self.items[eng_name]
            if not items:
                continue

            def make_body(items):
                def body(eng):
                    for fn, waits, incs in items:
                        for (s, v) in waits:
                            eng.wait_ge(s.h, v)
                        inst = fn(eng)
                        for (s, n) in incs:
                            if inst is None:
                                inst = eng.nop()
                            inst.then_inc(s.h, n)
                return body

            getattr(block, eng_name)(make_body(items))


def ceil_div(a, b):
    return (a + b - 1) // b



def emit_ln_chunk(S, bufs, X, tsl, ld_wait, out_ap, eps=EPS_LN, rms=False, nw=None):
    (pe, dv, sc) = bufs["pe"], bufs["dv"], bufs["sc"]
    ones1, ones128 = bufs["ones1"], bufs["ones128"]
    sq, row = bufs["sq"], bufs["row"]
    p_m = bufs["p_m"][0:1, :]
    p_sq = bufs["p_sq"][0:1, :]
    p_rb, p_mrb = bufs["p_rb"], bufs["p_mrb"]

    for k in range(8):
        pk = k % 2
        waits = list(ld_wait)
        if k >= 2:
            waits.append((pe, pe.v))
        S.op("scalar", (lambda k=k, pk=pk, tsl=tsl: lambda e: e.activation(
            out=sq[:, pk, :], in_=X[:, k, tsl], func=AF.Square))(),
            waits=waits, incs=[(sc, 1)])
        S.op("tensor", (lambda k=k, tsl=tsl: lambda e: e.matmul(
            p_m[:, :], ones1[:, :], X[:, k, tsl],
            start=(k == 0), stop=(k == 7)))(),
            waits=list(ld_wait) if k == 0 else [])
        S.op("tensor", (lambda k=k, pk=pk: lambda e: e.matmul(
            p_sq[:, :], ones1[:, :], sq[:, pk, :],
            start=(k == 0), stop=(k == 7)))(),
            waits=[(sc, sc.v)], incs=[(pe, 1)])
    pe_stats = emit_drain(S, bufs, bufs["ln_junk"])
    m_r = row[:, 0:512]; ms_r = row[:, 512:1024]; var_r = row[:, 1024:1536]
    r_r = row[:, 1536:2048]; mr_r = row[:, 512:1024]
    S.op("vector", lambda e: e.tensor_single_scalar(m_r, p_m[:, :], 1.0 / D, ALU.mult),
         waits=[(pe, pe_stats)])
    S.op("vector", lambda e: e.tensor_single_scalar(ms_r, p_sq[:, :], 1.0 / D, ALU.mult))
    S.op("vector", lambda e: e.tensor_mul(var_r, m_r, m_r))
    S.op("vector", lambda e: e.tensor_sub(var_r, ms_r, var_r), incs=[(dv, 1)])
    eps_ap = bufs["cb"][0:1, 0:1] if eps == EPS_LN else bufs["cb"][0:1, 2:3]
    S.op("scalar", (lambda var_r=var_r, eps_ap=eps_ap: lambda e: e.activation(
        out=var_r, in_=var_r, func=AF.Ln, bias=eps_ap))(),
        waits=[(dv, dv.v)])
    S.op("scalar", lambda e: e.activation(out=r_r, in_=var_r, func=AF.Exp, scale=-0.5),
         incs=[(sc, 1)])
    if rms:
        S.op("scalar", lambda e: e.activation(out=bufs["tmprow"][0:1, 0:512], in_=var_r,
                                              func=AF.Exp, scale=-1.0), incs=[(sc, 1)])
        S.op("vector", lambda e: e.tensor_single_scalar(
            bufs["tmprow"][0:1, 512:1024], bufs["tmprow"][0:1, 0:512], -EPS_LN, ALU.mult),
            waits=[(sc, sc.v)])
        S.op("vector", lambda e: e.tensor_single_scalar(
            bufs["tmprow"][0:1, 512:1024], bufs["tmprow"][0:1, 512:1024], 1.0, ALU.add),
            incs=[(dv, 1)])
        S.op("scalar", lambda e: e.activation(out=bufs["tmprow"][0:1, 512:1024],
                                              in_=bufs["tmprow"][0:1, 512:1024],
                                              func=AF.Ln, bias=bufs["cb"][0:1, 2:3]),
             waits=[(dv, dv.v)])
        S.op("scalar", lambda e: e.activation(out=bufs["tmprow"][0:1, 512:1024],
                                              in_=bufs["tmprow"][0:1, 512:1024],
                                              func=AF.Exp, scale=-0.5), incs=[(sc, 1)])
        S.op("vector", lambda e: e.tensor_mul(r_r, r_r, bufs["tmprow"][0:1, 512:1024]),
             waits=[(sc, sc.v)])
    S.op("vector", lambda e: e.tensor_mul(mr_r, m_r, r_r),
         waits=[(sc, sc.v)], incs=[(dv, 1)])
    S.op("tensor", lambda e: e.matmul(p_rb[:, :], ones128[:, :], r_r, start=True, stop=True),
         waits=[(dv, dv.v)])
    S.op("tensor", lambda e: e.matmul(p_mrb[:, :], ones128[:, :], mr_r, start=True, stop=True))
    pe_bc = emit_drain(S, bufs, bufs["ln_junk"])
    for k in range(8):
        S.op("vector", (lambda k=k, tsl=tsl: lambda e: e.tensor_mul(
            bufs["tmp8"][:, :], X[:, k, tsl], p_rb[:, :]))(),
            waits=[(pe, pe_bc)] if k == 0 else [])
        if nw is None:
            S.op("vector", (lambda k=k: lambda e: e.tensor_sub(
                out_ap(k), bufs["tmp8"][:, :], p_mrb[:, :]))(),
                incs=[(dv, 1)] if k == 7 else [])
        else:
            S.op("vector", (lambda k=k: lambda e: e.tensor_sub(
                bufs["tmp8"][:, :], bufs["tmp8"][:, :], p_mrb[:, :]))())
            S.op("vector", (lambda k=k: lambda e: e.tensor_scalar(
                out_ap(k), bufs["tmp8"][:, :], nw[:, k:k + 1], None, ALU.mult))(),
                incs=[(dv, 1)] if k == 7 else [])
    S.op("tensor", lambda e: None, waits=[(dv, dv.v)])
    return dv.v


def emit_layernorm(S, nc, bufs, X, xn, seq_len, ld_wait, eps=EPS_LN):
    for tc in range(seq_len // 512):
        tsl = slice(tc * 512, (tc + 1) * 512)
        emit_ln_chunk(S, bufs, X, tsl, ld_wait,
                      (lambda tsl=tsl: lambda k: xn[:, k, tsl])(), eps=eps)
    return bufs["dv"].v


def alloc_common(S, nc, ctx):
    sb = lambda name, shape, dt=F32: ctx.enter_context(nc.sbuf_tensor(name, shape, dt))
    ps = lambda name, shape, dt=F32: ctx.enter_context(nc.psum_tensor(name, shape, dt))
    bufs = {
        "pe": S.sem("pe"), "dv": S.sem("dv"), "sc": S.sem("sc"),
        "ld": S.sem("ld"), "gq": S.sem("gq"), "st": S.sem("st"),
        "ones1": sb("ones1", [128, 1]),
        "ones128": sb("ones128", [1, 128]),
        "onest": sb("onest", [128, 256]),
        "cb": sb("cb", [128, 8]),
        "sq": sb("sq", [128, 2, 512]),
        "row": sb("row", [1, 4 * 512]),
        "tmprow": sb("tmprow", [1, 2 * 512]),
        "tmp8": sb("tmp8", [128, 512]),
    }
    S.fence_buf = sb("fence", [128, 2])
    dv = bufs["dv"]
    S.op("vector", lambda e: e.memset(bufs["ones1"][:, :], 1.0), incs=[(dv, 1)])
    S.op("vector", lambda e: e.memset(bufs["ones128"][:, :], 1.0), incs=[(dv, 1)])
    S.op("vector", lambda e: e.memset(bufs["onest"][:, :], 1.0), incs=[(dv, 1)])
    S.op("vector", lambda e: e.memset(bufs["cb"][:, 0:1], EPS_LN), incs=[(dv, 1)])
    S.op("vector", lambda e: e.memset(bufs["cb"][:, 1:2], -SHIFT), incs=[(dv, 1)])
    S.op("vector", lambda e: e.memset(bufs["cb"][:, 2:3], EPS_RMS), incs=[(dv, 1)])
    bufs["init_dv"] = dv.v
    return bufs, sb, ps



def emit_drain(S, bufs, junk_view):
    pe = bufs["pe"]
    S.op("tensor", lambda e: e.matmul(junk_view, bufs["ones1"][0:1, 0:1],
                                      bufs["onest"][0:1, 0:256], start=True, stop=True),
         incs=[(pe, 1)])
    return pe.v


def emit_attn_head(S, bufs, *, qS, kS, vaug_v, X_add, h, qsl, kt_lo, kt_hi,
                   pt, p1buf, p_scA, p_scB, p_sums, p_o, p_rb2, p_sh,
                   osb, rro, t1, id64, att_junk, mask_sb=None, mask_tiles=(),
                   scale=0.125, c_pass1=8.0, nq=512):
    pe, dv, sc = bufs["pe"], bufs["dv"], bufs["sc"]
    nkt = kt_hi - kt_lo
    for i, kt in enumerate(range(kt_lo, kt_hi)):
        bank = p_scA if i % 2 == 0 else p_scB
        w = [(sc, sc.v)] if i >= 2 else [(dv, dv.v)]
        S.op("tensor", (lambda kt=kt, bank=bank, i=i: lambda e: e.matmul(
            bank[:, 0:nq], kS[0:64, kt * 128:(kt + 1) * 128], qS[0:64, qsl],
            start=True, stop=True))(), waits=w, incs=[(pe, 1)])
        if kt in mask_tiles:
            S.op("vector", (lambda kt=kt, bank=bank: lambda e: e.tensor_add(
                bank[:, 0:nq], bank[:, 0:nq], mask_tiles[kt]))(),
                waits=[(pe, pe.v)], incs=[(dv, 1)])
            S.op("scalar", (lambda i=i, bank=bank: lambda e: e.activation(
                out=p1buf[:, (i % 2) * nq:(i % 2) * nq + nq], in_=bank[:, 0:nq],
                func=AF.Exp, scale=scale / c_pass1))(),
                waits=[(dv, dv.v)], incs=[(sc, 1)])
        else:
            S.op("scalar", (lambda i=i, bank=bank: lambda e: e.activation(
                out=p1buf[:, (i % 2) * nq:(i % 2) * nq + nq], in_=bank[:, 0:nq],
                func=AF.Exp, scale=scale / c_pass1))(),
                waits=[(pe, pe.v)], incs=[(sc, 1)])
        sacc = bufs["sacc"]
        if i == 0:
            S.op("vector", lambda e: e.tensor_copy(sacc[:, 0:nq], p1buf[:, 0:nq]),
                 waits=[(sc, sc.v)])
        else:
            S.op("vector", (lambda i=i: lambda e: e.tensor_add(
                sacc[:, 0:nq], sacc[:, 0:nq],
                p1buf[:, (i % 2) * nq:(i % 2) * nq + nq]))(),
                waits=[(sc, sc.v)], incs=[(dv, 1)] if i == nkt - 1 else [])
    S.op("tensor", lambda e: e.matmul(p_sums[0:1, 0:nq], bufs["onesc_bf"][:, 0:1],
                                      sacc[:, 0:nq], start=True, stop=True),
         waits=[(dv, dv.v)])
    pe_s1 = emit_drain(S, bufs, att_junk)
    S.op("scalar", lambda e: e.activation(out=rro[0:1, 0:nq], in_=p_sums[0:1, 0:nq],
                                          func=AF.Ln),
         waits=[(pe, pe_s1)], incs=[(sc, 1)])
    S.op("vector", (lambda: lambda e: e.tensor_single_scalar(
        rro[0:1, 0:nq], rro[0:1, 0:nq], -c_pass1 / scale, ALU.mult))(),
        waits=[(sc, sc.v)], incs=[(dv, 1)])
    S.op("tensor", lambda e: e.matmul(p_sh[64:65, 0:nq], bufs["ones1"][0:1, 0:1],
                                      rro[0:1, 0:nq], start=True, stop=True),
         waits=[(dv, dv.v)])
    pe_m = emit_drain(S, bufs, att_junk)
    S.op("vector", (lambda: lambda e: e.tensor_copy(qS[64:65, qsl], p_sh[64:65, 0:nq]))(),
         waits=[(pe, pe_m)], incs=[(dv, 1)])
    dv_m = dv.v
    sc_tiles = []
    for i, kt in enumerate(range(kt_lo, kt_hi)):
        bank = p_scA if i % 2 == 0 else p_scB
        w = [(dv, dv_m)] if i < 2 else [(sc, sc.v)]
        S.op("tensor", (lambda kt=kt, bank=bank: lambda e: e.matmul(
            bank[:, 0:nq], kS[0:65, kt * 128:(kt + 1) * 128], qS[0:65, qsl],
            start=True, stop=True))(), waits=w, incs=[(pe, 1)])
        if kt in mask_tiles:
            S.op("vector", (lambda kt=kt, bank=bank: lambda e: e.tensor_add(
                bank[:, 0:nq], bank[:, 0:nq], mask_tiles[kt]))(),
                waits=[(pe, pe.v)], incs=[(dv, 1)])
            S.op("scalar", (lambda i=i, bank=bank: lambda e: e.activation(
                out=pt[:, i * nq:(i + 1) * nq], in_=bank[:, 0:nq],
                func=AF.Exp, scale=scale))(),
                waits=[(dv, dv.v)], incs=[(sc, 1)])
        else:
            S.op("scalar", (lambda i=i, bank=bank: lambda e: e.activation(
                out=pt[:, i * nq:(i + 1) * nq], in_=bank[:, 0:nq],
                func=AF.Exp, scale=scale))(),
                waits=[(pe, pe.v)], incs=[(sc, 1)])
        sc_tiles.append(sc.v)
    for i, kt in enumerate(range(kt_lo, kt_hi)):
        S.op("tensor", (lambda kt=kt, i=i, nkt=nkt: lambda e: e.matmul(
            p_o[0:65, 0:nq], vaug_v[:, kt, :], pt[:, i * nq:(i + 1) * nq],
            start=(i == 0), stop=(i == nkt - 1)))(),
            waits=[(sc, sc_tiles[min(i + 2, nkt - 1)])])
    pe_av = emit_drain(S, bufs, att_junk)
    S.op("vector", lambda e: e.tensor_copy(osb[:, 0:nq], p_o[0:65, 0:nq]),
         waits=[(pe, pe_av)])
    S.op("vector", lambda e: e.tensor_copy(rro[64:65, 0:nq], osb[64:65, 0:nq]))
    S.op("vector", lambda e: e.reciprocal(rro[64:65, 0:nq], rro[64:65, 0:nq]),
         incs=[(dv, 1)])
    S.op("tensor", lambda e: e.matmul(p_rb2[0:64, 0:nq], bufs["onest"][64:65, 0:64],
                                      rro[64:65, 0:nq], start=True, stop=True),
         waits=[(dv, dv.v)])
    pe_rb2 = emit_drain(S, bufs, att_junk)
    S.op("vector", lambda e: e.tensor_mul(t1[:, 0:nq], osb[0:64, 0:nq], p_rb2[0:64, 0:nq]),
         waits=[(pe, pe_rb2)], incs=[(dv, 1)])
    X_add(t1, p_sh, id64)
    S.op("tensor", lambda e: None, waits=[(dv, dv.v)])


def build_stage1(serialize=False, chained=False):
    nc = bass.Bass("TRN2", target_bir_lowering=False, debug=False)
    x0T = nc.dram_tensor("x0T", [D, TOK], F32, kind="ExternalInput").ap()
    wqkT = nc.dram_tensor("wqkT", [D, 2 * D], F32, kind="ExternalInput").ap()
    wvT = nc.dram_tensor("wvT", [D, D], F32, kind="ExternalInput").ap()
    idhi = nc.dram_tensor("idhi", [128, 64], F32, kind="ExternalInput").ap()
    x1T = nc.dram_tensor("x1T", [D, TOK], F32, kind="ExternalOutput").ap()

    S = Sched(serialize=serialize, chained=chained)
    ctx = ExitStack()
    bufs, sb, ps = alloc_common(S, nc, ctx)
    pe, dv, sc = bufs["pe"], bufs["dv"], bufs["sc"]
    ld, gq, st = bufs["ld"], bufs["gq"], bufs["st"]

    X = sb("X", [128, 8, T])
    xn = sb("xn", [128, 8, T], BF16)
    vaug = sb("vaug", [128, H * 16 * 65], BF16)
    qS = sb("qS", [65, 2 * T], BF16)
    kS = sb("kS", [65, 2 * T], BF16)
    pt = sb("pt", [128, 16 * 512], BF16)
    p1buf = sb("p1buf", [128, 2 * 512], BF16)
    wv = sb("wv", [128, 8, 512], BF16)
    wqk = sb("wqk", [128, 2, 8, 256], BF16)
    id64 = sb("id64", [128, 64])
    osb = sb("osb", [65, 512])
    rro = sb("rro", [128, 512])
    bufs["onesc_bf"] = sb("onesc_bf", [128, 1], BF16)
    bufs["sacc"] = pt[:, 15 * 512:16 * 512]
    t1 = bufs["tmp8"][0:64, :]

    P = [ps(f"P{i}", [128, 512]) for i in range(8)]
    bufs["p_m"], bufs["p_sq"], bufs["p_rb"], bufs["p_mrb"] = P[0], P[1], P[2], P[3]
    bufs["ln_junk"] = P[5][0:1, 0:256]
    p_qk = P[4]
    p_scA, p_scB = P[2], P[3]
    p_sums = P[1]
    p_o = P[4]
    p_rb2 = P[6]
    p_sh = P[5]
    att_junk = P[0][0:1, 0:256]

    vaug_vh = vaug.rearrange("p (h t c) -> p h t c", h=H, c=65)
    S.op("vector", lambda e: e.memset(vaug_vh[:, :, :, 64:65], 1.0), incs=[(dv, 1)])
    S.op("vector", lambda e: e.memset(kS[64:65, :], 1.0), incs=[(dv, 1)])
    S.op("vector", lambda e: e.memset(bufs["onesc_bf"][:, :], 1.0), incs=[(dv, 1)])
    init_dv = dv.v

    S.dma("sync", id64[:, :], idhi[:, :], sem=ld)
    ld_id = ld.v


    for s in range(SEQ_PER_CORE):
        t0 = s * T
        x_src = x0T[:, t0:t0 + T].rearrange("(c p) t -> p c t", p=128)
        S.dma("sync", X[:, :, :], x_src, waits=[(st, st.v)], sem=ld)
        ld_x = ld.v

        dv_xn = emit_layernorm(S, nc, bufs, X, xn, T,
                               [(ld, ld_x)] + ([(bufs["dv"], bufs["init_dv"])] if s == 0 else []))

        for half in range(2):
            S.op("gpsimd", lambda e: None, waits=[(pe, pe.v)])
            ld_wv = S.dma("gpsimd", wv[:, :, :],
                          wvT[:, half * 512:(half + 1) * 512].rearrange("(c p) f -> p c f", p=128),
                          sem=gq)
            for tt in range(16):
                for k in range(8):
                    S.op("tensor", (lambda tt=tt, k=k: lambda e: e.matmul(
                        p_qk[:, :], xn[:, k, tt * 128:(tt + 1) * 128],
                        wv[:, k, :],
                        start=(k == 0), stop=(k == 7)))(),
                        waits=([(dv, dv_xn), (gq, ld_wv)] if (tt == 0 and k == 0) else []))
                pe_v = emit_drain(S, bufs, att_junk)
                for hh in range(8):
                    h = half * 8 + hh
                    S.op("vector", (lambda tt=tt, h=h, hh=hh: lambda e: e.tensor_copy(
                        vaug_vh[:, h, tt, 0:64], p_qk[:, hh * 64:(hh + 1) * 64]))(),
                        waits=[(pe, pe_v)] if hh == 0 else [],
                        incs=[(dv, 1)] if hh == 7 else [])
                S.op("tensor", lambda e: None, waits=[(dv, dv.v)])

        for g in range(H // 2):
            gb = g % 2
            wsrc = wqkT[:, g * 256:(g + 1) * 256].rearrange("(c p) f -> p c f", p=128)
            if s > 0 or g >= 2:
                S.op("gpsimd", lambda e: None, waits=[(pe, pe.v)])
            ld_w = S.dma("gpsimd", wqk[:, gb, :, :], wsrc, sem=gq)

            for dst, c0 in ((qS, 0), (kS, 128)):
                for ncx in range(4):
                    for k in range(8):
                        S.op("tensor", (lambda gb=gb, ncx=ncx, k=k, c0=c0: lambda e: e.matmul(
                            p_qk[:, :], wqk[:, gb, k, c0:c0 + 128],
                            xn[:, k, ncx * 512:(ncx + 1) * 512],
                            start=(k == 0), stop=(k == 7)))(),
                            waits=[(gq, ld_w), (ld, ld_id)] if (k == 0 and ncx == 0 and c0 == 0) else [])
                    pe_qk = emit_drain(S, bufs, att_junk)
                    S.op("vector", (lambda dst=dst, ncx=ncx: lambda e: e.tensor_copy(
                        dst[0:64, ncx * 512:(ncx + 1) * 512], p_qk[0:64, :]))(),
                        waits=[(pe, pe_qk)])
                    S.op("vector", (lambda ncx=ncx: lambda e: e.tensor_copy(
                        bufs["tmp8"][64:128, :], p_qk[64:128, :]))(),
                        incs=[(dv, 1)])
                    S.op("tensor", lambda e: e.matmul(
                        p_sh[0:64, :], id64[64:128, :], bufs["tmp8"][64:128, :],
                        start=True, stop=True),
                        waits=[(dv, dv.v)])
                    pe_sh2 = emit_drain(S, bufs, att_junk)
                    S.op("vector", (lambda dst=dst, ncx=ncx: lambda e: e.tensor_copy(
                        dst[0:64, T + ncx * 512:T + (ncx + 1) * 512], p_sh[0:64, :]))(),
                        waits=[(pe, pe_sh2)], incs=[(dv, 1)])
                    S.op("tensor", lambda e: None, waits=[(dv, dv.v)])
            dv_qk = dv.v

            for sub in range(2):
                h = 2 * g + sub
                qSh = qS[:, sub * T:(sub + 1) * T]
                kSh = kS[:, sub * T:(sub + 1) * T]
                c = h // 2
                for qc in range(4):
                    qsl = slice(qc * 512, (qc + 1) * 512)

                    def X_add(t1v, p_shv, id64v, c=c, qsl=qsl, sub=sub):
                        if sub == 0:
                            S.op("vector", (lambda: lambda e: e.tensor_add(
                                X[0:64, c, qsl], X[0:64, c, qsl], t1v[:, 0:512]))(),
                                incs=[(dv, 1)])
                        else:
                            S.op("tensor", (lambda: lambda e: e.matmul(
                                p_shv[64:128, :], id64v[0:64, :],
                                t1v[:, 0:512], start=True, stop=True))(),
                                waits=[(dv, dv.v)])
                            pe_sh = emit_drain(S, bufs, att_junk)
                            S.op("vector", (lambda: lambda e: e.tensor_add(
                                X[64:128, c, qsl], X[64:128, c, qsl], p_shv[64:128, :]))(),
                                waits=[(pe, pe_sh)], incs=[(dv, 1)])

                    emit_attn_head(S, bufs, qS=qSh, kS=kSh, vaug_v=vaug_vh[:, h, :, :],
                                   X_add=X_add, h=h, qsl=qsl, kt_lo=0, kt_hi=16,
                                   pt=pt, p1buf=p1buf, p_scA=p_scA, p_scB=p_scB,
                                   p_sums=p_sums, p_o=p_o, p_rb2=p_rb2, p_sh=p_sh,
                                   osb=osb, rro=rro, t1=t1, id64=id64,
                                   att_junk=att_junk)

        x_dst = x1T[:, t0:t0 + T].rearrange("(c p) t -> p c t", p=128)
        S.dma("sync", x_dst, X[:, :, :], waits=[(dv, dv.v)], sem=st)

    S.op("sync", lambda e: None, waits=[(st, st.v)])
    S.build(nc, ctx)
    ctx.close()
    return nc


def shard_tokens_fm(x_full):
    out = []
    for c in range(N_CORES):
        sl = x_full[c * SEQ_PER_CORE:(c + 1) * SEQ_PER_CORE].reshape(TOK, D)
        out.append(np.ascontiguousarray(np.asarray(sl, dtype=np.float32).T))
    return out


def prep_wqkv(W_qkv):
    W = np.asarray(W_qkv, dtype=np.float32).reshape(H, 3, DH, D)
    pairs = []
    for g in range(H // 2):
        pairs.append(W[2 * g, 0]); pairs.append(W[2 * g + 1, 0])
        pairs.append(W[2 * g, 1]); pairs.append(W[2 * g + 1, 1])
    qk = np.concatenate(pairs, 0)
    v = W[:, 2].reshape(H * DH, D)
    return np.ascontiguousarray(qk.T), np.ascontiguousarray(v.T)






def build_stage2(serialize=False, chained=False):
    nc = bass.Bass("TRN2", target_bir_lowering=False, debug=False)
    x1T = nc.dram_tensor("x1T", [D, TOK], F32, kind="ExternalInput").ap()
    wqTb = nc.dram_tensor("wqTb", [D, D], BF16, kind="ExternalInput").ap()
    wscale = nc.dram_tensor("wscale", [128, 8], F32, kind="ExternalInput").ap()
    fbias = nc.dram_tensor("fbias", [128, 8], F32, kind="ExternalInput").ap()
    idf_in = nc.dram_tensor("idf", [128, 128], F32, kind="ExternalInput").ap()
    x2T = nc.dram_tensor("x2T", [D, TOK], F32, kind="ExternalOutput").ap()

    S = Sched(serialize=serialize, chained=chained)
    ctx = ExitStack()
    bufs, sb, ps = alloc_common(S, nc, ctx)
    pe, dv, sc = bufs["pe"], bufs["dv"], bufs["sc"]
    ld, gq, st = bufs["ld"], bufs["gq"], bufs["st"]

    X = sb("X", [128, 8, T])
    xnf = sb("xnf", [128, 8, 512])
    xq = sb("xq", [128, 8, 512], BF16)
    wq = sb("wq", [128, 8, D], BF16)
    wsc = sb("wsc", [128, 8])
    fbc = sb("fbc", [128, 8])
    bufs["idf"] = sb("idf_sb", [128, 128])
    amaxc = sb("amaxc", [128, 2])
    rrow = sb("rrow", [1, 3 * 512])
    ti = sb("ti", [128, 512], I32)
    tf = sb("tf", [128, 512])
    td = sb("td", [128, 512])
    tg = sb("tg", [128, 512])
    t2 = sb("t2", [128, 512])

    P = [ps(f"P{i}", [128, 512]) for i in range(8)]
    bufs["p_m"], bufs["p_sq"], bufs["p_rb"], bufs["p_mrb"] = P[0], P[1], P[2], P[3]
    bufs["ln_junk"] = P[5][0:1, 0:256]
    p_tr = P[4]
    p_out = P[5]
    p_xs = P[6]
    p_junk = P[7][0:1, 0:256]

    ld_w = S.dma("gpsimd", wq[:, :, :], wqTb.rearrange("(c p) f -> p c f", p=128), sem=gq)
    S.dma("sync", wsc[:, :], wscale[:, :], sem=ld)
    S.dma("sync", fbc[:, :], fbias[:, :], sem=ld)
    S.dma("sync", bufs["idf"][:, :], idf_in[:, :], sem=ld)
    ld_const = ld.v

    for s in range(SEQ_PER_CORE):
        t0 = s * T
        x_src = x1T[:, t0:t0 + T].rearrange("(c p) t -> p c t", p=128)
        S.dma("sync", X[:, :, :], x_src, waits=[(st, st.v)], sem=ld)
        ld_x = ld.v

        for tc in range(4):
            tsl = slice(tc * 512, (tc + 1) * 512)
            dv_ln = emit_ln_chunk(
                S, bufs, X, tsl,
                [(ld, ld_x)] + ([(dv, bufs["init_dv"]), (ld, ld_const)] if (s == 0 and tc == 0) else []),
                lambda k: xnf[:, k, :])
            for tt in range(4):
                for k in range(8):
                    S.op("tensor", (lambda k=k, tt=tt: lambda e: e.transpose(
                        p_tr[:, 0:128], xnf[:, k, tt * 128:(tt + 1) * 128], bufs["idf"][:, :]))(),
                        waits=[(dv, dv_ln)] if (tt == 0 and k == 0) else [])
                    pe_t = emit_drain(S, bufs, p_junk)
                    S.op("vector", (lambda k=k: lambda e: e.tensor_reduce(
                        amaxc[:, 1:2], p_tr[:, 0:128], AX.X, ALU.max,
                        apply_absolute_value=True))(),
                        waits=[(pe, pe_t)])
                    if k == 0:
                        S.op("vector", lambda e: e.tensor_copy(amaxc[:, 0:1], amaxc[:, 1:2]),
                             waits=[(pe, pe.v)], incs=[(dv, 1)])
                    else:
                        S.op("vector", lambda e: e.tensor_tensor(
                            amaxc[:, 0:1], amaxc[:, 0:1], amaxc[:, 1:2], ALU.max),
                            incs=[(dv, 1)])
                    S.op("tensor", lambda e: None, waits=[(dv, dv.v)])
                S.op("tensor", (lambda tt=tt: lambda e: e.transpose(
                    p_tr[0:1, 0:128], amaxc[:, 0:1], bufs["idf"][:, :]))())
                pe_t2 = emit_drain(S, bufs, p_junk)
                S.op("vector", (lambda tt=tt: lambda e: e.tensor_copy(
                    rrow[0:1, tt * 128:(tt + 1) * 128], p_tr[0:1, 0:128]))(),
                    waits=[(pe, pe_t2)], incs=[(dv, 1)])
                S.op("tensor", lambda e: None, waits=[(dv, dv.v)])
            S.op("vector", lambda e: e.tensor_single_scalar(
                rrow[0:1, 0:512], rrow[0:1, 0:512], 1e-6, ALU.max))
            S.op("vector", lambda e: e.tensor_single_scalar(
                rrow[0:1, 512:1024], rrow[0:1, 0:512], 1.0 / 127.0, ALU.mult))
            S.op("vector", lambda e: e.reciprocal(rrow[0:1, 1024:1536], rrow[0:1, 512:1024]),
                 incs=[(dv, 1)])
            S.op("tensor", lambda e: e.matmul(p_tr[:, :], bufs["ones128"][:, :],
                                              rrow[0:1, 1024:1536], start=True, stop=True),
                 waits=[(dv, dv.v)])
            S.op("tensor", lambda e: e.matmul(p_xs[:, :], bufs["ones128"][:, :],
                                              rrow[0:1, 512:1024], start=True, stop=True))
            pe_bc = emit_drain(S, bufs, p_junk)
            for k in range(8):
                S.op("vector", (lambda k=k: lambda e: e.tensor_mul(
                    tf[:, :], xnf[:, k, :], p_tr[:, :]))(),
                    waits=[(pe, pe_bc)] if k == 0 else [])
                S.op("vector", lambda e: e.tensor_copy(ti[:, :], tf[:, :]))
                S.op("vector", lambda e: e.tensor_copy(td[:, :], ti[:, :]))
                S.op("vector", lambda e: e.tensor_sub(tg[:, :], tf[:, :], td[:, :]))
                S.op("vector", lambda e: e.scalar_tensor_tensor(
                    t2[:, :], tg[:, :], 0.5, td[:, :], ALU.is_ge, ALU.add))
                S.op("vector", lambda e: e.tensor_single_scalar(
                    tg[:, :], tg[:, :], -0.5, ALU.is_le))
                S.op("vector", (lambda k=k: lambda e: e.tensor_sub(
                    xq[:, k, :], t2[:, :], tg[:, :]))(),
                    incs=[(dv, 1)] if k == 7 else [])
            dv_q = dv.v
            for o in range(8):
                for k in range(8):
                    S.op("tensor", (lambda o=o, k=k: lambda e: e.matmul(
                        p_out[:, :], wq[:, k, o * 128:(o + 1) * 128], xq[:, k, :],
                        start=(k == 0), stop=(k == 7)))(),
                        waits=[(dv, dv_q), (gq, ld_w)] if (o == 0 and k == 0) else
                              ([(dv, dv.v)] if k == 0 else []))
                pe_o = emit_drain(S, bufs, p_junk)
                S.op("scalar", (lambda o=o: lambda e: e.activation(
                    out=t2[:, :], in_=p_out[:, :], func=AF.Copy, scale=wsc[:, o:o + 1]))(),
                    waits=[(pe, pe_o)], incs=[(sc, 1)])
                S.op("vector", lambda e: e.tensor_mul(t2[:, :], t2[:, :], p_xs[:, :]),
                     waits=[(sc, sc.v)])
                S.op("vector", (lambda o=o: lambda e: e.tensor_scalar(
                    t2[:, :], t2[:, :], fbc[:, o:o + 1], None, ALU.add))(),)
                S.op("vector", (lambda o=o, tsl=tsl: lambda e: e.tensor_add(
                    X[:, o, tsl], X[:, o, tsl], t2[:, :]))(),
                    incs=[(dv, 1)])
            S.op("tensor", lambda e: None, waits=[(dv, dv.v)])

        x_dst = x2T[:, t0:t0 + T].rearrange("(c p) t -> p c t", p=128)
        S.dma("sync", x_dst, X[:, :, :], waits=[(dv, dv.v)], sem=st)

    S.op("sync", lambda e: None, waits=[(st, st.v)])
    S.build(nc, ctx)
    ctx.close()
    return nc



def build_stage3(serialize=False, chained=False):
    nc = bass.Bass("TRN2", target_bir_lowering=False, debug=False)
    x2T = nc.dram_tensor("x2T", [D, TOK], F32, kind="ExternalInput").ap()
    wqkT3 = nc.dram_tensor("wqkT3", [D, 2 * D], F32, kind="ExternalInput").ap()
    wvT3 = nc.dram_tensor("wvT3", [D, D], F32, kind="ExternalInput").ap()
    woT = nc.dram_tensor("woT", [D, D], F32, kind="ExternalInput").ap()
    nw_in = nc.dram_tensor("nw", [128, 8], F32, kind="ExternalInput").ap()
    bqk_in = nc.dram_tensor("bqk", [1, 2 * D], F32, kind="ExternalInput").ap()
    bv_in = nc.dram_tensor("bv", [1, D], F32, kind="ExternalInput").ap()
    bo_in = nc.dram_tensor("bo", [1, D], F32, kind="ExternalInput").ap()
    idhi = nc.dram_tensor("idhi", [128, 64], F32, kind="ExternalInput").ap()
    cmask_in = nc.dram_tensor("cmask", [128, 896], F32, kind="ExternalInput").ap()
    x3T = nc.dram_tensor("x3T", [D, TOK], F32, kind="ExternalOutput").ap()

    S = Sched(serialize=serialize, chained=chained)
    ctx = ExitStack()
    bufs, sb, ps = alloc_common(S, nc, ctx)
    pe, dv, sc = bufs["pe"], bufs["dv"], bufs["sc"]
    ld, gq, st = bufs["ld"], bufs["gq"], bufs["st"]

    X = sb("X", [128, 8, T])
    xn = sb("xn", [128, 8, T], BF16)
    vaug = sb("vaug", [128, 8 * 16 * 65], BF16)
    qS = sb("qS", [65, 2 * T], BF16)
    kS = sb("kS", [65, 2 * T], BF16)
    Ob = sb("Ob", [128, 4, T], BF16)
    pt = sb("pt", [128, 16 * 512], BF16)
    p1buf = pt[:, 0:2 * 512]
    wshare = sb("wshare", [128, 4096], BF16)
    wv = wshare.rearrange("p (a b) -> p a b", a=8)
    wo = wshare.rearrange("p (a b) -> p a b", a=4)
    wqk3 = sb("wqk", [128, 1, 8, 256], BF16)
    id64 = sb("id64", [128, 64])
    osb = sb("osb", [65, 512])
    rro = sb("rro", [128, 512])
    nw = sb("nw_sb", [128, 8])
    brow = sb("brow", [1, 2 * D], BF16)
    bvrow = sb("bvrow", [1, D], BF16)
    borow = sb("borow", [1, D], BF16)
    onesr = sb("onesr", [1, 512], BF16)
    cmask = sb("cmask_sb", [128, 896], BF16)
    bufs["onesc_bf"] = sb("onesc_bf", [128, 1], BF16)
    bufs["sacc"] = pt[:, 15 * 512:16 * 512]
    t1 = bufs["tmp8"][0:64, :]

    P = [ps(f"P{i}", [128, 512]) for i in range(8)]
    bufs["p_m"], bufs["p_sq"], bufs["p_rb"], bufs["p_mrb"] = P[0], P[1], P[2], P[3]
    bufs["ln_junk"] = P[5][0:1, 0:256]
    p_qk = P[4]
    p_scA, p_scB = P[2], P[3]
    p_sums = P[1]
    p_o = P[4]
    p_rb2 = P[6]
    p_sh = P[5]
    att_junk = P[0][0:1, 0:256]
    p_wo = P[7]

    vaug_vh = vaug.rearrange("p (h t c) -> p h t c", h=8, c=65)
    S.op("vector", lambda e: e.memset(vaug_vh[:, :, :, 64:65], 1.0), incs=[(dv, 1)])
    S.op("vector", lambda e: e.memset(kS[64:65, :], 1.0), incs=[(dv, 1)])
    S.op("vector", lambda e: e.memset(bufs["onesc_bf"][:, :], 1.0), incs=[(dv, 1)])
    S.op("vector", lambda e: e.memset(onesr[:, :], 1.0), incs=[(dv, 1)])
    init_dv = dv.v

    S.dma("sync", id64[:, :], idhi[:, :], sem=ld)
    S.dma("sync", nw[:, :], nw_in[:, :], sem=ld)
    S.dma("gpsimd", cmask[:, :], cmask_in[:, :], sem=gq)
    S.dma("gpsimd", brow[:, :], bqk_in[:, :], sem=gq)
    S.dma("gpsimd", bvrow[:, :], bv_in[:, :], sem=gq)
    S.dma("gpsimd", borow[:, :], bo_in[:, :], sem=gq)
    ld_const = ld.v
    gq_const = gq.v

    for s in range(SEQ_PER_CORE):
        t0 = s * T
        x_src = x2T[:, t0:t0 + T].rearrange("(c p) t -> p c t", p=128)
        S.dma("sync", X[:, :, :], x_src, waits=[(st, st.v)], sem=ld)
        ld_x = ld.v

        dv_xn = 0
        for tc in range(4):
            tsl = slice(tc * 512, (tc + 1) * 512)
            w0 = [(ld, ld_x)]
            if s == 0 and tc == 0:
                w0 += [(dv, init_dv), (ld, ld_const), (gq, gq_const)]
            dv_xn = emit_ln_chunk(S, bufs, X, tsl, w0,
                                  (lambda tsl=tsl: lambda k: xn[:, k, tsl])(),
                                  rms=True, nw=nw)

        for half in range(2):
            S.op("gpsimd", lambda e: None, waits=[(pe, pe.v)])
            ld_wv = S.dma("gpsimd", wv[:, :, :],
                          wvT3[:, half * 512:(half + 1) * 512].rearrange("(c p) f -> p c f", p=128),
                          sem=gq)
            for tt in range(16):
                for k in range(8):
                    S.op("tensor", (lambda tt=tt, k=k: lambda e: e.matmul(
                        p_qk[:, :], xn[:, k, tt * 128:(tt + 1) * 128], wv[:, k, :],
                        start=(k == 0), stop=False))(),
                        waits=([(dv, dv_xn), (gq, ld_wv)] if (tt == 0 and k == 0) else []))
                S.op("tensor", (lambda tt=tt, half=half: lambda e: e.matmul(
                    p_qk[:, :], onesr[0:1, 0:128], bvrow[0:1, half * 512:(half + 1) * 512],
                    start=False, stop=True))())
                pe_v = emit_drain(S, bufs, att_junk)
                for hh in range(8):
                    S.op("vector", (lambda tt=tt, hh=hh: lambda e: e.tensor_copy(
                        vaug_vh[:, hh, tt, 0:64], p_qk[:, hh * 64:(hh + 1) * 64]))(),
                        waits=[(pe, pe_v)] if hh == 0 else [],
                        incs=[(dv, 1)] if hh == 7 else [])
                S.op("tensor", lambda e: None, waits=[(dv, dv.v)])

            for gl in range(4):
                g = half * 4 + gl
                gb = 0
                wsrc = wqkT3[:, g * 256:(g + 1) * 256].rearrange("(c p) f -> p c f", p=128)
                S.op("gpsimd", lambda e: None, waits=[(pe, pe.v)])
                ld_w = S.dma("gpsimd", wqk3[:, 0, :, :], wsrc, sem=gq)

                for dst, c0 in ((qS, 0), (kS, 128)):
                    for ncx in range(4):
                        for k in range(8):
                            S.op("tensor", (lambda ncx=ncx, k=k, c0=c0: lambda e: e.matmul(
                                p_qk[:, :], wqk3[:, 0, k, c0:c0 + 128],
                                xn[:, k, ncx * 512:(ncx + 1) * 512],
                                start=(k == 0), stop=False))(),
                                waits=[(gq, ld_w)] if (k == 0 and ncx == 0 and c0 == 0) else [])
                        S.op("tensor", (lambda g=g, ncx=ncx, c0=c0: lambda e: e.matmul(
                            p_qk[:, :], brow[0:1, g * 256 + c0:g * 256 + c0 + 128],
                            onesr[0:1, :],
                            start=False, stop=True))())
                        pe_qk = emit_drain(S, bufs, att_junk)
                        S.op("vector", (lambda dst=dst, ncx=ncx: lambda e: e.tensor_copy(
                            dst[0:64, ncx * 512:(ncx + 1) * 512], p_qk[0:64, :]))(),
                            waits=[(pe, pe_qk)])
                        S.op("vector", (lambda ncx=ncx: lambda e: e.tensor_copy(
                            bufs["tmp8"][64:128, :], p_qk[64:128, :]))(),
                            incs=[(dv, 1)])
                        S.op("tensor", lambda e: e.matmul(
                            p_sh[0:64, :], id64[64:128, :], bufs["tmp8"][64:128, :],
                            start=True, stop=True),
                            waits=[(dv, dv.v)])
                        pe_sh2 = emit_drain(S, bufs, att_junk)
                        S.op("vector", (lambda dst=dst, ncx=ncx: lambda e: e.tensor_copy(
                            dst[0:64, T + ncx * 512:T + (ncx + 1) * 512], p_sh[0:64, :]))(),
                            waits=[(pe, pe_sh2)], incs=[(dv, 1)])
                        S.op("tensor", lambda e: None, waits=[(dv, dv.v)])

                for sub in range(2):
                    hl = 2 * gl + sub
                    qSh = qS[:, sub * T:(sub + 1) * T]
                    kSh = kS[:, sub * T:(sub + 1) * T]
                    c = hl // 2
                    for qc in range(4):
                        qsl = slice(qc * 512, (qc + 1) * 512)
                        kt_hi = 4 * qc + 4
                        mt = {}
                        for j in range(4):
                            kt = 4 * qc + j
                            cc = kt * 128 - qc * 512
                            mt[kt] = cmask[:, 384 - cc:384 - cc + 512]

                        def X_add(t1v, p_shv, id64v, c=c, qsl=qsl, sub=sub):
                            if sub == 0:
                                S.op("vector", (lambda: lambda e: e.tensor_copy(
                                    Ob[0:64, c, qsl], t1v[:, 0:512]))(),
                                    incs=[(dv, 1)])
                            else:
                                S.op("tensor", (lambda: lambda e: e.matmul(
                                    p_shv[64:128, :], id64v[0:64, :],
                                    t1v[:, 0:512], start=True, stop=True))(),
                                    waits=[(dv, dv.v)])
                                pe_sh = emit_drain(S, bufs, att_junk)
                                S.op("vector", (lambda: lambda e: e.tensor_copy(
                                    Ob[64:128, c, qsl], p_shv[64:128, :]))(),
                                    waits=[(pe, pe_sh)], incs=[(dv, 1)])

                        emit_attn_head(S, bufs, qS=qSh, kS=kSh,
                                       vaug_v=vaug_vh[:, hl, :, :],
                                       X_add=X_add, h=hl, qsl=qsl, kt_lo=0, kt_hi=kt_hi,
                                       pt=pt, p1buf=p1buf, p_scA=p_scA, p_scB=p_scB,
                                       p_sums=p_sums, p_o=p_o, p_rb2=p_rb2, p_sh=p_sh,
                                       osb=osb, rro=rro, t1=t1, id64=id64,
                                       att_junk=att_junk, mask_tiles=mt)

            S.op("gpsimd", lambda e: None, waits=[(pe, pe.v)])
            ld_wo = S.dma("gpsimd", wo[:, :, :],
                          woT[half * 512:(half + 1) * 512, :].rearrange("(c p) f -> p c f", p=128),
                          sem=gq)
            dv_ob = dv.v
            for qc in range(4):
                qsl = slice(qc * 512, (qc + 1) * 512)
                for o in range(8):
                    for k in range(4):
                        S.op("tensor", (lambda o=o, k=k, qsl=qsl: lambda e: e.matmul(
                            p_wo[:, :], wo[:, k, o * 128:(o + 1) * 128], Ob[:, k, qsl],
                            start=(k == 0), stop=(half == 1 or k < 3)))(),
                            waits=[(gq, ld_wo), (dv, dv_ob)] if (qc == 0 and o == 0 and k == 0) else [])
                    if half == 0:
                        S.op("tensor", (lambda o=o: lambda e: e.matmul(
                            p_wo[:, :], borow[0:1, o * 128:(o + 1) * 128], onesr[0:1, :],
                            start=False, stop=True))())
                    pe_wo = emit_drain(S, bufs, att_junk)
                    if half == 0:
                        S.op("vector", (lambda o=o, qsl=qsl: lambda e: e.tensor_add(
                            bufs["tmp8"][:, :], p_wo[:, :], xn[:, o, qsl]))(),
                            waits=[(pe, pe_wo)])
                    else:
                        S.op("vector", (lambda o=o, qsl=qsl: lambda e: e.tensor_copy(
                            bufs["tmp8"][:, :], p_wo[:, :]))(),
                            waits=[(pe, pe_wo)])
                    S.op("vector", (lambda o=o, qsl=qsl: lambda e: e.tensor_add(
                        X[:, o, qsl], X[:, o, qsl], bufs["tmp8"][:, :]))(),
                        incs=[(dv, 1)])
                    S.op("tensor", lambda e: None, waits=[(dv, dv.v)])

        x_dst = x3T[:, t0:t0 + T].rearrange("(c p) t -> p c t", p=128)
        S.dma("sync", x_dst, X[:, :, :], waits=[(dv, dv.v)], sem=st)

    S.op("sync", lambda e: None, waits=[(st, st.v)])
    S.build(nc, ctx)
    ctx.close()
    return nc



def build_stage4(serialize=False, chained=False, ser_set=None):
    nc = bass.Bass("TRN2", target_bir_lowering=False, debug=False)
    x3T = nc.dram_tensor("x3T", [D, TOK], F32, kind="ExternalInput").ap()
    gwT = nc.dram_tensor("gwT", [D, E], F32, kind="ExternalInput").ap()
    gbrow = nc.dram_tensor("gbrow", [1, E], F32, kind="ExternalInput").ap()
    ic64_in = nc.dram_tensor("ic64", [128, E], F32, kind="ExternalInput").ap()
    ut128_in = nc.dram_tensor("ut128", [128, 128], F32, kind="ExternalInput").ap()
    idb_in = nc.dram_tensor("idb", [128, 128], BF16, kind="ExternalInput").ap()
    xfT = nc.dram_tensor("xfT", [D, TOK], BF16, kind="ExternalOutput").ap()
    xfr = nc.dram_tensor("xfr", [TOK, D], BF16, kind="ExternalOutput").ap()
    cnt = nc.dram_tensor("cnt", [1, E], F32, kind="ExternalOutput").ap()
    rinfo = nc.dram_tensor("rinfo", [128, 32, 18], F32, kind="ExternalOutput").ap()

    S = Sched(serialize=serialize, chained=chained, ser_set=ser_set)
    ctx = ExitStack()
    bufs, sb, ps = alloc_common(S, nc, ctx)
    pe, dv, sc = bufs["pe"], bufs["dv"], bufs["sc"]
    ld, gq, st = bufs["ld"], bufs["gq"], bufs["st"]

    X = sb("X", [128, 8, T])
    xf = sb("xf", [128, 8, T], BF16)
    gw = sb("gw", [128, 8, E], BF16)
    gbb = sb("gbb", [128, E])
    ic64 = sb("ic64_sb", [128, E])
    ut128 = sb("ut128_sb", [128, 128])
    idb = sb("idb_sb", [128, 128], BF16)
    bufs["tmp8b"] = sb("tmp8b", [128, 1024], BF16)
    sco = sb("sco", [128, E])
    sp = sb("sp", [128, E])
    oh = sb("oh", [128, E])
    cmat = sb("cmat", [128, E])
    pplus = sb("pplus", [128, E])
    top8 = sb("top8", [128, 8])
    run = sb("run", [1, E])
    info = sb("info", [128, 32, 18])
    junkrow = sb("junkrow", [128, E])
    rowmax = sb("rowmax", [128, 2])
    jnkw = sb("jnkw", [128, 256])

    def _spw(inc=False):
        S.op("vector", lambda e: e.tensor_copy(jnkw[:, 0:128], jnkw[:, 128:256]),
             incs=[(dv, 1)] if inc else [])

    P = [ps(f"P{i}", [128, 512]) for i in range(7)]
    bufs["p_m"], bufs["p_sq"], bufs["p_rb"], bufs["p_mrb"] = P[0], P[1], P[2], P[3]
    bufs["ln_junk"] = P[5][0:1, 0:256]
    p_g = P[4]
    p_off = P[5]
    p_tot = P[6]
    att_junk = P[0][0:1, 0:256]
    S.ser_junk = P[6][0:1, 256:512]
    S.ser_ones1 = bufs["ones1"][0:1, 0:1]
    S.ser_onest = bufs["onest"][0:1, 0:256]
    p_trb = ctx.enter_context(nc.psum_tensor("p_trb", [128, 1024], BF16))

    S.op("vector", lambda e: e.memset(run[:, :], 0.0), incs=[(dv, 1)])
    init_dv0 = dv.v

    S.dma("gpsimd", gw[:, :, :], gwT.rearrange("(c p) f -> p c f", p=128), sem=gq)
    S.dma("sync", ic64[:, :], ic64_in[:, :], sem=ld)
    S.dma("sync", ut128[:, :], ut128_in[:, :], sem=ld)
    S.dma("sync", idb[:, :], idb_in[:, :], sem=ld)
    ld_const = ld.v
    gq_const = gq.v
    gbr = sb("gbr", [1, E])
    S.dma("sync", gbr[:, :], gbrow[:, :], sem=ld)
    ld_gb = ld.v
    S.op("tensor", lambda e: e.matmul(p_g[:, 0:E], bufs["ones128"][:, :], gbr[:, :],
                                      start=True, stop=True),
         waits=[(ld, ld_gb), (dv, init_dv0)])
    pe_gb = emit_drain(S, bufs, att_junk)
    S.op("vector", lambda e: e.tensor_copy(gbb[:, :], p_g[:, 0:E]),
         waits=[(pe, pe_gb)], incs=[(dv, 1)])
    init_dv = dv.v

    for s in range(SEQ_PER_CORE):
        t0 = s * T
        x_src = x3T[:, t0:t0 + T].rearrange("(c p) t -> p c t", p=128)
        S.dma("sync", X[:, :, :], x_src, waits=[(st, st.v)], sem=ld)
        ld_x = ld.v

        dv_xf = emit_layernorm(S, nc, bufs, X, xf, T,
                               [(ld, ld_x)] + ([(dv, init_dv), (ld, ld_const)] if s == 0 else []))
        S.dma("sync", xfT[:, t0:t0 + T].rearrange("(c p) t -> p c t", p=128),
              xf[:, :, :], waits=[(dv, dv_xf)], sem=st)

        for tt in range(16):
            gt = s * 16 + tt
            for k in range(8):
                S.op("tensor", (lambda k=k, tt=tt: lambda e: e.transpose(
                    p_trb[:, k * 128:(k + 1) * 128], xf[:, k, tt * 128:(tt + 1) * 128],
                    idb[:, :]))(),
                    waits=[(dv, dv_xf)] if (k == 0) else [])
            pe_tr = emit_drain(S, bufs, att_junk)
            S.op("vector", lambda e: e.tensor_copy(bufs["tmp8b"][:, :], p_trb[:, :]),
                 waits=[(pe, pe_tr), (st, st.v)], incs=[(dv, 1)])
            S.dma("sync", xfr[t0 + tt * 128:t0 + (tt + 1) * 128, :],
                  bufs["tmp8b"][:, :], waits=[(dv, dv.v)], sem=st)
            dv_tr = dv.v

            for k in range(8):
                S.op("tensor", (lambda k=k, tt=tt: lambda e: e.matmul(
                    p_g[:, 0:E], xf[:, k, tt * 128:(tt + 1) * 128], gw[:, k, :],
                    start=(k == 0), stop=(k == 7)))(),
                    waits=([(dv, dv.v)] + ([(gq, gq_const)] if (s == 0 and tt == 0) else [])) if k == 0 else [])
            pe_g = emit_drain(S, bufs, att_junk)
            S.op("vector", lambda e: e.tensor_copy(junkrow[:, 0:8], p_g[:, 0:8]),
                 waits=[(pe, pe_g)])
            S.op("vector", lambda e: e.tensor_reduce(rowmax[:, 0:1], p_g[:, 0:E],
                                                     AX.X, ALU.max))
            _spw(); _spw()
            S.op("vector", lambda e: e.tensor_single_scalar(rowmax[:, 1:2], rowmax[:, 0:1],
                                                            -1.0, ALU.mult),
                 incs=[(dv, 1)])
            _spw(inc=True)
            _spw(inc=True)
            S.op("scalar", lambda e: e.activation(out=sco[:, :], in_=p_g[:, 0:E],
                                                  func=AF.Exp, bias=rowmax[:, 1:2],
                                                  accum_out=rowmax[:, 0:1]),
                 waits=[(dv, dv.v)])
            S.op("scalar", lambda e: e.activation(out=jnkw[:, 0:64], in_=jnkw[:, 64:128],
                                                  func=AF.Identity))
            S.op("scalar", lambda e: e.activation(out=jnkw[:, 0:64], in_=jnkw[:, 64:128],
                                                  func=AF.Identity), incs=[(sc, 1)])
            S.op("vector", lambda e: e.reciprocal(rowmax[:, 1:2], rowmax[:, 0:1]),
                 waits=[(sc, sc.v)])
            _spw(); _spw()
            S.op("vector", lambda e: e.tensor_scalar(sco[:, :], sco[:, :],
                                                     rowmax[:, 1:2], None, ALU.mult))
            _spw(); _spw()
            S.op("vector", lambda e: e.tensor_add(sp[:, :], sco[:, :], gbb[:, :]),
                 incs=[(dv, 1)])
            _spw(); _spw()
            S.op("vector", lambda e: e.max(out=top8[:, :], in_=sp[:, :]))
            _spw(); _spw(); _spw()
            for k6 in range(TOPK):
                S.op("vector", (lambda k6=k6: lambda e: e.tensor_scalar(
                    oh[:, :], sp[:, :], top8[:, k6:k6 + 1], None, ALU.is_equal))())
                _spw(); _spw()
                S.op("vector", (lambda k6=k6, gt=gt: lambda e: e.scalar_tensor_tensor(
                    junkrow[:, :], sco[:, :], 1.0, oh[:, :], ALU.mult, ALU.mult,
                    accum_out=info[:, gt, 6 + k6:7 + k6]))())
                S.op("vector", (lambda k6=k6, gt=gt: lambda e: e.scalar_tensor_tensor(
                    junkrow[:, :], ic64[:, :], 1.0, oh[:, :], ALU.mult, ALU.mult,
                    accum_out=info[:, gt, k6:k6 + 1]))())
                if k6 == 0:
                    S.op("vector", lambda e: e.tensor_copy(cmat[:, :], oh[:, :]))
                else:
                    S.op("vector", lambda e: e.tensor_add(cmat[:, :], cmat[:, :], oh[:, :]),
                         incs=[(dv, 1)] if k6 == TOPK - 1 else [])
            dv_c = dv.v
            S.op("tensor", lambda e: e.matmul(p_g[:, 0:E], ut128[:, :], cmat[:, :],
                                              start=True, stop=True),
                 waits=[(dv, dv_c)])
            S.op("tensor", lambda e: e.matmul(p_tot[0:1, 0:E], bufs["ones1"][:, :],
                                              cmat[:, :], start=True, stop=True))
            S.op("tensor", lambda e: e.matmul(p_off[:, 0:E], bufs["ones128"][:, :],
                                              run[:, :], start=True, stop=True))
            pe_pr = emit_drain(S, bufs, att_junk)
            S.op("vector", lambda e: e.tensor_copy(pplus[:, :], p_g[:, 0:E]),
                 waits=[(pe, pe_pr)])
            _spw(); _spw()
            S.op("vector", lambda e: e.tensor_add(pplus[:, :], pplus[:, :], p_off[:, 0:E]))
            S.op("vector", lambda e: e.tensor_add(run[:, :], run[:, :], p_tot[0:1, 0:E]),
                 incs=[(dv, 1)])
            _spw(); _spw()
            for k6 in range(TOPK):
                S.op("vector", (lambda k6=k6: lambda e: e.tensor_scalar(
                    oh[:, :], sp[:, :], top8[:, k6:k6 + 1], None, ALU.is_equal))())
                _spw(); _spw()
                S.op("vector", (lambda k6=k6, gt=gt: lambda e: e.scalar_tensor_tensor(
                    junkrow[:, :], pplus[:, :], 1.0, oh[:, :], ALU.mult, ALU.mult,
                    accum_out=info[:, gt, 12 + k6:13 + k6]))(),
                    incs=[(dv, 1)] if k6 == TOPK - 1 else [])
            S.op("tensor", lambda e: None, waits=[(dv, dv.v)])

    _spw(); _spw(); _spw(inc=True)
    S.dma("sync", cnt[:, :], run[:, :], waits=[(bufs["dv"], bufs["dv"].v)], sem=st)
    S.dma("sync", rinfo[:, :, :], info[:, :, :], sem=st)
    S.op("sync", lambda e: None, waits=[(st, st.v)])
    S.build(nc, ctx)
    ctx.close()
    return nc



def build_stage4b(serialize=False, chained=False):
    nc = bass.Bass("TRN2", target_bir_lowering=False, debug=False)
    cnts = nc.dram_tensor("cnts", [8, E], F32, kind="ExternalInput").ap()
    rinfo = nc.dram_tensor("rinfo", [128, 32, 18], F32, kind="ExternalInput").ap()
    csel = nc.dram_tensor("csel", [8, 1], F32, kind="ExternalInput").ap()
    i512_in = nc.dram_tensor("i512", [128, 512], F32, kind="ExternalInput").ap()
    tbase_in = nc.dram_tensor("tbase", [128, 1], F32, kind="ExternalInput").ap()
    ic64_in = nc.dram_tensor("ic64", [128, E], F32, kind="ExternalInput").ap()
    mpart = nc.dram_tensor("mpart", [E, 512], F32, kind="ExternalOutput").ap()
    sinfo = nc.dram_tensor("sinfo", [128, 32, 12], F32, kind="ExternalOutput").ap()

    S = Sched(serialize=serialize, chained=chained)
    ctx = ExitStack()
    bufs, sb, ps = alloc_common(S, nc, ctx)
    pe, dv, sc = bufs["pe"], bufs["dv"], bufs["sc"]
    ld, gq, st = bufs["ld"], bufs["gq"], bufs["st"]

    cntb = sb("cntb", [8, E])
    cselb = sb("cselb", [8, 1])
    i512 = sb("i512_sb", [128, 512])
    tbase = sb("tbase_sb", [128, 1])
    ic64 = sb("ic64_sb", [128, E])
    info = sb("info", [128, 32, 18])
    sout = sb("sout", [128, 32, 12])
    offb = sb("offb", [128, E])
    oh = sb("oh", [128, E])
    ohp = sb("ohp", [128, 2, 512])
    oht = sb("oht", [128, 2, E])
    tg = sb("tg", [128, 1])
    posg = sb("posg", [128, 1])
    lt = sb("lt", [128, 1])
    junkrow = sb("junkrow", [128, E])
    msb = sb("msb", [64, 512])
    jnkw = sb("jnkw", [128, 256])

    def _spw(inc=False):
        S.op("vector", lambda e: e.tensor_copy(jnkw[:, 0:128], jnkw[:, 128:256]),
             incs=[(dv, 1)] if inc else [])

    P = [ps(f"P{i}", [128, 512]) for i in range(4)]
    p_off = P[0]
    p_m = P[1]
    att_junk = P[2][0:1, 0:256]
    S.ser_junk = P[3][0:1, 0:256]
    S.ser_ones1 = bufs["ones1"][0:1, 0:1]
    S.ser_onest = bufs["onest"][0:1, 0:256]

    S.dma("sync", cntb[:, :], cnts[:, :], sem=ld)
    S.dma("sync", cselb[:, :], csel[:, :], sem=ld)
    S.dma("sync", i512[:, :], i512_in[:, :], sem=ld)
    S.dma("sync", tbase[:, :], tbase_in[:, :], sem=ld)
    S.dma("sync", ic64[:, :], ic64_in[:, :], sem=ld)
    S.dma("sync", info[:, :, :], rinfo[:, :, :], sem=ld)
    ld_c = ld.v

    S.op("tensor", lambda e: e.matmul(p_off[0:1, 0:E], cselb[:, :], cntb[:, :],
                                      start=True, stop=True),
         waits=[(ld, ld_c)])
    pe1 = emit_drain(S, bufs, att_junk)
    S.op("vector", lambda e: e.tensor_copy(bufs["tmp8"][0:1, 0:E], p_off[0:1, 0:E]),
         waits=[(pe, pe1)], incs=[(dv, 1)])
    S.op("tensor", lambda e: e.matmul(p_off[:, 0:E], bufs["ones128"][:, :],
                                      bufs["tmp8"][0:1, 0:E], start=True, stop=True),
         waits=[(dv, dv.v)])
    pe2 = emit_drain(S, bufs, att_junk)
    S.op("vector", lambda e: e.tensor_copy(offb[:, :], p_off[:, 0:E]),
         waits=[(pe, pe2)], incs=[(dv, 1)])
    dv_off = dv.v

    first = True
    pecnt = []
    j = 0
    for gt in range(32):
        S.op("vector", (lambda gt=gt: lambda e: e.tensor_single_scalar(
            tg[:, :], tbase[:, :], float(gt * 128), ALU.add))(),
            waits=[(dv, dv_off)] if first else [])
        _spw(); _spw()
        for k6 in range(TOPK):
            jb = j % 2
            S.op("vector", (lambda gt=gt, k6=k6: lambda e: e.tensor_scalar(
                oh[:, :], ic64[:, :], info[:, gt, k6:k6 + 1], None, ALU.is_equal))())
            _spw(); _spw()
            S.op("vector", (lambda: lambda e: e.scalar_tensor_tensor(
                junkrow[:, :], offb[:, :], 1.0, oh[:, :], ALU.mult, ALU.mult,
                accum_out=posg[:, :]))())
            _spw(); _spw()
            S.op("vector", (lambda gt=gt, k6=k6: lambda e: e.tensor_scalar(
                posg[:, :], posg[:, :], info[:, gt, 12 + k6:13 + k6], None, ALU.add))())
            _spw(); _spw()
            S.op("vector", lambda e: e.tensor_single_scalar(lt[:, :], posg[:, :], 512.0, ALU.is_lt))
            _spw(); _spw()
            S.op("vector", (lambda gt=gt, k6=k6: lambda e: e.tensor_scalar(
                sout[:, gt, k6:k6 + 1], lt[:, :], info[:, gt, 6 + k6:7 + k6], None, ALU.mult))())
            S.op("vector", (lambda gt=gt, k6=k6: lambda e: e.tensor_single_scalar(
                bufs["tmp8"][:, 0:1], info[:, gt, k6:k6 + 1], 512.0, ALU.mult))())
            _spw(); _spw()
            S.op("vector", lambda e: e.tensor_add(bufs["tmp8"][:, 0:1], bufs["tmp8"][:, 0:1], posg[:, :]))
            S.op("vector", lambda e: e.tensor_single_scalar(lt[:, :], lt[:, :], -1.0, ALU.add))
            _spw(); _spw()
            S.op("vector", lambda e: e.tensor_single_scalar(lt[:, :], lt[:, :], -1e7, ALU.mult))
            _spw(); _spw()
            S.op("vector", (lambda gt=gt, k6=k6: lambda e: e.tensor_add(
                sout[:, gt, 6 + k6:7 + k6], bufs["tmp8"][:, 0:1], lt[:, :]))())
            war = [(pe, pecnt[j - 2])] if j >= 2 else []
            S.op("vector", (lambda jb=jb: lambda e: e.tensor_scalar(
                oht[:, jb, :], oh[:, :], tg[:, :], None, ALU.mult))(), waits=war)
            S.op("vector", (lambda jb=jb: lambda e: e.tensor_scalar(
                ohp[:, jb, :], i512[:, :], posg[:, :], None, ALU.is_equal))(),
                incs=[(dv, 1)])
            S.op("tensor", (lambda gt=gt, k6=k6, jb=jb: lambda e: e.matmul(
                p_m[0:64, :], oht[:, jb, :], ohp[:, jb, :],
                start=(gt == 0 and k6 == 0), stop=(gt == 31 and k6 == TOPK - 1)))(),
                waits=[(dv, dv.v)], incs=[(pe, 1)])
            pecnt.append(pe.v)
            j += 1
        first = False
    pe_m = emit_drain(S, bufs, att_junk)
    S.op("vector", lambda e: e.tensor_copy(msb[:, :], p_m[0:64, :]),
         waits=[(pe, pe_m)], incs=[(dv, 1)])
    _spw(); _spw(inc=True)
    S.dma("sync", mpart[:, :], msb[:, :], waits=[(dv, dv.v)], sem=st)
    S.dma("sync", sinfo[:, :, :], sout[:, :, :], waits=[(dv, dv.v)], sem=st)
    S.op("sync", lambda e: None, waits=[(st, st.v)])
    S.build(nc, ctx)
    ctx.close()
    return nc


I_PAD = 768


def build_stage5(serialize=False, chained=False):
    nc = bass.Bass("TRN2", target_bir_lowering=False, debug=False)
    mloc = nc.dram_tensor("mloc", [8, 512, 8], F32, kind="ExternalInput").ap()
    xf_full = nc.dram_tensor("xf_full", [B * T, D], BF16, kind="ExternalInput").ap()
    w1T = nc.dram_tensor("w1T", [8, D, I_PAD], F32, kind="ExternalInput").ap()
    w3T = nc.dram_tensor("w3T", [8, D, I_PAD], F32, kind="ExternalInput").ap()
    w2T = nc.dram_tensor("w2T", [8, I_PAD, D], F32, kind="ExternalInput").ap()
    idb_in = nc.dram_tensor("idb", [128, 128], BF16, kind="ExternalInput").ap()
    yb = nc.dram_tensor("yb", [8 * 512, D], BF16, kind="ExternalOutput").ap()

    S = Sched(serialize=serialize, chained=chained)
    ctx = ExitStack()
    bufs, sb, ps = alloc_common(S, nc, ctx)
    pe, dv, sc = bufs["pe"], bufs["dv"], bufs["sc"]
    ld, gq, st = bufs["ld"], bufs["gq"], bufs["st"]

    NI = I_PAD // 128
    marr = sb("marr", [128, 4, 8])
    midx = sb("midx", [128, 4], I32)
    mfl = sb("mfl", [128, 4])
    xb = sb("xb", [128, 4, D], BF16)
    xbT = sb("xbT", [128, 8, 512], BF16)
    w1b = sb("w1b", [128, 2, 8, I_PAD], BF16)
    w3b = sb("w3b", [128, 2, 8, I_PAD], BF16)
    w2b = sb("w2b", [128, 2, NI, D], BF16)
    us = sb("us", [128, NI, 512], BF16)
    vs = sb("vs", [128, NI, 512], BF16)
    idb = sb("idb_sb", [128, 128], BF16)
    ybt = sb("ybt", [128, 2, 512], BF16)
    jnk5 = sb("jnk5", [128, 256])

    P = [ps(f"P{i}", [128, 512]) for i in range(6)]
    p_u = P[0]
    p_v = P[1]
    p_y = P[2]
    att_junk = P[3][0:1, 0:256]
    p_trb = ctx.enter_context(nc.psum_tensor("p_trb", [128, 1024], BF16))

    S.dma("sync", idb[:, :], idb_in[:, :], sem=ld)
    ld_id = ld.v

    prev_gather = 0
    for e_i in range(8):
        eb = e_i % 2
        if e_i >= 2:
            S.op("gpsimd", lambda e: None, waits=[(pe, pe.v)])
        l1 = S.dma("gpsimd", w1b[:, eb, :, :],
                   w1T[e_i, :, :].rearrange("(c p) f -> p c f", p=128), sem=gq)
        l3 = S.dma("gpsimd", w3b[:, eb, :, :],
                   w3T[e_i, :, :].rearrange("(c p) f -> p c f", p=128), sem=gq)
        l2 = S.dma("gpsimd", w2b[:, eb, :, :],
                   w2T[e_i, :, :].rearrange("(c p) f -> p c f", p=128), sem=gq)

        S.dma("sync", marr[:, :, :],
              mloc[e_i, :, :].rearrange("(c p) k -> p c k", p=128),
              waits=[(dv, dv.v)], sem=ld)
        def _sp():
            S.op("vector", lambda e: e.tensor_copy(jnk5[:, 0:128], jnk5[:, 128:256]))
        S.op("vector", lambda e: e.tensor_reduce(mfl[:, :], marr[:, :, :], AX.X, ALU.add),
             waits=[(ld, ld.v)])
        _sp(); _sp()
        S.op("vector", lambda e: e.tensor_single_scalar(mfl[:, :], mfl[:, :], -1.0, ALU.add))
        _sp(); _sp()
        S.op("vector", lambda e: e.tensor_scalar_max(mfl[:, :], mfl[:, :], 0.0))
        _sp(); _sp()
        S.op("vector", lambda e: e.tensor_copy(midx[:, :], mfl[:, :]),
             waits=[(gq, prev_gather)] if prev_gather else [])
        _sp(); _sp()
        S.op("vector", lambda e: e.tensor_copy(jnk5[:, 0:128], jnk5[:, 128:256]),
             incs=[(dv, 1)])
        dv_idx = dv.v
        for c in range(4):
            S.op("gpsimd", (lambda c=c: lambda e: e.indirect_dma_start(
                out=xb[:, c, :], out_offset=None,
                in_=xf_full[:, :], in_offset=IndirectOffsetOnAxis(midx[:, c:c + 1], 0),
                bounds_check=B * T - 1, oob_is_err=False))(),
                waits=[(dv, dv_idx)] if c == 0 else [], incs=[(gq, 16)])
        gq_gather = gq.v
        prev_gather = gq_gather
        for c in range(4):
            for k in range(8):
                S.op("tensor", (lambda c=c, k=k: lambda e: e.transpose(
                    p_trb[:, k * 128:(k + 1) * 128], xb[:, c, k * 128:(k + 1) * 128],
                    idb[:, :]))(),
                    waits=[(gq, gq_gather), (ld, ld_id)] if (c == 0 and k == 0) else [])
            pe_tr = emit_drain(S, bufs, att_junk)
            S.op("vector", (lambda c=c: lambda e: e.tensor_copy(
                xbT[:, :, c * 128:(c + 1) * 128],
                p_trb[:, :].rearrange("p (a b) -> p a b", a=8)))(),
                waits=[(pe, pe_tr)], incs=[(dv, 1)])
            S.op("tensor", lambda e: None, waits=[(dv, dv.v)])
        dv_xbT = dv.v

        for i in range(NI):
            for k in range(8):
                S.op("tensor", (lambda i=i, k=k, eb=eb: lambda e: e.matmul(
                    p_u[:, :], w1b[:, eb, k, i * 128:(i + 1) * 128], xbT[:, k, :],
                    start=(k == 0), stop=(k == 7)))(),
                    waits=[(dv, dv_xbT), (gq, l3)] if (i == 0 and k == 0) else [])
                S.op("tensor", (lambda i=i, k=k, eb=eb: lambda e: e.matmul(
                    p_v[:, :], w3b[:, eb, k, i * 128:(i + 1) * 128], xbT[:, k, :],
                    start=(k == 0), stop=(k == 7)))())
            pe_uv = emit_drain(S, bufs, att_junk)
            S.op("scalar", (lambda i=i: lambda e: e.activation(
                out=us[:, i, :], in_=p_u[:, :], func=AF.Silu))(),
                waits=[(pe, pe_uv)], incs=[(sc, 1)])
            S.op("vector", (lambda i=i: lambda e: e.tensor_copy(vs[:, i, :], p_v[:, :]))(),
                 waits=[(pe, pe_uv)])
            S.op("vector", (lambda i=i: lambda e: e.tensor_mul(
                us[:, i, :], us[:, i, :], vs[:, i, :]))(),
                waits=[(sc, sc.v)], incs=[(dv, 1)])
            S.op("tensor", lambda e: None, waits=[(dv, dv.v)])
        dv_h = dv.v

        for c in range(4):
            for dh in range(2):
                for i in range(NI):
                    S.op("tensor", (lambda c=c, dh=dh, i=i, eb=eb: lambda e: e.matmul(
                        p_y[:, :], us[:, i, c * 128:(c + 1) * 128],
                        w2b[:, eb, i, dh * 512:(dh + 1) * 512],
                        start=(i == 0), stop=(i == NI - 1)))(),
                        waits=[(dv, dv_h), (gq, l2)] if (c == 0 and dh == 0 and i == 0) else [])
                pe_y = emit_drain(S, bufs, att_junk)
                S.op("vector", (lambda dh=dh: lambda e: e.tensor_copy(
                    ybt[:, dh, :], p_y[:, :]))(),
                    waits=[(pe, pe_y), (st, st.v)], incs=[(dv, 1)])
                S.op("tensor", lambda e: None, waits=[(dv, dv.v)])
            S.dma("sync", yb[e_i * 512 + c * 128:e_i * 512 + (c + 1) * 128, :],
                  ybt[:, :, :].rearrange("p a b -> p (a b)"), waits=[(dv, dv.v)], sem=st)

    S.op("sync", lambda e: None, waits=[(st, st.v)])
    S.build(nc, ctx)
    ctx.close()
    return nc



def build_stage6(serialize=False, chained=False):
    nc = bass.Bass("TRN2", target_bir_lowering=False, debug=False)
    x3T = nc.dram_tensor("x3T", [D, TOK], F32, kind="ExternalInput").ap()
    xfT_in = nc.dram_tensor("xfT", [D, TOK], BF16, kind="ExternalInput").ap()
    yb_full = nc.dram_tensor("yb_full", [B * T, D], BF16, kind="ExternalInput").ap()
    sinfo_in = nc.dram_tensor("sinfo", [128, 32, 12], F32, kind="ExternalInput").ap()
    ws1T = nc.dram_tensor("ws1T", [D, SI], F32, kind="ExternalInput").ap()
    ws3T = nc.dram_tensor("ws3T", [D, SI], F32, kind="ExternalInput").ap()
    ws2T = nc.dram_tensor("ws2T", [SI, D], F32, kind="ExternalInput").ap()
    idf_in = nc.dram_tensor("idf", [128, 128], F32, kind="ExternalInput").ap()
    outr = nc.dram_tensor("outr", [TOK, D], F32, kind="ExternalOutput").ap()

    S = Sched(serialize=serialize, chained=chained)
    ctx = ExitStack()
    bufs, sb, ps = alloc_common(S, nc, ctx)
    pe, dv, sc = bufs["pe"], bufs["dv"], bufs["sc"]
    ld, gq, st = bufs["ld"], bufs["gq"], bufs["st"]

    NS = SI // 128
    X = sb("X", [128, 8, T])
    xf = sb("xf", [128, 8, T], BF16)
    w1b = sb("w1b", [128, 8, SI], BF16)
    w3b = sb("w3b", [128, 8, SI], BF16)
    w2b = sb("w2b", [128, NS, D], BF16)
    us = sb("us", [128, NS, 512], BF16)
    sinfo = sb("sinfo_sb", [128, 32, 12])
    sidx = sb("sidx", [128, 1], I32)
    gbuf = sb("gbuf", [128, 1, D], BF16)
    acc = sb("acc", [128, D])
    idf = sb("idf_sb", [128, 128])
    jnk = sb("jnk", [128, 256])

    P = [ps(f"P{i}", [128, 512]) for i in range(6)]
    p_u = P[0]
    p_v = P[1]
    p_z = P[2]
    p_x = P[3]
    att_junk = P[4][0:1, 0:256]

    S.dma("sync", sinfo[:, :, :], sinfo_in[:, :, :], sem=ld)
    S.dma("sync", idf[:, :], idf_in[:, :], sem=ld)
    l1 = S.dma("gpsimd", w1b[:, :, :], ws1T.rearrange("(c p) f -> p c f", p=128), sem=gq)
    l3 = S.dma("gpsimd", w3b[:, :, :], ws3T.rearrange("(c p) f -> p c f", p=128), sem=gq)
    l2 = S.dma("gpsimd", w2b[:, :, :], ws2T.rearrange("(c p) f -> p c f", p=128), sem=gq)
    ld_c = ld.v
    S.op("vector", lambda e: e.memset(gbuf[:, 0, :], 0.0), incs=[(dv, 1)])
    init_dv = dv.v
    _bh = {}

    def _breg(e):
        if "r" not in _bh:
            _bh["r"] = e.alloc_register("bnd")
            e.reg_mov(_bh["r"], B * T - 1)
        return _bh["r"]


    for s in range(SEQ_PER_CORE):
        t0 = s * T
        S.dma("sync", X[:, :, :], x3T[:, t0:t0 + T].rearrange("(c p) t -> p c t", p=128),
              waits=[(st, st.v)], sem=ld)
        S.dma("sync", xf[:, :, :], xfT_in[:, t0:t0 + T].rearrange("(c p) t -> p c t", p=128),
              sem=ld)
        ld_x = ld.v

        for tc in range(4):
            tsl = slice(tc * 512, (tc + 1) * 512)
            for i in range(NS):
                for k in range(8):
                    S.op("tensor", (lambda i=i, k=k, tsl=tsl: lambda e: e.matmul(
                        p_u[:, :], w1b[:, k, i * 128:(i + 1) * 128], xf[:, k, tsl],
                        start=(k == 0), stop=(k == 7)))(),
                        waits=[(ld, ld_x), (gq, l3), (dv, init_dv)] if (i == 0 and k == 0 and tc == 0 and s == 0) else ([(ld, ld_x)] if (i == 0 and k == 0) else []))
                    S.op("tensor", (lambda i=i, k=k, tsl=tsl: lambda e: e.matmul(
                        p_v[:, :], w3b[:, k, i * 128:(i + 1) * 128], xf[:, k, tsl],
                        start=(k == 0), stop=(k == 7)))())
                pe_uv = emit_drain(S, bufs, att_junk)
                S.op("scalar", (lambda i=i: lambda e: e.activation(
                    out=us[:, i, :], in_=p_u[:, :], func=AF.Silu))(),
                    waits=[(pe, pe_uv)], incs=[(sc, 1)])
                S.op("vector", (lambda i=i: lambda e: e.tensor_mul(
                    us[:, i, :], us[:, i, :], p_v[:, :]))(),
                    waits=[(sc, sc.v)], incs=[(dv, 1)])
                S.op("tensor", lambda e: None, waits=[(dv, dv.v)])
            dv_h = dv.v

            for ttl in range(4):
                gt = s * 16 + tc * 4 + ttl
                for dh in range(2):
                    for i in range(NS):
                        S.op("tensor", (lambda ttl=ttl, dh=dh, i=i: lambda e: e.matmul(
                            p_z[:, :], us[:, i, ttl * 128:(ttl + 1) * 128],
                            w2b[:, i, dh * 512:(dh + 1) * 512],
                            start=(i == 0), stop=(i == NS - 1)))(),
                            waits=[(dv, dv_h), (gq, l2)] if (ttl == 0 and dh == 0 and i == 0) else [])
                    pe_z = emit_drain(S, bufs, att_junk)
                    S.op("vector", (lambda dh=dh: lambda e: e.tensor_copy(
                        acc[:, dh * 512:(dh + 1) * 512], p_z[:, :]))(),
                        waits=[(pe, pe_z)], incs=[(dv, 1)])
                    S.op("tensor", lambda e: None, waits=[(dv, dv.v)])
                for k in range(8):
                    S.op("tensor", (lambda k=k, tc=tc, ttl=ttl: lambda e: e.transpose(
                        p_x[:, 0:128], X[:, k, tc * 512 + ttl * 128:tc * 512 + (ttl + 1) * 128],
                        idf[:, :]))(),
                        waits=[(ld, ld_c)] if k == 0 else [])
                    pe_x = emit_drain(S, bufs, att_junk)
                    S.op("vector", (lambda k=k: lambda e: e.tensor_add(
                        acc[:, k * 128:(k + 1) * 128], acc[:, k * 128:(k + 1) * 128],
                        p_x[:, 0:128]))(),
                        waits=[(pe, pe_x)], incs=[(dv, 1)])
                    S.op("tensor", lambda e: None, waits=[(dv, dv.v)])
                for k6 in range(TOPK):
                    S.op("vector", (lambda gt=gt, k6=k6: lambda e: e.tensor_copy(
                        sidx[:, :], sinfo[:, gt, 6 + k6:7 + k6]))())
                    S.op("vector", lambda e: e.tensor_copy(jnk[:, 0:128], jnk[:, 128:256]))
                    S.op("vector", lambda e: e.tensor_copy(jnk[:, 0:128], jnk[:, 128:256]))
                    S.op("vector", lambda e: e.tensor_copy(jnk[:, 0:128], jnk[:, 128:256]),
                         incs=[(dv, 1)])
                    S.op("gpsimd", (lambda: lambda e: e.indirect_dma_start(
                        out=gbuf[:, 0, :], out_offset=None,
                        in_=yb_full[:, :], in_offset=IndirectOffsetOnAxis(sidx[:, :], 0),
                        bounds_check=_breg(e), oob_is_err=False))(),
                        waits=[(dv, dv.v)], incs=[(gq, 16)])
                    S.op("vector", (lambda gt=gt, k6=k6: lambda e: e.scalar_tensor_tensor(
                        acc[:, :], gbuf[:, 0, :], sinfo[:, gt, k6:k6 + 1], acc[:, :],
                        ALU.mult, ALU.add))(),
                        waits=[(gq, gq.v)], incs=[(dv, 1)])
                S.dma("sync", outr[t0 + (tc * 4 + ttl) * 128:t0 + (tc * 4 + ttl + 1) * 128, :],
                      acc[:, :], waits=[(dv, dv.v)], sem=st)
                S.op("vector", lambda e: None, waits=[(st, st.v)])

    S.op("sync", lambda e: None, waits=[(st, st.v)])
    S.build(nc, ctx)
    ctx.close()
    return nc



def host_consts():
    idhi = np.concatenate([np.eye(64, dtype=np.float32)] * 2, 0)
    idf = np.eye(128, dtype=np.float32)
    idb = np.eye(128).astype(BF)
    ic64 = np.broadcast_to(np.arange(E, dtype=np.float32), (128, E)).copy()
    i512 = np.broadcast_to(np.arange(512, dtype=np.float32), (128, 512)).copy()
    ut = (np.arange(128)[:, None] < np.arange(128)[None, :]).astype(np.float32)
    kk = np.arange(128)[:, None]
    m = np.arange(896)[None, :]
    cmask = np.where(m - 384 >= kk, 0.0, -1e9).astype(np.float32)
    return dict(idhi=idhi, idf=idf, idb=idb, ic64=ic64, i512=i512, ut128=ut, cmask=cmask)


def prep_stage3_weights(Wq, Wk, Wv, Wo, nw, bq, bk, bv, bo):
    qk, bqk = [], []
    for g in range(H // 2):
        for h in (2 * g, 2 * g + 1):
            qk.append(Wq[h * DH:(h + 1) * DH, :])
        for h in (2 * g, 2 * g + 1):
            qk.append(Wk[h * DH:(h + 1) * DH, :])
        bqk.extend([bq[2 * g * DH:(2 * g + 1) * DH], bq[(2 * g + 1) * DH:(2 * g + 2) * DH],
                    bk[2 * g * DH:(2 * g + 1) * DH], bk[(2 * g + 1) * DH:(2 * g + 2) * DH]])
    return dict(
        wqkT3=np.ascontiguousarray(np.concatenate(qk, 0).T),
        wvT3=np.ascontiguousarray(Wv.T),
        woT=np.ascontiguousarray(Wo.T),
        nw=np.ascontiguousarray(nw.reshape(8, 128).T),
        bqk=np.ascontiguousarray(np.concatenate(bqk)[None, :]),
        bv=np.ascontiguousarray(bv[None, :]),
        bo=np.ascontiguousarray(bo[None, :]),
    )


_BUILT = {}


def _get(name, builder):
    if name not in _BUILT:
        _BUILT[name] = builder()
    return _BUILT[name]


def kernel(L0_attn_z, L0_attn_W_qkv, L0_ffn_weight_q, L0_ffn_weight_scale,
           L0_ffn_bias, L1_attn_norm_w, L1_attn_Wq, L1_attn_bq, L1_attn_Wk,
           L1_attn_bk, L1_attn_Wv, L1_attn_bv, L1_attn_Wo, L1_attn_bo,
           L1_ffn_gate_w, L1_ffn_gate_b, L1_ffn_w1, L1_ffn_w2, L1_ffn_w3,
           L1_ffn_ws1, L1_ffn_ws2, L1_ffn_ws3):
    f32 = lambda a: np.asarray(a, dtype=np.float32)
    C = host_consts()
    cores = list(range(N_CORES))

    shards = shard_tokens_fm(f32(L0_attn_z))
    wqkT, wvT = prep_wqkv(f32(L0_attn_W_qkv))
    nc1 = _get("s1", build_stage1)
    r1 = run_bass_kernel_spmd(nc1, [
        {"x0T": shards[c], "wqkT": wqkT, "wvT": wvT, "idhi": C["idhi"]}
        for c in cores], core_ids=cores)
    x1 = [r1.results[c]["x1T"] for c in cores]

    wqTb = np.ascontiguousarray(np.asarray(L0_ffn_weight_q).T.astype(BF))
    wsc = np.ascontiguousarray(f32(L0_ffn_weight_scale).reshape(8, 128).T)
    fbc = np.ascontiguousarray(f32(L0_ffn_bias).reshape(8, 128).T)
    nc2 = _get("s2", build_stage2)
    r2 = run_bass_kernel_spmd(nc2, [
        {"x1T": x1[c], "wqTb": wqTb, "wscale": wsc, "fbias": fbc, "idf": C["idf"]}
        for c in cores], core_ids=cores)
    x2 = [r2.results[c]["x2T"] for c in cores]

    w3d = prep_stage3_weights(f32(L1_attn_Wq), f32(L1_attn_Wk), f32(L1_attn_Wv),
                              f32(L1_attn_Wo), f32(L1_attn_norm_w), f32(L1_attn_bq),
                              f32(L1_attn_bk), f32(L1_attn_bv), f32(L1_attn_bo))
    w3d["idhi"] = C["idhi"]
    w3d["cmask"] = C["cmask"]
    nc3 = _get("s3", build_stage3)
    r3 = run_bass_kernel_spmd(nc3, [dict(w3d, x2T=x2[c]) for c in cores], core_ids=cores)
    x3 = [r3.results[c]["x3T"] for c in cores]

    gwT = np.ascontiguousarray(f32(L1_ffn_gate_w).T)
    gbrow = np.ascontiguousarray(f32(L1_ffn_gate_b)[None, :])
    nc4 = _get("s4", build_stage4)
    r4 = run_bass_kernel_spmd(nc4, [
        {"x3T": x3[c], "gwT": gwT, "gbrow": gbrow, "ic64": C["ic64"],
         "ut128": C["ut128"], "idb": C["idb"]} for c in cores], core_ids=cores)
    xfT = [r4.results[c]["xfT"] for c in cores]
    xfr = [r4.results[c]["xfr"] for c in cores]
    cnts = np.concatenate([r4.results[c]["cnt"] for c in cores], 0)
    rinfo = [r4.results[c]["rinfo"] for c in cores]

    nc4b = _get("s4b", build_stage4b)
    in4b = []
    for c in cores:
        csel = (np.arange(8) < c).astype(np.float32)[:, None]
        tbase = (c * TOK + np.arange(128) + 1).astype(np.float32)[:, None]
        in4b.append({"cnts": cnts, "rinfo": rinfo[c], "csel": np.ascontiguousarray(csel),
                     "i512": C["i512"], "tbase": np.ascontiguousarray(tbase),
                     "ic64": C["ic64"]})
    r4b = run_bass_kernel_spmd(nc4b, in4b, core_ids=cores)
    mparts = [r4b.results[c]["mpart"] for c in cores]
    sinfo = [r4b.results[c]["sinfo"] for c in cores]

    xf_full = np.ascontiguousarray(np.concatenate(xfr, 0))
    w1 = f32(L1_ffn_w1); w2 = f32(L1_ffn_w2); w3 = f32(L1_ffn_w3)
    nc5 = _get("s5", build_stage5)
    in5 = []
    for c in cores:
        mloc = np.ascontiguousarray(
            np.stack([mp[c * 8:(c + 1) * 8] for mp in mparts], -1))
        w1T = np.zeros((8, D, I_PAD), np.float32)
        w1T[:, :, :I_EXP] = w1[c * 8:(c + 1) * 8].transpose(0, 2, 1)
        w3T = np.zeros((8, D, I_PAD), np.float32)
        w3T[:, :, :I_EXP] = w3[c * 8:(c + 1) * 8].transpose(0, 2, 1)
        w2T = np.zeros((8, I_PAD, D), np.float32)
        w2T[:, :I_EXP, :] = w2[c * 8:(c + 1) * 8].transpose(0, 2, 1)
        in5.append({"mloc": mloc, "xf_full": xf_full, "w1T": np.ascontiguousarray(w1T),
                    "w3T": np.ascontiguousarray(w3T), "w2T": np.ascontiguousarray(w2T),
                    "idb": C["idb"]})
    r5 = run_bass_kernel_spmd(nc5, in5, core_ids=cores)
    yb_full = np.ascontiguousarray(np.concatenate([r5.results[c]["yb"] for c in cores], 0))

    ws1T = np.ascontiguousarray(f32(L1_ffn_ws1).T)
    ws3T = np.ascontiguousarray(f32(L1_ffn_ws3).T)
    ws2T = np.ascontiguousarray(f32(L1_ffn_ws2).T)
    nc6 = _get("s6", build_stage6)
    r6 = run_bass_kernel_spmd(nc6, [
        {"x3T": x3[c], "xfT": xfT[c], "yb_full": yb_full, "sinfo": sinfo[c],
         "ws1T": ws1T, "ws3T": ws3T, "ws2T": ws2T, "idf": C["idf"]}
        for c in cores], core_ids=cores)
    out = np.concatenate([r6.results[c]["outr"].reshape(SEQ_PER_CORE, T, D)
                          for c in cores], 0)
    return out


# version 03
# speedup vs baseline: 1.1702x; 1.0031x over previous
import numpy as np
import ml_dtypes
import concourse.bass as bass
import concourse.mybir as mybir
from concourse.bass_utils import run_bass_kernel_spmd
from concourse.bass_core_rust import IndirectOffsetOnAxis
from contextlib import ExitStack

F32 = mybir.dt.float32
BF16 = mybir.dt.bfloat16
I32 = mybir.dt.int32
AF = mybir.ActivationFunctionType
ALU = mybir.AluOpType
AX = mybir.AxisListType

N_CORES = 8
B, T, D = 16, 2048, 1024
SEQ_PER_CORE = B // N_CORES
TOK = SEQ_PER_CORE * T
H, DH = 16, 64
E, TOPK, CAP = 64, 6, 512
I_EXP = 704
SI = 1408
EPS_LN = 1e-5
EPS_RMS = 1e-6
SHIFT = 12.0

BF = ml_dtypes.bfloat16


ENGINES = ("sync", "tensor", "vector", "scalar", "gpsimd")


class PySem:
    __slots__ = ("name", "v", "h")

    def __init__(self, name):
        self.name = name
        self.v = 0
        self.h = None


class Sched:

    def __init__(self, serialize=False, chained=False, ser_set=None):
        self.ser_set = set(ser_set) if ser_set else None
        self.items = {e: [] for e in ENGINES}
        self.sems = []
        self.fence_buf = None
        self.chained = chained
        self.chain_sems = {}
        self.serialize = serialize
        self.eng_sems = {}

    def sem(self, name):
        s = PySem(name)
        self.sems.append(s)
        return s

    def op(self, eng, fn, waits=(), incs=(), isdma=False):
        if self.chained:
            if eng not in self.chain_sems:
                self.chain_sems[eng] = self.sem(f"ch_{eng}")
                self.chain_last = getattr(self, "chain_last", {})
                self.chain_last[eng] = None
            last = self.chain_last[eng]
            if last is not None:
                waits = list(waits) + [last]
            if incs:
                s0, n0 = incs[0]
                self.chain_last[eng] = (s0, s0.v + n0)
            else:
                cs = self.chain_sems[eng]
                incs = list(incs) + [(cs, 16 if isdma else 1)]
                self.chain_last[eng] = (cs, cs.v + (16 if isdma else 1))
        if self.serialize:
            if not self.eng_sems:
                for e2 in ENGINES:
                    self.eng_sems[e2] = self.sem(f"ser_{e2}")
                self.last = {e2: None for e2 in ENGINES}
            if self.ser_set and "PEWAIT_ONLY" in self.ser_set:
                if eng in ("vector", "scalar"):
                    pesem = next((x for x in self.sems if x.name == "pe"), None)
                    if pesem is not None and pesem.v:
                        waits = list(waits) + [(pesem, pesem.v)]
                for (s_, n) in incs:
                    s_.v += n
                if eng in ("vector", "scalar") and incs and not isdma:
                    self.items[eng].append((fn, tuple(waits), ()))
                    self.items[eng].append((self._fence_fn(eng), (), tuple(incs)))
                else:
                    self.items[eng].append((fn, tuple(waits), tuple(incs)))
                return
            if self.ser_set and "DIR_PE_WAITS" in self.ser_set:
                extra = [w for e2, w in self.last.items()
                         if e2 != eng and w] if eng == "tensor" else []
            elif self.ser_set and "DIR_WAITS_PE" in self.ser_set:
                extra = [self.last["tensor"]] if (eng != "tensor" and self.last.get("tensor")) else []
            else:
                extra = [w for e2, w in self.last.items()
                         if e2 != eng and w and (self.ser_set is None or e2 in self.ser_set
                                                 or eng in self.ser_set)]
            waits = list(waits) + extra
        for (s, n) in incs:
            s.v += n
        if eng in ("vector", "scalar") and incs and not isdma:
            self.items[eng].append((fn, tuple(waits), ()))
            self.items[eng].append((self._fence_fn(eng), (), tuple(incs)))
        else:
            self.items[eng].append((fn, tuple(waits), tuple(incs)))
        if self.serialize:
            if isdma and incs:
                self.last[eng] = (incs[0][0], incs[0][0].v)
            else:
                ser = self.eng_sems[eng]
                ser.v += 1
                self.items[eng].append((self._ser_fence_fn(eng), (), ((ser, 1),)))
                self.last[eng] = (ser, ser.v)

    def _ser_fence_fn(self, eng):
        if eng in ("vector", "scalar"):
            return self._fence_fn(eng)
        if eng == "tensor":
            return lambda e: e.matmul(self.ser_junk, self.ser_ones1, self.ser_onest,
                                      start=True, stop=True)
        return lambda e: e.nop()

    def _fence_fn(self, eng):
        if self.fence_buf is None:
            raise RuntimeError("fence buffer not set")
        fb = self.fence_buf
        if eng == "vector":
            return lambda e: e.memset(fb[:, 0:1], 0.0)
        return lambda e: e.activation(out=fb[:, 1:2], in_=fb[:, 1:2], func=AF.Identity)

    def dma(self, eng, out, in_, waits=(), sem=None, **kw):
        incs = [(sem, 16)] if sem is not None else []
        self.op(eng, lambda e: e.dma_start(out=out, in_=in_, **kw), waits, incs, isdma=True)
        return sem.v if sem is not None else None

    def build(self, nc, ctx):
        for s in self.sems:
            s.h = ctx.enter_context(nc.semaphore(s.name))
        block = ctx.enter_context(nc.Block())
        for eng_name in ENGINES:
            items = self.items[eng_name]
            if not items:
                continue

            def make_body(items):
                def body(eng):
                    for fn, waits, incs in items:
                        for (s, v) in waits:
                            eng.wait_ge(s.h, v)
                        inst = fn(eng)
                        for (s, n) in incs:
                            if inst is None:
                                inst = eng.nop()
                            inst.then_inc(s.h, n)
                return body

            getattr(block, eng_name)(make_body(items))


def ceil_div(a, b):
    return (a + b - 1) // b



def emit_ln_chunk(S, bufs, X, tsl, ld_wait, out_ap, eps=EPS_LN, rms=False, nw=None):
    (pe, dv, sc) = bufs["pe"], bufs["dv"], bufs["sc"]
    ones1, ones128 = bufs["ones1"], bufs["ones128"]
    sq, row = bufs["sq"], bufs["row"]
    p_m = bufs["p_m"][0:1, :]
    p_sq = bufs["p_sq"][0:1, :]
    p_rb, p_mrb = bufs["p_rb"], bufs["p_mrb"]

    for k in range(8):
        pk = k % 2
        waits = list(ld_wait)
        if k >= 2:
            waits.append((pe, pe.v))
        S.op("scalar", (lambda k=k, pk=pk, tsl=tsl: lambda e: e.activation(
            out=sq[:, pk, :], in_=X[:, k, tsl], func=AF.Square))(),
            waits=waits, incs=[(sc, 1)])
        S.op("tensor", (lambda k=k, tsl=tsl: lambda e: e.matmul(
            p_m[:, :], ones1[:, :], X[:, k, tsl],
            start=(k == 0), stop=(k == 7)))(),
            waits=list(ld_wait) if k == 0 else [])
        S.op("tensor", (lambda k=k, pk=pk: lambda e: e.matmul(
            p_sq[:, :], ones1[:, :], sq[:, pk, :],
            start=(k == 0), stop=(k == 7)))(),
            waits=[(sc, sc.v)], incs=[(pe, 1)])
    pe_stats = emit_drain(S, bufs, bufs["ln_junk"])
    m_r = row[:, 0:512]; ms_r = row[:, 512:1024]; var_r = row[:, 1024:1536]
    r_r = row[:, 1536:2048]; mr_r = row[:, 512:1024]
    S.op("vector", lambda e: e.tensor_single_scalar(m_r, p_m[:, :], 1.0 / D, ALU.mult),
         waits=[(pe, pe_stats)])
    S.op("vector", lambda e: e.tensor_single_scalar(ms_r, p_sq[:, :], 1.0 / D, ALU.mult))
    S.op("vector", lambda e: e.tensor_mul(var_r, m_r, m_r))
    S.op("vector", lambda e: e.tensor_sub(var_r, ms_r, var_r), incs=[(dv, 1)])
    eps_ap = bufs["cb"][0:1, 0:1] if eps == EPS_LN else bufs["cb"][0:1, 2:3]
    S.op("scalar", (lambda var_r=var_r, eps_ap=eps_ap: lambda e: e.activation(
        out=var_r, in_=var_r, func=AF.Ln, bias=eps_ap))(),
        waits=[(dv, dv.v)])
    S.op("scalar", lambda e: e.activation(out=r_r, in_=var_r, func=AF.Exp, scale=-0.5),
         incs=[(sc, 1)])
    if rms:
        S.op("scalar", lambda e: e.activation(out=bufs["tmprow"][0:1, 0:512], in_=var_r,
                                              func=AF.Exp, scale=-1.0), incs=[(sc, 1)])
        S.op("vector", lambda e: e.tensor_single_scalar(
            bufs["tmprow"][0:1, 512:1024], bufs["tmprow"][0:1, 0:512], -EPS_LN, ALU.mult),
            waits=[(sc, sc.v)])
        S.op("vector", lambda e: e.tensor_single_scalar(
            bufs["tmprow"][0:1, 512:1024], bufs["tmprow"][0:1, 512:1024], 1.0, ALU.add),
            incs=[(dv, 1)])
        S.op("scalar", lambda e: e.activation(out=bufs["tmprow"][0:1, 512:1024],
                                              in_=bufs["tmprow"][0:1, 512:1024],
                                              func=AF.Ln, bias=bufs["cb"][0:1, 2:3]),
             waits=[(dv, dv.v)])
        S.op("scalar", lambda e: e.activation(out=bufs["tmprow"][0:1, 512:1024],
                                              in_=bufs["tmprow"][0:1, 512:1024],
                                              func=AF.Exp, scale=-0.5), incs=[(sc, 1)])
        S.op("vector", lambda e: e.tensor_mul(r_r, r_r, bufs["tmprow"][0:1, 512:1024]),
             waits=[(sc, sc.v)])
    S.op("vector", lambda e: e.tensor_mul(mr_r, m_r, r_r),
         waits=[(sc, sc.v)], incs=[(dv, 1)])
    S.op("tensor", lambda e: e.matmul(p_rb[:, :], ones128[:, :], r_r, start=True, stop=True),
         waits=[(dv, dv.v)])
    S.op("tensor", lambda e: e.matmul(p_mrb[:, :], ones128[:, :], mr_r, start=True, stop=True))
    pe_bc = emit_drain(S, bufs, bufs["ln_junk"])
    for k in range(8):
        S.op("vector", (lambda k=k, tsl=tsl: lambda e: e.tensor_mul(
            bufs["tmp8"][:, :], X[:, k, tsl], p_rb[:, :]))(),
            waits=[(pe, pe_bc)] if k == 0 else [])
        if nw is None:
            S.op("vector", (lambda k=k: lambda e: e.tensor_sub(
                out_ap(k), bufs["tmp8"][:, :], p_mrb[:, :]))(),
                incs=[(dv, 1)] if k == 7 else [])
        else:
            S.op("vector", (lambda k=k: lambda e: e.tensor_sub(
                bufs["tmp8"][:, :], bufs["tmp8"][:, :], p_mrb[:, :]))())
            S.op("vector", (lambda k=k: lambda e: e.tensor_scalar(
                out_ap(k), bufs["tmp8"][:, :], nw[:, k:k + 1], None, ALU.mult))(),
                incs=[(dv, 1)] if k == 7 else [])
    S.op("tensor", lambda e: None, waits=[(dv, dv.v)])
    return dv.v


def emit_layernorm(S, nc, bufs, X, xn, seq_len, ld_wait, eps=EPS_LN):
    for tc in range(seq_len // 512):
        tsl = slice(tc * 512, (tc + 1) * 512)
        emit_ln_chunk(S, bufs, X, tsl, ld_wait,
                      (lambda tsl=tsl: lambda k: xn[:, k, tsl])(), eps=eps)
    return bufs["dv"].v


def alloc_common(S, nc, ctx):
    sb = lambda name, shape, dt=F32: ctx.enter_context(nc.sbuf_tensor(name, shape, dt))
    ps = lambda name, shape, dt=F32: ctx.enter_context(nc.psum_tensor(name, shape, dt))
    bufs = {
        "pe": S.sem("pe"), "dv": S.sem("dv"), "sc": S.sem("sc"),
        "ld": S.sem("ld"), "gq": S.sem("gq"), "st": S.sem("st"),
        "ones1": sb("ones1", [128, 1]),
        "ones128": sb("ones128", [1, 128]),
        "onest": sb("onest", [128, 256]),
        "cb": sb("cb", [128, 8]),
        "sq": sb("sq", [128, 2, 512]),
        "row": sb("row", [1, 4 * 512]),
        "tmprow": sb("tmprow", [1, 2 * 512]),
        "tmp8": sb("tmp8", [128, 512]),
    }
    S.fence_buf = sb("fence", [128, 2])
    dv = bufs["dv"]
    S.op("vector", lambda e: e.memset(bufs["ones1"][:, :], 1.0), incs=[(dv, 1)])
    S.op("vector", lambda e: e.memset(bufs["ones128"][:, :], 1.0), incs=[(dv, 1)])
    S.op("vector", lambda e: e.memset(bufs["onest"][:, :], 1.0), incs=[(dv, 1)])
    S.op("vector", lambda e: e.memset(bufs["cb"][:, 0:1], EPS_LN), incs=[(dv, 1)])
    S.op("vector", lambda e: e.memset(bufs["cb"][:, 1:2], -SHIFT), incs=[(dv, 1)])
    S.op("vector", lambda e: e.memset(bufs["cb"][:, 2:3], EPS_RMS), incs=[(dv, 1)])
    bufs["init_dv"] = dv.v
    return bufs, sb, ps



def emit_drain(S, bufs, junk_view):
    pe = bufs["pe"]
    S.op("tensor", lambda e: e.matmul(junk_view, bufs["ones1"][0:1, 0:1],
                                      bufs["onest"][0:1, 0:256], start=True, stop=True),
         incs=[(pe, 1)])
    return pe.v


def emit_attn_head(S, bufs, *, qS, kS, vaug_v, X_add, h, qsl, kt_lo, kt_hi,
                   pt, p1buf, p_scA, p_scB, p_sums, p_o, p_rb2, p_sh,
                   osb, rro, t1, id64, att_junk, mask_sb=None, mask_tiles=(),
                   scale=0.125, c_pass1=8.0, nq=512):
    pe, dv, sc = bufs["pe"], bufs["dv"], bufs["sc"]
    nkt = kt_hi - kt_lo
    dv_floor = bufs.get("dv_p2", None)
    if dv_floor is None:
        dv_floor = dv.v
    for i, kt in enumerate(range(kt_lo, kt_hi)):
        bank = p_scA if i % 2 == 0 else p_scB
        w = [(sc, sc.v)] if i >= 2 else [(sc, sc.v), (dv, dv_floor)]
        S.op("tensor", (lambda kt=kt, bank=bank, i=i: lambda e: e.matmul(
            bank[:, 0:nq], kS[0:64, kt * 128:(kt + 1) * 128], qS[0:64, qsl],
            start=True, stop=True))(), waits=w, incs=[(pe, 1)])
        if kt in mask_tiles:
            S.op("vector", (lambda kt=kt, bank=bank: lambda e: e.tensor_add(
                bank[:, 0:nq], bank[:, 0:nq], mask_tiles[kt]))(),
                waits=[(pe, pe.v)], incs=[(dv, 1)])
            S.op("scalar", (lambda i=i, bank=bank: lambda e: e.activation(
                out=p1buf[:, (i % 2) * nq:(i % 2) * nq + nq], in_=bank[:, 0:nq],
                func=AF.Exp, scale=scale / c_pass1))(),
                waits=[(dv, dv.v)], incs=[(sc, 1)])
        else:
            S.op("scalar", (lambda i=i, bank=bank: lambda e: e.activation(
                out=p1buf[:, (i % 2) * nq:(i % 2) * nq + nq], in_=bank[:, 0:nq],
                func=AF.Exp, scale=scale / c_pass1))(),
                waits=[(pe, pe.v)], incs=[(sc, 1)])
        sacc = bufs["sacc"]
        if i == 0:
            S.op("vector", lambda e: e.tensor_copy(sacc[:, 0:nq], p1buf[:, 0:nq]),
                 waits=[(sc, sc.v)])
        else:
            S.op("vector", (lambda i=i: lambda e: e.tensor_add(
                sacc[:, 0:nq], sacc[:, 0:nq],
                p1buf[:, (i % 2) * nq:(i % 2) * nq + nq]))(),
                waits=[(sc, sc.v)], incs=[(dv, 1)] if i == nkt - 1 else [])
    S.op("tensor", lambda e: e.matmul(p_sums[0:1, 0:nq], bufs["onesc_bf"][:, 0:1],
                                      sacc[:, 0:nq], start=True, stop=True),
         waits=[(dv, dv.v)])
    pe_s1 = emit_drain(S, bufs, att_junk)
    S.op("scalar", lambda e: e.activation(out=rro[0:1, 0:nq], in_=p_sums[0:1, 0:nq],
                                          func=AF.Ln),
         waits=[(pe, pe_s1)], incs=[(sc, 1)])
    S.op("vector", (lambda: lambda e: e.tensor_single_scalar(
        rro[0:1, 0:nq], rro[0:1, 0:nq], -c_pass1 / scale, ALU.mult))(),
        waits=[(sc, sc.v)], incs=[(dv, 1)])
    S.op("tensor", lambda e: e.matmul(p_sh[64:65, 0:nq], bufs["ones1"][0:1, 0:1],
                                      rro[0:1, 0:nq], start=True, stop=True),
         waits=[(dv, dv.v)])
    pe_m = emit_drain(S, bufs, att_junk)
    S.op("vector", (lambda: lambda e: e.tensor_copy(qS[64:65, qsl], p_sh[64:65, 0:nq]))(),
         waits=[(pe, pe_m)], incs=[(dv, 1)])
    dv_m = dv.v
    sc_tiles = []
    for i, kt in enumerate(range(kt_lo, kt_hi)):
        bank = p_scA if i % 2 == 0 else p_scB
        w = [(dv, dv_m)] if i < 2 else [(sc, sc.v)]
        S.op("tensor", (lambda kt=kt, bank=bank: lambda e: e.matmul(
            bank[:, 0:nq], kS[0:65, kt * 128:(kt + 1) * 128], qS[0:65, qsl],
            start=True, stop=True))(), waits=w, incs=[(pe, 1)])
        if kt in mask_tiles:
            S.op("vector", (lambda kt=kt, bank=bank: lambda e: e.tensor_add(
                bank[:, 0:nq], bank[:, 0:nq], mask_tiles[kt]))(),
                waits=[(pe, pe.v)], incs=[(dv, 1)])
            S.op("scalar", (lambda i=i, bank=bank: lambda e: e.activation(
                out=pt[:, i * nq:(i + 1) * nq], in_=bank[:, 0:nq],
                func=AF.Exp, scale=scale))(),
                waits=[(dv, dv.v)], incs=[(sc, 1)])
        else:
            S.op("scalar", (lambda i=i, bank=bank: lambda e: e.activation(
                out=pt[:, i * nq:(i + 1) * nq], in_=bank[:, 0:nq],
                func=AF.Exp, scale=scale))(),
                waits=[(pe, pe.v)], incs=[(sc, 1)])
        sc_tiles.append(sc.v)
    bufs["dv_p2"] = dv.v
    for i, kt in enumerate(range(kt_lo, kt_hi)):
        S.op("tensor", (lambda kt=kt, i=i, nkt=nkt: lambda e: e.matmul(
            p_o[0:65, 0:nq], vaug_v[:, kt, :], pt[:, i * nq:(i + 1) * nq],
            start=(i == 0), stop=(i == nkt - 1)))(),
            waits=[(sc, sc_tiles[min(i + 2, nkt - 1)])])
    pe_av = emit_drain(S, bufs, att_junk)
    S.op("vector", lambda e: e.tensor_copy(osb[:, 0:nq], p_o[0:65, 0:nq]),
         waits=[(pe, pe_av)])
    S.op("vector", lambda e: e.tensor_copy(rro[64:65, 0:nq], osb[64:65, 0:nq]))
    S.op("vector", lambda e: e.reciprocal(rro[64:65, 0:nq], rro[64:65, 0:nq]),
         incs=[(dv, 1)])
    S.op("tensor", lambda e: e.matmul(p_rb2[0:64, 0:nq], bufs["onest"][64:65, 0:64],
                                      rro[64:65, 0:nq], start=True, stop=True),
         waits=[(dv, dv.v)])
    pe_rb2 = emit_drain(S, bufs, att_junk)
    S.op("vector", lambda e: e.tensor_mul(t1[:, 0:nq], osb[0:64, 0:nq], p_rb2[0:64, 0:nq]),
         waits=[(pe, pe_rb2)], incs=[(dv, 1)])
    X_add(t1, p_sh, id64)
    S.op("tensor", lambda e: None, waits=[(dv, dv.v)])


def build_stage1(serialize=False, chained=False):
    nc = bass.Bass("TRN2", target_bir_lowering=False, debug=False)
    x0T = nc.dram_tensor("x0T", [D, TOK], F32, kind="ExternalInput").ap()
    wqkT = nc.dram_tensor("wqkT", [D, 2 * D], F32, kind="ExternalInput").ap()
    wvT = nc.dram_tensor("wvT", [D, D], F32, kind="ExternalInput").ap()
    idhi = nc.dram_tensor("idhi", [128, 64], F32, kind="ExternalInput").ap()
    x1T = nc.dram_tensor("x1T", [D, TOK], F32, kind="ExternalOutput").ap()

    S = Sched(serialize=serialize, chained=chained)
    ctx = ExitStack()
    bufs, sb, ps = alloc_common(S, nc, ctx)
    pe, dv, sc = bufs["pe"], bufs["dv"], bufs["sc"]
    ld, gq, st = bufs["ld"], bufs["gq"], bufs["st"]

    X = sb("X", [128, 8, T])
    xn = sb("xn", [128, 8, T], BF16)
    vaug = sb("vaug", [128, H * 16 * 65], BF16)
    qS = sb("qS", [65, 2 * T], BF16)
    kS = sb("kS", [65, 2 * T], BF16)
    pt = sb("pt", [128, 16 * 512], BF16)
    p1buf = sb("p1buf", [128, 2 * 512], BF16)
    wv = sb("wv", [128, 8, 512], BF16)
    wqk = sb("wqk", [128, 2, 8, 256], BF16)
    id64 = sb("id64", [128, 64])
    osb = sb("osb", [65, 512])
    rro = sb("rro", [128, 512])
    bufs["onesc_bf"] = sb("onesc_bf", [128, 1], BF16)
    bufs["sacc"] = pt[:, 15 * 512:16 * 512]
    t1 = bufs["tmp8"][0:64, :]

    P = [ps(f"P{i}", [128, 512]) for i in range(8)]
    bufs["p_m"], bufs["p_sq"], bufs["p_rb"], bufs["p_mrb"] = P[0], P[1], P[2], P[3]
    bufs["ln_junk"] = P[5][0:1, 0:256]
    p_qk = P[4]
    p_scA, p_scB = P[2], P[3]
    p_sums = P[1]
    p_o = P[4]
    p_rb2 = P[6]
    p_sh = P[5]
    att_junk = P[0][0:1, 0:256]

    vaug_vh = vaug.rearrange("p (h t c) -> p h t c", h=H, c=65)
    S.op("vector", lambda e: e.memset(vaug_vh[:, :, :, 64:65], 1.0), incs=[(dv, 1)])
    S.op("vector", lambda e: e.memset(kS[64:65, :], 1.0), incs=[(dv, 1)])
    S.op("vector", lambda e: e.memset(bufs["onesc_bf"][:, :], 1.0), incs=[(dv, 1)])
    init_dv = dv.v

    S.dma("sync", id64[:, :], idhi[:, :], sem=ld)
    ld_id = ld.v


    for s in range(SEQ_PER_CORE):
        t0 = s * T
        x_src = x0T[:, t0:t0 + T].rearrange("(c p) t -> p c t", p=128)
        S.dma("sync", X[:, :, :], x_src, waits=[(st, st.v)], sem=ld)
        ld_x = ld.v

        dv_xn = emit_layernorm(S, nc, bufs, X, xn, T,
                               [(ld, ld_x)] + ([(bufs["dv"], bufs["init_dv"])] if s == 0 else []))

        for half in range(2):
            S.op("gpsimd", lambda e: None, waits=[(pe, pe.v)])
            ld_wv = S.dma("gpsimd", wv[:, :, :],
                          wvT[:, half * 512:(half + 1) * 512].rearrange("(c p) f -> p c f", p=128),
                          sem=gq)
            for tt in range(16):
                for k in range(8):
                    S.op("tensor", (lambda tt=tt, k=k: lambda e: e.matmul(
                        p_qk[:, :], xn[:, k, tt * 128:(tt + 1) * 128],
                        wv[:, k, :],
                        start=(k == 0), stop=(k == 7)))(),
                        waits=([(dv, dv_xn), (gq, ld_wv)] if (tt == 0 and k == 0) else []))
                pe_v = emit_drain(S, bufs, att_junk)
                for hh in range(8):
                    h = half * 8 + hh
                    S.op("vector", (lambda tt=tt, h=h, hh=hh: lambda e: e.tensor_copy(
                        vaug_vh[:, h, tt, 0:64], p_qk[:, hh * 64:(hh + 1) * 64]))(),
                        waits=[(pe, pe_v)] if hh == 0 else [],
                        incs=[(dv, 1)] if hh == 7 else [])
                S.op("tensor", lambda e: None, waits=[(dv, dv.v)])

        for g in range(H // 2):
            gb = g % 2
            wsrc = wqkT[:, g * 256:(g + 1) * 256].rearrange("(c p) f -> p c f", p=128)
            if s > 0 or g >= 2:
                S.op("gpsimd", lambda e: None, waits=[(pe, pe.v)])
            ld_w = S.dma("gpsimd", wqk[:, gb, :, :], wsrc, sem=gq)

            for dst, c0 in ((qS, 0), (kS, 128)):
                for ncx in range(4):
                    for k in range(8):
                        S.op("tensor", (lambda gb=gb, ncx=ncx, k=k, c0=c0: lambda e: e.matmul(
                            p_qk[:, :], wqk[:, gb, k, c0:c0 + 128],
                            xn[:, k, ncx * 512:(ncx + 1) * 512],
                            start=(k == 0), stop=(k == 7)))(),
                            waits=[(gq, ld_w), (ld, ld_id)] if (k == 0 and ncx == 0 and c0 == 0) else [])
                    pe_qk = emit_drain(S, bufs, att_junk)
                    S.op("vector", (lambda dst=dst, ncx=ncx: lambda e: e.tensor_copy(
                        dst[0:64, ncx * 512:(ncx + 1) * 512], p_qk[0:64, :]))(),
                        waits=[(pe, pe_qk)])
                    S.op("vector", (lambda ncx=ncx: lambda e: e.tensor_copy(
                        bufs["tmp8"][64:128, :], p_qk[64:128, :]))(),
                        incs=[(dv, 1)])
                    S.op("tensor", lambda e: e.matmul(
                        p_sh[0:64, :], id64[64:128, :], bufs["tmp8"][64:128, :],
                        start=True, stop=True),
                        waits=[(dv, dv.v)])
                    pe_sh2 = emit_drain(S, bufs, att_junk)
                    S.op("vector", (lambda dst=dst, ncx=ncx: lambda e: e.tensor_copy(
                        dst[0:64, T + ncx * 512:T + (ncx + 1) * 512], p_sh[0:64, :]))(),
                        waits=[(pe, pe_sh2)], incs=[(dv, 1)])
                    S.op("tensor", lambda e: None, waits=[(dv, dv.v)])
            dv_qk = dv.v

            for sub in range(2):
                h = 2 * g + sub
                qSh = qS[:, sub * T:(sub + 1) * T]
                kSh = kS[:, sub * T:(sub + 1) * T]
                c = h // 2
                for qc in range(4):
                    qsl = slice(qc * 512, (qc + 1) * 512)

                    def X_add(t1v, p_shv, id64v, c=c, qsl=qsl, sub=sub):
                        if sub == 0:
                            S.op("vector", (lambda: lambda e: e.tensor_add(
                                X[0:64, c, qsl], X[0:64, c, qsl], t1v[:, 0:512]))(),
                                incs=[(dv, 1)])
                        else:
                            S.op("tensor", (lambda: lambda e: e.matmul(
                                p_shv[64:128, :], id64v[0:64, :],
                                t1v[:, 0:512], start=True, stop=True))(),
                                waits=[(dv, dv.v)])
                            pe_sh = emit_drain(S, bufs, att_junk)
                            S.op("vector", (lambda: lambda e: e.tensor_add(
                                X[64:128, c, qsl], X[64:128, c, qsl], p_shv[64:128, :]))(),
                                waits=[(pe, pe_sh)], incs=[(dv, 1)])

                    emit_attn_head(S, bufs, qS=qSh, kS=kSh, vaug_v=vaug_vh[:, h, :, :],
                                   X_add=X_add, h=h, qsl=qsl, kt_lo=0, kt_hi=16,
                                   pt=pt, p1buf=p1buf, p_scA=p_scA, p_scB=p_scB,
                                   p_sums=p_sums, p_o=p_o, p_rb2=p_rb2, p_sh=p_sh,
                                   osb=osb, rro=rro, t1=t1, id64=id64,
                                   att_junk=att_junk)

        x_dst = x1T[:, t0:t0 + T].rearrange("(c p) t -> p c t", p=128)
        S.dma("sync", x_dst, X[:, :, :], waits=[(dv, dv.v)], sem=st)

    S.op("sync", lambda e: None, waits=[(st, st.v)])
    S.build(nc, ctx)
    ctx.close()
    return nc


def shard_tokens_fm(x_full):
    out = []
    for c in range(N_CORES):
        sl = x_full[c * SEQ_PER_CORE:(c + 1) * SEQ_PER_CORE].reshape(TOK, D)
        out.append(np.ascontiguousarray(np.asarray(sl, dtype=np.float32).T))
    return out


def prep_wqkv(W_qkv):
    W = np.asarray(W_qkv, dtype=np.float32).reshape(H, 3, DH, D)
    pairs = []
    for g in range(H // 2):
        pairs.append(W[2 * g, 0]); pairs.append(W[2 * g + 1, 0])
        pairs.append(W[2 * g, 1]); pairs.append(W[2 * g + 1, 1])
    qk = np.concatenate(pairs, 0)
    v = W[:, 2].reshape(H * DH, D)
    return np.ascontiguousarray(qk.T), np.ascontiguousarray(v.T)






def build_stage2(serialize=False, chained=False):
    nc = bass.Bass("TRN2", target_bir_lowering=False, debug=False)
    x1T = nc.dram_tensor("x1T", [D, TOK], F32, kind="ExternalInput").ap()
    wqTb = nc.dram_tensor("wqTb", [D, D], BF16, kind="ExternalInput").ap()
    wscale = nc.dram_tensor("wscale", [128, 8], F32, kind="ExternalInput").ap()
    fbias = nc.dram_tensor("fbias", [128, 8], F32, kind="ExternalInput").ap()
    idf_in = nc.dram_tensor("idf", [128, 128], F32, kind="ExternalInput").ap()
    x2T = nc.dram_tensor("x2T", [D, TOK], F32, kind="ExternalOutput").ap()

    S = Sched(serialize=serialize, chained=chained)
    ctx = ExitStack()
    bufs, sb, ps = alloc_common(S, nc, ctx)
    pe, dv, sc = bufs["pe"], bufs["dv"], bufs["sc"]
    ld, gq, st = bufs["ld"], bufs["gq"], bufs["st"]

    X = sb("X", [128, 8, T])
    xnf = sb("xnf", [128, 8, 512])
    xq = sb("xq", [128, 8, 512], BF16)
    wq = sb("wq", [128, 8, D], BF16)
    wsc = sb("wsc", [128, 8])
    fbc = sb("fbc", [128, 8])
    bufs["idf"] = sb("idf_sb", [128, 128])
    amaxc = sb("amaxc", [128, 2])
    rrow = sb("rrow", [1, 3 * 512])
    ti = sb("ti", [128, 512], I32)
    tf = sb("tf", [128, 512])
    td = sb("td", [128, 512])
    tg = sb("tg", [128, 512])
    t2 = sb("t2", [128, 512])

    P = [ps(f"P{i}", [128, 512]) for i in range(8)]
    bufs["p_m"], bufs["p_sq"], bufs["p_rb"], bufs["p_mrb"] = P[0], P[1], P[2], P[3]
    bufs["ln_junk"] = P[5][0:1, 0:256]
    p_tr = P[4]
    p_out = P[5]
    p_xs = P[6]
    p_junk = P[7][0:1, 0:256]

    ld_w = S.dma("gpsimd", wq[:, :, :], wqTb.rearrange("(c p) f -> p c f", p=128), sem=gq)
    S.dma("sync", wsc[:, :], wscale[:, :], sem=ld)
    S.dma("sync", fbc[:, :], fbias[:, :], sem=ld)
    S.dma("sync", bufs["idf"][:, :], idf_in[:, :], sem=ld)
    ld_const = ld.v

    for s in range(SEQ_PER_CORE):
        t0 = s * T
        x_src = x1T[:, t0:t0 + T].rearrange("(c p) t -> p c t", p=128)
        S.dma("sync", X[:, :, :], x_src, waits=[(st, st.v)], sem=ld)
        ld_x = ld.v

        for tc in range(4):
            tsl = slice(tc * 512, (tc + 1) * 512)
            dv_ln = emit_ln_chunk(
                S, bufs, X, tsl,
                [(ld, ld_x)] + ([(dv, bufs["init_dv"]), (ld, ld_const)] if (s == 0 and tc == 0) else []),
                lambda k: xnf[:, k, :])
            for tt in range(4):
                for k in range(8):
                    S.op("tensor", (lambda k=k, tt=tt: lambda e: e.transpose(
                        p_tr[:, 0:128], xnf[:, k, tt * 128:(tt + 1) * 128], bufs["idf"][:, :]))(),
                        waits=[(dv, dv_ln)] if (tt == 0 and k == 0) else [])
                    pe_t = emit_drain(S, bufs, p_junk)
                    S.op("vector", (lambda k=k: lambda e: e.tensor_reduce(
                        amaxc[:, 1:2], p_tr[:, 0:128], AX.X, ALU.max,
                        apply_absolute_value=True))(),
                        waits=[(pe, pe_t)])
                    if k == 0:
                        S.op("vector", lambda e: e.tensor_copy(amaxc[:, 0:1], amaxc[:, 1:2]),
                             waits=[(pe, pe.v)], incs=[(dv, 1)])
                    else:
                        S.op("vector", lambda e: e.tensor_tensor(
                            amaxc[:, 0:1], amaxc[:, 0:1], amaxc[:, 1:2], ALU.max),
                            incs=[(dv, 1)])
                    S.op("tensor", lambda e: None, waits=[(dv, dv.v)])
                S.op("tensor", (lambda tt=tt: lambda e: e.transpose(
                    p_tr[0:1, 0:128], amaxc[:, 0:1], bufs["idf"][:, :]))())
                pe_t2 = emit_drain(S, bufs, p_junk)
                S.op("vector", (lambda tt=tt: lambda e: e.tensor_copy(
                    rrow[0:1, tt * 128:(tt + 1) * 128], p_tr[0:1, 0:128]))(),
                    waits=[(pe, pe_t2)], incs=[(dv, 1)])
                S.op("tensor", lambda e: None, waits=[(dv, dv.v)])
            S.op("vector", lambda e: e.tensor_single_scalar(
                rrow[0:1, 0:512], rrow[0:1, 0:512], 1e-6, ALU.max))
            S.op("vector", lambda e: e.tensor_single_scalar(
                rrow[0:1, 512:1024], rrow[0:1, 0:512], 1.0 / 127.0, ALU.mult))
            S.op("vector", lambda e: e.reciprocal(rrow[0:1, 1024:1536], rrow[0:1, 512:1024]),
                 incs=[(dv, 1)])
            S.op("tensor", lambda e: e.matmul(p_tr[:, :], bufs["ones128"][:, :],
                                              rrow[0:1, 1024:1536], start=True, stop=True),
                 waits=[(dv, dv.v)])
            S.op("tensor", lambda e: e.matmul(p_xs[:, :], bufs["ones128"][:, :],
                                              rrow[0:1, 512:1024], start=True, stop=True))
            pe_bc = emit_drain(S, bufs, p_junk)
            for k in range(8):
                S.op("vector", (lambda k=k: lambda e: e.tensor_mul(
                    tf[:, :], xnf[:, k, :], p_tr[:, :]))(),
                    waits=[(pe, pe_bc)] if k == 0 else [])
                S.op("vector", lambda e: e.tensor_copy(ti[:, :], tf[:, :]))
                S.op("vector", lambda e: e.tensor_copy(td[:, :], ti[:, :]))
                S.op("vector", lambda e: e.tensor_sub(tg[:, :], tf[:, :], td[:, :]))
                S.op("vector", lambda e: e.scalar_tensor_tensor(
                    t2[:, :], tg[:, :], 0.5, td[:, :], ALU.is_ge, ALU.add))
                S.op("vector", lambda e: e.tensor_single_scalar(
                    tg[:, :], tg[:, :], -0.5, ALU.is_le))
                S.op("vector", (lambda k=k: lambda e: e.tensor_sub(
                    xq[:, k, :], t2[:, :], tg[:, :]))(),
                    incs=[(dv, 1)] if k == 7 else [])
            dv_q = dv.v
            for o in range(8):
                for k in range(8):
                    S.op("tensor", (lambda o=o, k=k: lambda e: e.matmul(
                        p_out[:, :], wq[:, k, o * 128:(o + 1) * 128], xq[:, k, :],
                        start=(k == 0), stop=(k == 7)))(),
                        waits=[(dv, dv_q), (gq, ld_w)] if (o == 0 and k == 0) else
                              ([(dv, dv.v)] if k == 0 else []))
                pe_o = emit_drain(S, bufs, p_junk)
                S.op("scalar", (lambda o=o: lambda e: e.activation(
                    out=t2[:, :], in_=p_out[:, :], func=AF.Copy, scale=wsc[:, o:o + 1]))(),
                    waits=[(pe, pe_o)], incs=[(sc, 1)])
                S.op("vector", lambda e: e.tensor_mul(t2[:, :], t2[:, :], p_xs[:, :]),
                     waits=[(sc, sc.v)])
                S.op("vector", (lambda o=o: lambda e: e.tensor_scalar(
                    t2[:, :], t2[:, :], fbc[:, o:o + 1], None, ALU.add))(),)
                S.op("vector", (lambda o=o, tsl=tsl: lambda e: e.tensor_add(
                    X[:, o, tsl], X[:, o, tsl], t2[:, :]))(),
                    incs=[(dv, 1)])
            S.op("tensor", lambda e: None, waits=[(dv, dv.v)])

        x_dst = x2T[:, t0:t0 + T].rearrange("(c p) t -> p c t", p=128)
        S.dma("sync", x_dst, X[:, :, :], waits=[(dv, dv.v)], sem=st)

    S.op("sync", lambda e: None, waits=[(st, st.v)])
    S.build(nc, ctx)
    ctx.close()
    return nc



def build_stage3(serialize=False, chained=False):
    nc = bass.Bass("TRN2", target_bir_lowering=False, debug=False)
    x2T = nc.dram_tensor("x2T", [D, TOK], F32, kind="ExternalInput").ap()
    wqkT3 = nc.dram_tensor("wqkT3", [D, 2 * D], F32, kind="ExternalInput").ap()
    wvT3 = nc.dram_tensor("wvT3", [D, D], F32, kind="ExternalInput").ap()
    woT = nc.dram_tensor("woT", [D, D], F32, kind="ExternalInput").ap()
    nw_in = nc.dram_tensor("nw", [128, 8], F32, kind="ExternalInput").ap()
    bqk_in = nc.dram_tensor("bqk", [1, 2 * D], F32, kind="ExternalInput").ap()
    bv_in = nc.dram_tensor("bv", [1, D], F32, kind="ExternalInput").ap()
    bo_in = nc.dram_tensor("bo", [1, D], F32, kind="ExternalInput").ap()
    idhi = nc.dram_tensor("idhi", [128, 64], F32, kind="ExternalInput").ap()
    cmask_in = nc.dram_tensor("cmask", [128, 896], F32, kind="ExternalInput").ap()
    x3T = nc.dram_tensor("x3T", [D, TOK], F32, kind="ExternalOutput").ap()

    S = Sched(serialize=serialize, chained=chained)
    ctx = ExitStack()
    bufs, sb, ps = alloc_common(S, nc, ctx)
    pe, dv, sc = bufs["pe"], bufs["dv"], bufs["sc"]
    ld, gq, st = bufs["ld"], bufs["gq"], bufs["st"]

    X = sb("X", [128, 8, T])
    xn = sb("xn", [128, 8, T], BF16)
    vaug = sb("vaug", [128, 8 * 16 * 65], BF16)
    qS = sb("qS", [65, 2 * T], BF16)
    kS = sb("kS", [65, 2 * T], BF16)
    Ob = sb("Ob", [128, 4, T], BF16)
    pt = sb("pt", [128, 16 * 512], BF16)
    p1buf = pt[:, 0:2 * 512]
    wshare = sb("wshare", [128, 4096], BF16)
    wv = wshare.rearrange("p (a b) -> p a b", a=8)
    wo = wshare.rearrange("p (a b) -> p a b", a=4)
    wqk3 = sb("wqk", [128, 1, 8, 256], BF16)
    id64 = sb("id64", [128, 64])
    osb = sb("osb", [65, 512])
    rro = sb("rro", [128, 512])
    nw = sb("nw_sb", [128, 8])
    brow = sb("brow", [1, 2 * D], BF16)
    bvrow = sb("bvrow", [1, D], BF16)
    borow = sb("borow", [1, D], BF16)
    onesr = sb("onesr", [1, 512], BF16)
    cmask = sb("cmask_sb", [128, 896], BF16)
    bufs["onesc_bf"] = sb("onesc_bf", [128, 1], BF16)
    bufs["sacc"] = pt[:, 15 * 512:16 * 512]
    t1 = bufs["tmp8"][0:64, :]

    P = [ps(f"P{i}", [128, 512]) for i in range(8)]
    bufs["p_m"], bufs["p_sq"], bufs["p_rb"], bufs["p_mrb"] = P[0], P[1], P[2], P[3]
    bufs["ln_junk"] = P[5][0:1, 0:256]
    p_qk = P[4]
    p_scA, p_scB = P[2], P[3]
    p_sums = P[1]
    p_o = P[4]
    p_rb2 = P[6]
    p_sh = P[5]
    att_junk = P[0][0:1, 0:256]
    p_wo = P[7]

    vaug_vh = vaug.rearrange("p (h t c) -> p h t c", h=8, c=65)
    S.op("vector", lambda e: e.memset(vaug_vh[:, :, :, 64:65], 1.0), incs=[(dv, 1)])
    S.op("vector", lambda e: e.memset(kS[64:65, :], 1.0), incs=[(dv, 1)])
    S.op("vector", lambda e: e.memset(bufs["onesc_bf"][:, :], 1.0), incs=[(dv, 1)])
    S.op("vector", lambda e: e.memset(onesr[:, :], 1.0), incs=[(dv, 1)])
    init_dv = dv.v

    S.dma("sync", id64[:, :], idhi[:, :], sem=ld)
    S.dma("sync", nw[:, :], nw_in[:, :], sem=ld)
    S.dma("gpsimd", cmask[:, :], cmask_in[:, :], sem=gq)
    S.dma("gpsimd", brow[:, :], bqk_in[:, :], sem=gq)
    S.dma("gpsimd", bvrow[:, :], bv_in[:, :], sem=gq)
    S.dma("gpsimd", borow[:, :], bo_in[:, :], sem=gq)
    ld_const = ld.v
    gq_const = gq.v

    for s in range(SEQ_PER_CORE):
        t0 = s * T
        x_src = x2T[:, t0:t0 + T].rearrange("(c p) t -> p c t", p=128)
        S.dma("sync", X[:, :, :], x_src, waits=[(st, st.v)], sem=ld)
        ld_x = ld.v

        dv_xn = 0
        for tc in range(4):
            tsl = slice(tc * 512, (tc + 1) * 512)
            w0 = [(ld, ld_x)]
            if s == 0 and tc == 0:
                w0 += [(dv, init_dv), (ld, ld_const), (gq, gq_const)]
            dv_xn = emit_ln_chunk(S, bufs, X, tsl, w0,
                                  (lambda tsl=tsl: lambda k: xn[:, k, tsl])(),
                                  rms=True, nw=nw)

        for half in range(2):
            S.op("gpsimd", lambda e: None, waits=[(pe, pe.v)])
            ld_wv = S.dma("gpsimd", wv[:, :, :],
                          wvT3[:, half * 512:(half + 1) * 512].rearrange("(c p) f -> p c f", p=128),
                          sem=gq)
            for tt in range(16):
                for k in range(8):
                    S.op("tensor", (lambda tt=tt, k=k: lambda e: e.matmul(
                        p_qk[:, :], xn[:, k, tt * 128:(tt + 1) * 128], wv[:, k, :],
                        start=(k == 0), stop=False))(),
                        waits=([(dv, dv_xn), (gq, ld_wv)] if (tt == 0 and k == 0) else []))
                S.op("tensor", (lambda tt=tt, half=half: lambda e: e.matmul(
                    p_qk[:, :], onesr[0:1, 0:128], bvrow[0:1, half * 512:(half + 1) * 512],
                    start=False, stop=True))())
                pe_v = emit_drain(S, bufs, att_junk)
                for hh in range(8):
                    S.op("vector", (lambda tt=tt, hh=hh: lambda e: e.tensor_copy(
                        vaug_vh[:, hh, tt, 0:64], p_qk[:, hh * 64:(hh + 1) * 64]))(),
                        waits=[(pe, pe_v)] if hh == 0 else [],
                        incs=[(dv, 1)] if hh == 7 else [])
                S.op("tensor", lambda e: None, waits=[(dv, dv.v)])

            for gl in range(4):
                g = half * 4 + gl
                gb = 0
                wsrc = wqkT3[:, g * 256:(g + 1) * 256].rearrange("(c p) f -> p c f", p=128)
                S.op("gpsimd", lambda e: None, waits=[(pe, pe.v)])
                ld_w = S.dma("gpsimd", wqk3[:, 0, :, :], wsrc, sem=gq)

                for dst, c0 in ((qS, 0), (kS, 128)):
                    for ncx in range(4):
                        for k in range(8):
                            S.op("tensor", (lambda ncx=ncx, k=k, c0=c0: lambda e: e.matmul(
                                p_qk[:, :], wqk3[:, 0, k, c0:c0 + 128],
                                xn[:, k, ncx * 512:(ncx + 1) * 512],
                                start=(k == 0), stop=False))(),
                                waits=[(gq, ld_w)] if (k == 0 and ncx == 0 and c0 == 0) else [])
                        S.op("tensor", (lambda g=g, ncx=ncx, c0=c0: lambda e: e.matmul(
                            p_qk[:, :], brow[0:1, g * 256 + c0:g * 256 + c0 + 128],
                            onesr[0:1, :],
                            start=False, stop=True))())
                        pe_qk = emit_drain(S, bufs, att_junk)
                        S.op("vector", (lambda dst=dst, ncx=ncx: lambda e: e.tensor_copy(
                            dst[0:64, ncx * 512:(ncx + 1) * 512], p_qk[0:64, :]))(),
                            waits=[(pe, pe_qk)])
                        S.op("vector", (lambda ncx=ncx: lambda e: e.tensor_copy(
                            bufs["tmp8"][64:128, :], p_qk[64:128, :]))(),
                            incs=[(dv, 1)])
                        S.op("tensor", lambda e: e.matmul(
                            p_sh[0:64, :], id64[64:128, :], bufs["tmp8"][64:128, :],
                            start=True, stop=True),
                            waits=[(dv, dv.v)])
                        pe_sh2 = emit_drain(S, bufs, att_junk)
                        S.op("vector", (lambda dst=dst, ncx=ncx: lambda e: e.tensor_copy(
                            dst[0:64, T + ncx * 512:T + (ncx + 1) * 512], p_sh[0:64, :]))(),
                            waits=[(pe, pe_sh2)], incs=[(dv, 1)])
                        S.op("tensor", lambda e: None, waits=[(dv, dv.v)])

                for sub in range(2):
                    hl = 2 * gl + sub
                    qSh = qS[:, sub * T:(sub + 1) * T]
                    kSh = kS[:, sub * T:(sub + 1) * T]
                    c = hl // 2
                    for qc in range(4):
                        qsl = slice(qc * 512, (qc + 1) * 512)
                        kt_hi = 4 * qc + 4
                        mt = {}
                        for j in range(4):
                            kt = 4 * qc + j
                            cc = kt * 128 - qc * 512
                            mt[kt] = cmask[:, 384 - cc:384 - cc + 512]

                        def X_add(t1v, p_shv, id64v, c=c, qsl=qsl, sub=sub):
                            if sub == 0:
                                S.op("vector", (lambda: lambda e: e.tensor_copy(
                                    Ob[0:64, c, qsl], t1v[:, 0:512]))(),
                                    incs=[(dv, 1)])
                            else:
                                S.op("tensor", (lambda: lambda e: e.matmul(
                                    p_shv[64:128, :], id64v[0:64, :],
                                    t1v[:, 0:512], start=True, stop=True))(),
                                    waits=[(dv, dv.v)])
                                pe_sh = emit_drain(S, bufs, att_junk)
                                S.op("vector", (lambda: lambda e: e.tensor_copy(
                                    Ob[64:128, c, qsl], p_shv[64:128, :]))(),
                                    waits=[(pe, pe_sh)], incs=[(dv, 1)])

                        emit_attn_head(S, bufs, qS=qSh, kS=kSh,
                                       vaug_v=vaug_vh[:, hl, :, :],
                                       X_add=X_add, h=hl, qsl=qsl, kt_lo=0, kt_hi=kt_hi,
                                       pt=pt, p1buf=p1buf, p_scA=p_scA, p_scB=p_scB,
                                       p_sums=p_sums, p_o=p_o, p_rb2=p_rb2, p_sh=p_sh,
                                       osb=osb, rro=rro, t1=t1, id64=id64,
                                       att_junk=att_junk, mask_tiles=mt)

            S.op("gpsimd", lambda e: None, waits=[(pe, pe.v)])
            ld_wo = S.dma("gpsimd", wo[:, :, :],
                          woT[half * 512:(half + 1) * 512, :].rearrange("(c p) f -> p c f", p=128),
                          sem=gq)
            dv_ob = dv.v
            for qc in range(4):
                qsl = slice(qc * 512, (qc + 1) * 512)
                for o in range(8):
                    for k in range(4):
                        S.op("tensor", (lambda o=o, k=k, qsl=qsl: lambda e: e.matmul(
                            p_wo[:, :], wo[:, k, o * 128:(o + 1) * 128], Ob[:, k, qsl],
                            start=(k == 0), stop=(half == 1 or k < 3)))(),
                            waits=[(gq, ld_wo), (dv, dv_ob)] if (qc == 0 and o == 0 and k == 0) else [])
                    if half == 0:
                        S.op("tensor", (lambda o=o: lambda e: e.matmul(
                            p_wo[:, :], borow[0:1, o * 128:(o + 1) * 128], onesr[0:1, :],
                            start=False, stop=True))())
                    pe_wo = emit_drain(S, bufs, att_junk)
                    if half == 0:
                        S.op("vector", (lambda o=o, qsl=qsl: lambda e: e.tensor_add(
                            bufs["tmp8"][:, :], p_wo[:, :], xn[:, o, qsl]))(),
                            waits=[(pe, pe_wo)])
                    else:
                        S.op("vector", (lambda o=o, qsl=qsl: lambda e: e.tensor_copy(
                            bufs["tmp8"][:, :], p_wo[:, :]))(),
                            waits=[(pe, pe_wo)])
                    S.op("vector", (lambda o=o, qsl=qsl: lambda e: e.tensor_add(
                        X[:, o, qsl], X[:, o, qsl], bufs["tmp8"][:, :]))(),
                        incs=[(dv, 1)])
                    S.op("tensor", lambda e: None, waits=[(dv, dv.v)])

        x_dst = x3T[:, t0:t0 + T].rearrange("(c p) t -> p c t", p=128)
        S.dma("sync", x_dst, X[:, :, :], waits=[(dv, dv.v)], sem=st)

    S.op("sync", lambda e: None, waits=[(st, st.v)])
    S.build(nc, ctx)
    ctx.close()
    return nc



def build_stage4(serialize=False, chained=False, ser_set=None):
    nc = bass.Bass("TRN2", target_bir_lowering=False, debug=False)
    x3T = nc.dram_tensor("x3T", [D, TOK], F32, kind="ExternalInput").ap()
    gwT = nc.dram_tensor("gwT", [D, E], F32, kind="ExternalInput").ap()
    gbrow = nc.dram_tensor("gbrow", [1, E], F32, kind="ExternalInput").ap()
    ic64_in = nc.dram_tensor("ic64", [128, E], F32, kind="ExternalInput").ap()
    ut128_in = nc.dram_tensor("ut128", [128, 128], F32, kind="ExternalInput").ap()
    idb_in = nc.dram_tensor("idb", [128, 128], BF16, kind="ExternalInput").ap()
    xfT = nc.dram_tensor("xfT", [D, TOK], BF16, kind="ExternalOutput").ap()
    xfr = nc.dram_tensor("xfr", [TOK, D], BF16, kind="ExternalOutput").ap()
    cnt = nc.dram_tensor("cnt", [1, E], F32, kind="ExternalOutput").ap()
    rinfo = nc.dram_tensor("rinfo", [128, 32, 18], F32, kind="ExternalOutput").ap()

    S = Sched(serialize=serialize, chained=chained, ser_set=ser_set)
    ctx = ExitStack()
    bufs, sb, ps = alloc_common(S, nc, ctx)
    pe, dv, sc = bufs["pe"], bufs["dv"], bufs["sc"]
    ld, gq, st = bufs["ld"], bufs["gq"], bufs["st"]

    X = sb("X", [128, 8, T])
    xf = sb("xf", [128, 8, T], BF16)
    gw = sb("gw", [128, 8, E], BF16)
    gbb = sb("gbb", [128, E])
    ic64 = sb("ic64_sb", [128, E])
    ut128 = sb("ut128_sb", [128, 128])
    idb = sb("idb_sb", [128, 128], BF16)
    bufs["tmp8b"] = sb("tmp8b", [128, 1024], BF16)
    sco = sb("sco", [128, E])
    sp = sb("sp", [128, E])
    oh = sb("oh", [128, E])
    cmat = sb("cmat", [128, E])
    pplus = sb("pplus", [128, E])
    top8 = sb("top8", [128, 8])
    run = sb("run", [1, E])
    info = sb("info", [128, 32, 18])
    junkrow = sb("junkrow", [128, E])
    rowmax = sb("rowmax", [128, 2])
    jnkw = sb("jnkw", [128, 256])

    def _spw(inc=False):
        S.op("vector", lambda e: e.tensor_copy(jnkw[:, 0:128], jnkw[:, 128:256]),
             incs=[(dv, 1)] if inc else [])

    P = [ps(f"P{i}", [128, 512]) for i in range(7)]
    bufs["p_m"], bufs["p_sq"], bufs["p_rb"], bufs["p_mrb"] = P[0], P[1], P[2], P[3]
    bufs["ln_junk"] = P[5][0:1, 0:256]
    p_g = P[4]
    p_off = P[5]
    p_tot = P[6]
    att_junk = P[0][0:1, 0:256]
    S.ser_junk = P[6][0:1, 256:512]
    S.ser_ones1 = bufs["ones1"][0:1, 0:1]
    S.ser_onest = bufs["onest"][0:1, 0:256]
    p_trb = ctx.enter_context(nc.psum_tensor("p_trb", [128, 1024], BF16))

    S.op("vector", lambda e: e.memset(run[:, :], 0.0), incs=[(dv, 1)])
    init_dv0 = dv.v

    S.dma("gpsimd", gw[:, :, :], gwT.rearrange("(c p) f -> p c f", p=128), sem=gq)
    S.dma("sync", ic64[:, :], ic64_in[:, :], sem=ld)
    S.dma("sync", ut128[:, :], ut128_in[:, :], sem=ld)
    S.dma("sync", idb[:, :], idb_in[:, :], sem=ld)
    ld_const = ld.v
    gq_const = gq.v
    gbr = sb("gbr", [1, E])
    S.dma("sync", gbr[:, :], gbrow[:, :], sem=ld)
    ld_gb = ld.v
    S.op("tensor", lambda e: e.matmul(p_g[:, 0:E], bufs["ones128"][:, :], gbr[:, :],
                                      start=True, stop=True),
         waits=[(ld, ld_gb), (dv, init_dv0)])
    pe_gb = emit_drain(S, bufs, att_junk)
    S.op("vector", lambda e: e.tensor_copy(gbb[:, :], p_g[:, 0:E]),
         waits=[(pe, pe_gb)], incs=[(dv, 1)])
    init_dv = dv.v

    for s in range(SEQ_PER_CORE):
        t0 = s * T
        x_src = x3T[:, t0:t0 + T].rearrange("(c p) t -> p c t", p=128)
        S.dma("sync", X[:, :, :], x_src, waits=[(st, st.v)], sem=ld)
        ld_x = ld.v

        dv_xf = emit_layernorm(S, nc, bufs, X, xf, T,
                               [(ld, ld_x)] + ([(dv, init_dv), (ld, ld_const)] if s == 0 else []))
        S.dma("sync", xfT[:, t0:t0 + T].rearrange("(c p) t -> p c t", p=128),
              xf[:, :, :], waits=[(dv, dv_xf)], sem=st)

        for tt in range(16):
            gt = s * 16 + tt
            for k in range(8):
                S.op("tensor", (lambda k=k, tt=tt: lambda e: e.transpose(
                    p_trb[:, k * 128:(k + 1) * 128], xf[:, k, tt * 128:(tt + 1) * 128],
                    idb[:, :]))(),
                    waits=[(dv, dv_xf)] if (k == 0) else [])
            pe_tr = emit_drain(S, bufs, att_junk)
            S.op("vector", lambda e: e.tensor_copy(bufs["tmp8b"][:, :], p_trb[:, :]),
                 waits=[(pe, pe_tr), (st, st.v)], incs=[(dv, 1)])
            S.dma("sync", xfr[t0 + tt * 128:t0 + (tt + 1) * 128, :],
                  bufs["tmp8b"][:, :], waits=[(dv, dv.v)], sem=st)
            dv_tr = dv.v

            for k in range(8):
                S.op("tensor", (lambda k=k, tt=tt: lambda e: e.matmul(
                    p_g[:, 0:E], xf[:, k, tt * 128:(tt + 1) * 128], gw[:, k, :],
                    start=(k == 0), stop=(k == 7)))(),
                    waits=([(dv, dv.v)] + ([(gq, gq_const)] if (s == 0 and tt == 0) else [])) if k == 0 else [])
            pe_g = emit_drain(S, bufs, att_junk)
            S.op("vector", lambda e: e.tensor_copy(junkrow[:, 0:8], p_g[:, 0:8]),
                 waits=[(pe, pe_g)])
            S.op("vector", lambda e: e.tensor_reduce(rowmax[:, 0:1], p_g[:, 0:E],
                                                     AX.X, ALU.max))
            _spw(); _spw()
            S.op("vector", lambda e: e.tensor_single_scalar(rowmax[:, 1:2], rowmax[:, 0:1],
                                                            -1.0, ALU.mult),
                 incs=[(dv, 1)])
            _spw(inc=True)
            _spw(inc=True)
            S.op("scalar", lambda e: e.activation(out=sco[:, :], in_=p_g[:, 0:E],
                                                  func=AF.Exp, bias=rowmax[:, 1:2],
                                                  accum_out=rowmax[:, 0:1]),
                 waits=[(dv, dv.v)])
            S.op("scalar", lambda e: e.activation(out=jnkw[:, 0:64], in_=jnkw[:, 64:128],
                                                  func=AF.Identity))
            S.op("scalar", lambda e: e.activation(out=jnkw[:, 0:64], in_=jnkw[:, 64:128],
                                                  func=AF.Identity), incs=[(sc, 1)])
            S.op("vector", lambda e: e.reciprocal(rowmax[:, 1:2], rowmax[:, 0:1]),
                 waits=[(sc, sc.v)])
            _spw(); _spw()
            S.op("vector", lambda e: e.tensor_scalar(sco[:, :], sco[:, :],
                                                     rowmax[:, 1:2], None, ALU.mult))
            _spw(); _spw()
            S.op("vector", lambda e: e.tensor_add(sp[:, :], sco[:, :], gbb[:, :]),
                 incs=[(dv, 1)])
            _spw(); _spw()
            S.op("vector", lambda e: e.max(out=top8[:, :], in_=sp[:, :]))
            _spw(); _spw(); _spw()
            for k6 in range(TOPK):
                S.op("vector", (lambda k6=k6: lambda e: e.tensor_scalar(
                    oh[:, :], sp[:, :], top8[:, k6:k6 + 1], None, ALU.is_equal))())
                _spw(); _spw()
                S.op("vector", (lambda k6=k6, gt=gt: lambda e: e.scalar_tensor_tensor(
                    junkrow[:, :], sco[:, :], 1.0, oh[:, :], ALU.mult, ALU.mult,
                    accum_out=info[:, gt, 6 + k6:7 + k6]))())
                S.op("vector", (lambda k6=k6, gt=gt: lambda e: e.scalar_tensor_tensor(
                    junkrow[:, :], ic64[:, :], 1.0, oh[:, :], ALU.mult, ALU.mult,
                    accum_out=info[:, gt, k6:k6 + 1]))())
                if k6 == 0:
                    S.op("vector", lambda e: e.tensor_copy(cmat[:, :], oh[:, :]))
                else:
                    S.op("vector", lambda e: e.tensor_add(cmat[:, :], cmat[:, :], oh[:, :]),
                         incs=[(dv, 1)] if k6 == TOPK - 1 else [])
            dv_c = dv.v
            S.op("tensor", lambda e: e.matmul(p_g[:, 0:E], ut128[:, :], cmat[:, :],
                                              start=True, stop=True),
                 waits=[(dv, dv_c)])
            S.op("tensor", lambda e: e.matmul(p_tot[0:1, 0:E], bufs["ones1"][:, :],
                                              cmat[:, :], start=True, stop=True))
            S.op("tensor", lambda e: e.matmul(p_off[:, 0:E], bufs["ones128"][:, :],
                                              run[:, :], start=True, stop=True))
            pe_pr = emit_drain(S, bufs, att_junk)
            S.op("vector", lambda e: e.tensor_copy(pplus[:, :], p_g[:, 0:E]),
                 waits=[(pe, pe_pr)])
            _spw(); _spw()
            S.op("vector", lambda e: e.tensor_add(pplus[:, :], pplus[:, :], p_off[:, 0:E]))
            S.op("vector", lambda e: e.tensor_add(run[:, :], run[:, :], p_tot[0:1, 0:E]),
                 incs=[(dv, 1)])
            _spw(); _spw()
            for k6 in range(TOPK):
                S.op("vector", (lambda k6=k6: lambda e: e.tensor_scalar(
                    oh[:, :], sp[:, :], top8[:, k6:k6 + 1], None, ALU.is_equal))())
                _spw(); _spw()
                S.op("vector", (lambda k6=k6, gt=gt: lambda e: e.scalar_tensor_tensor(
                    junkrow[:, :], pplus[:, :], 1.0, oh[:, :], ALU.mult, ALU.mult,
                    accum_out=info[:, gt, 12 + k6:13 + k6]))(),
                    incs=[(dv, 1)] if k6 == TOPK - 1 else [])
            S.op("tensor", lambda e: None, waits=[(dv, dv.v)])

    _spw(); _spw(); _spw(inc=True)
    S.dma("sync", cnt[:, :], run[:, :], waits=[(bufs["dv"], bufs["dv"].v)], sem=st)
    S.dma("sync", rinfo[:, :, :], info[:, :, :], sem=st)
    S.op("sync", lambda e: None, waits=[(st, st.v)])
    S.build(nc, ctx)
    ctx.close()
    return nc



def build_stage4b(serialize=False, chained=False):
    nc = bass.Bass("TRN2", target_bir_lowering=False, debug=False)
    cnts = nc.dram_tensor("cnts", [8, E], F32, kind="ExternalInput").ap()
    rinfo = nc.dram_tensor("rinfo", [128, 32, 18], F32, kind="ExternalInput").ap()
    csel = nc.dram_tensor("csel", [8, 1], F32, kind="ExternalInput").ap()
    i512_in = nc.dram_tensor("i512", [128, 512], F32, kind="ExternalInput").ap()
    tbase_in = nc.dram_tensor("tbase", [128, 1], F32, kind="ExternalInput").ap()
    ic64_in = nc.dram_tensor("ic64", [128, E], F32, kind="ExternalInput").ap()
    mpart = nc.dram_tensor("mpart", [E, 512], F32, kind="ExternalOutput").ap()
    sinfo = nc.dram_tensor("sinfo", [128, 32, 12], F32, kind="ExternalOutput").ap()

    S = Sched(serialize=serialize, chained=chained)
    ctx = ExitStack()
    bufs, sb, ps = alloc_common(S, nc, ctx)
    pe, dv, sc = bufs["pe"], bufs["dv"], bufs["sc"]
    ld, gq, st = bufs["ld"], bufs["gq"], bufs["st"]

    cntb = sb("cntb", [8, E])
    cselb = sb("cselb", [8, 1])
    i512 = sb("i512_sb", [128, 512])
    tbase = sb("tbase_sb", [128, 1])
    ic64 = sb("ic64_sb", [128, E])
    info = sb("info", [128, 32, 18])
    sout = sb("sout", [128, 32, 12])
    offb = sb("offb", [128, E])
    oh = sb("oh", [128, E])
    ohp = sb("ohp", [128, 2, 512])
    oht = sb("oht", [128, 2, E])
    tg = sb("tg", [128, 1])
    posg = sb("posg", [128, 1])
    lt = sb("lt", [128, 1])
    junkrow = sb("junkrow", [128, E])
    msb = sb("msb", [64, 512])
    jnkw = sb("jnkw", [128, 256])

    def _spw(inc=False):
        S.op("vector", lambda e: e.tensor_copy(jnkw[:, 0:128], jnkw[:, 128:256]),
             incs=[(dv, 1)] if inc else [])

    P = [ps(f"P{i}", [128, 512]) for i in range(4)]
    p_off = P[0]
    p_m = P[1]
    att_junk = P[2][0:1, 0:256]
    S.ser_junk = P[3][0:1, 0:256]
    S.ser_ones1 = bufs["ones1"][0:1, 0:1]
    S.ser_onest = bufs["onest"][0:1, 0:256]

    S.dma("sync", cntb[:, :], cnts[:, :], sem=ld)
    S.dma("sync", cselb[:, :], csel[:, :], sem=ld)
    S.dma("sync", i512[:, :], i512_in[:, :], sem=ld)
    S.dma("sync", tbase[:, :], tbase_in[:, :], sem=ld)
    S.dma("sync", ic64[:, :], ic64_in[:, :], sem=ld)
    S.dma("sync", info[:, :, :], rinfo[:, :, :], sem=ld)
    ld_c = ld.v

    S.op("tensor", lambda e: e.matmul(p_off[0:1, 0:E], cselb[:, :], cntb[:, :],
                                      start=True, stop=True),
         waits=[(ld, ld_c)])
    pe1 = emit_drain(S, bufs, att_junk)
    S.op("vector", lambda e: e.tensor_copy(bufs["tmp8"][0:1, 0:E], p_off[0:1, 0:E]),
         waits=[(pe, pe1)], incs=[(dv, 1)])
    S.op("tensor", lambda e: e.matmul(p_off[:, 0:E], bufs["ones128"][:, :],
                                      bufs["tmp8"][0:1, 0:E], start=True, stop=True),
         waits=[(dv, dv.v)])
    pe2 = emit_drain(S, bufs, att_junk)
    S.op("vector", lambda e: e.tensor_copy(offb[:, :], p_off[:, 0:E]),
         waits=[(pe, pe2)], incs=[(dv, 1)])
    dv_off = dv.v

    first = True
    pecnt = []
    j = 0
    for gt in range(32):
        S.op("vector", (lambda gt=gt: lambda e: e.tensor_single_scalar(
            tg[:, :], tbase[:, :], float(gt * 128), ALU.add))(),
            waits=[(dv, dv_off)] if first else [])
        _spw(); _spw()
        for k6 in range(TOPK):
            jb = j % 2
            S.op("vector", (lambda gt=gt, k6=k6: lambda e: e.tensor_scalar(
                oh[:, :], ic64[:, :], info[:, gt, k6:k6 + 1], None, ALU.is_equal))())
            _spw(); _spw()
            S.op("vector", (lambda: lambda e: e.scalar_tensor_tensor(
                junkrow[:, :], offb[:, :], 1.0, oh[:, :], ALU.mult, ALU.mult,
                accum_out=posg[:, :]))())
            _spw(); _spw()
            S.op("vector", (lambda gt=gt, k6=k6: lambda e: e.tensor_scalar(
                posg[:, :], posg[:, :], info[:, gt, 12 + k6:13 + k6], None, ALU.add))())
            _spw(); _spw()
            S.op("vector", lambda e: e.tensor_single_scalar(lt[:, :], posg[:, :], 512.0, ALU.is_lt))
            _spw(); _spw()
            S.op("vector", (lambda gt=gt, k6=k6: lambda e: e.tensor_scalar(
                sout[:, gt, k6:k6 + 1], lt[:, :], info[:, gt, 6 + k6:7 + k6], None, ALU.mult))())
            S.op("vector", (lambda gt=gt, k6=k6: lambda e: e.tensor_single_scalar(
                bufs["tmp8"][:, 0:1], info[:, gt, k6:k6 + 1], 512.0, ALU.mult))())
            _spw(); _spw()
            S.op("vector", lambda e: e.tensor_add(bufs["tmp8"][:, 0:1], bufs["tmp8"][:, 0:1], posg[:, :]))
            S.op("vector", lambda e: e.tensor_single_scalar(lt[:, :], lt[:, :], -1.0, ALU.add))
            _spw(); _spw()
            S.op("vector", lambda e: e.tensor_single_scalar(lt[:, :], lt[:, :], -1e7, ALU.mult))
            _spw(); _spw()
            S.op("vector", (lambda gt=gt, k6=k6: lambda e: e.tensor_add(
                sout[:, gt, 6 + k6:7 + k6], bufs["tmp8"][:, 0:1], lt[:, :]))())
            war = [(pe, pecnt[j - 2])] if j >= 2 else []
            S.op("vector", (lambda jb=jb: lambda e: e.tensor_scalar(
                oht[:, jb, :], oh[:, :], tg[:, :], None, ALU.mult))(), waits=war)
            S.op("vector", (lambda jb=jb: lambda e: e.tensor_scalar(
                ohp[:, jb, :], i512[:, :], posg[:, :], None, ALU.is_equal))(),
                incs=[(dv, 1)])
            S.op("tensor", (lambda gt=gt, k6=k6, jb=jb: lambda e: e.matmul(
                p_m[0:64, :], oht[:, jb, :], ohp[:, jb, :],
                start=(gt == 0 and k6 == 0), stop=(gt == 31 and k6 == TOPK - 1)))(),
                waits=[(dv, dv.v)], incs=[(pe, 1)])
            pecnt.append(pe.v)
            j += 1
        first = False
    pe_m = emit_drain(S, bufs, att_junk)
    S.op("vector", lambda e: e.tensor_copy(msb[:, :], p_m[0:64, :]),
         waits=[(pe, pe_m)], incs=[(dv, 1)])
    _spw(); _spw(inc=True)
    S.dma("sync", mpart[:, :], msb[:, :], waits=[(dv, dv.v)], sem=st)
    S.dma("sync", sinfo[:, :, :], sout[:, :, :], waits=[(dv, dv.v)], sem=st)
    S.op("sync", lambda e: None, waits=[(st, st.v)])
    S.build(nc, ctx)
    ctx.close()
    return nc


I_PAD = 768


def build_stage5(serialize=False, chained=False):
    nc = bass.Bass("TRN2", target_bir_lowering=False, debug=False)
    mloc = nc.dram_tensor("mloc", [8, 512, 8], F32, kind="ExternalInput").ap()
    xf_full = nc.dram_tensor("xf_full", [B * T, D], BF16, kind="ExternalInput").ap()
    w1T = nc.dram_tensor("w1T", [8, D, I_PAD], F32, kind="ExternalInput").ap()
    w3T = nc.dram_tensor("w3T", [8, D, I_PAD], F32, kind="ExternalInput").ap()
    w2T = nc.dram_tensor("w2T", [8, I_PAD, D], F32, kind="ExternalInput").ap()
    idb_in = nc.dram_tensor("idb", [128, 128], BF16, kind="ExternalInput").ap()
    yb = nc.dram_tensor("yb", [8 * 512, D], BF16, kind="ExternalOutput").ap()

    S = Sched(serialize=serialize, chained=chained)
    ctx = ExitStack()
    bufs, sb, ps = alloc_common(S, nc, ctx)
    pe, dv, sc = bufs["pe"], bufs["dv"], bufs["sc"]
    ld, gq, st = bufs["ld"], bufs["gq"], bufs["st"]

    NI = I_PAD // 128
    marr = sb("marr", [128, 4, 8])
    midx = sb("midx", [128, 4], I32)
    mfl = sb("mfl", [128, 4])
    xb = sb("xb", [128, 4, D], BF16)
    xbT = sb("xbT", [128, 8, 512], BF16)
    w1b = sb("w1b", [128, 2, 8, I_PAD], BF16)
    w3b = sb("w3b", [128, 2, 8, I_PAD], BF16)
    w2b = sb("w2b", [128, 2, NI, D], BF16)
    us = sb("us", [128, NI, 512], BF16)
    vs = sb("vs", [128, NI, 512], BF16)
    idb = sb("idb_sb", [128, 128], BF16)
    ybt = sb("ybt", [128, 2, 512], BF16)
    jnk5 = sb("jnk5", [128, 256])

    P = [ps(f"P{i}", [128, 512]) for i in range(6)]
    p_u = P[0]
    p_v = P[1]
    p_y = P[2]
    att_junk = P[3][0:1, 0:256]
    p_trb = ctx.enter_context(nc.psum_tensor("p_trb", [128, 1024], BF16))

    S.dma("sync", idb[:, :], idb_in[:, :], sem=ld)
    ld_id = ld.v

    prev_gather = 0
    for e_i in range(8):
        eb = e_i % 2
        if e_i >= 2:
            S.op("gpsimd", lambda e: None, waits=[(pe, pe.v)])
        l1 = S.dma("gpsimd", w1b[:, eb, :, :],
                   w1T[e_i, :, :].rearrange("(c p) f -> p c f", p=128), sem=gq)
        l3 = S.dma("gpsimd", w3b[:, eb, :, :],
                   w3T[e_i, :, :].rearrange("(c p) f -> p c f", p=128), sem=gq)
        l2 = S.dma("gpsimd", w2b[:, eb, :, :],
                   w2T[e_i, :, :].rearrange("(c p) f -> p c f", p=128), sem=gq)

        S.dma("sync", marr[:, :, :],
              mloc[e_i, :, :].rearrange("(c p) k -> p c k", p=128),
              waits=[(dv, dv.v)], sem=ld)
        def _sp():
            S.op("vector", lambda e: e.tensor_copy(jnk5[:, 0:128], jnk5[:, 128:256]))
        S.op("vector", lambda e: e.tensor_reduce(mfl[:, :], marr[:, :, :], AX.X, ALU.add),
             waits=[(ld, ld.v)])
        _sp(); _sp()
        S.op("vector", lambda e: e.tensor_single_scalar(mfl[:, :], mfl[:, :], -1.0, ALU.add))
        _sp(); _sp()
        S.op("vector", lambda e: e.tensor_scalar_max(mfl[:, :], mfl[:, :], 0.0))
        _sp(); _sp()
        S.op("vector", lambda e: e.tensor_copy(midx[:, :], mfl[:, :]),
             waits=[(gq, prev_gather)] if prev_gather else [])
        _sp(); _sp()
        S.op("vector", lambda e: e.tensor_copy(jnk5[:, 0:128], jnk5[:, 128:256]),
             incs=[(dv, 1)])
        dv_idx = dv.v
        for c in range(4):
            S.op("gpsimd", (lambda c=c: lambda e: e.indirect_dma_start(
                out=xb[:, c, :], out_offset=None,
                in_=xf_full[:, :], in_offset=IndirectOffsetOnAxis(midx[:, c:c + 1], 0),
                bounds_check=B * T - 1, oob_is_err=False))(),
                waits=[(dv, dv_idx)] if c == 0 else [], incs=[(gq, 16)])
        gq_gather = gq.v
        prev_gather = gq_gather
        for c in range(4):
            for k in range(8):
                S.op("tensor", (lambda c=c, k=k: lambda e: e.transpose(
                    p_trb[:, k * 128:(k + 1) * 128], xb[:, c, k * 128:(k + 1) * 128],
                    idb[:, :]))(),
                    waits=[(gq, gq_gather), (ld, ld_id)] if (c == 0 and k == 0) else [])
            pe_tr = emit_drain(S, bufs, att_junk)
            S.op("vector", (lambda c=c: lambda e: e.tensor_copy(
                xbT[:, :, c * 128:(c + 1) * 128],
                p_trb[:, :].rearrange("p (a b) -> p a b", a=8)))(),
                waits=[(pe, pe_tr)], incs=[(dv, 1)])
            S.op("tensor", lambda e: None, waits=[(dv, dv.v)])
        dv_xbT = dv.v

        for i in range(NI):
            for k in range(8):
                S.op("tensor", (lambda i=i, k=k, eb=eb: lambda e: e.matmul(
                    p_u[:, :], w1b[:, eb, k, i * 128:(i + 1) * 128], xbT[:, k, :],
                    start=(k == 0), stop=(k == 7)))(),
                    waits=[(dv, dv_xbT), (gq, l3)] if (i == 0 and k == 0) else [])
                S.op("tensor", (lambda i=i, k=k, eb=eb: lambda e: e.matmul(
                    p_v[:, :], w3b[:, eb, k, i * 128:(i + 1) * 128], xbT[:, k, :],
                    start=(k == 0), stop=(k == 7)))())
            pe_uv = emit_drain(S, bufs, att_junk)
            S.op("scalar", (lambda i=i: lambda e: e.activation(
                out=us[:, i, :], in_=p_u[:, :], func=AF.Silu))(),
                waits=[(pe, pe_uv)], incs=[(sc, 1)])
            S.op("vector", (lambda i=i: lambda e: e.tensor_copy(vs[:, i, :], p_v[:, :]))(),
                 waits=[(pe, pe_uv)])
            S.op("vector", (lambda i=i: lambda e: e.tensor_mul(
                us[:, i, :], us[:, i, :], vs[:, i, :]))(),
                waits=[(sc, sc.v)], incs=[(dv, 1)])
            S.op("tensor", lambda e: None, waits=[(dv, dv.v)])
        dv_h = dv.v

        for c in range(4):
            for dh in range(2):
                for i in range(NI):
                    S.op("tensor", (lambda c=c, dh=dh, i=i, eb=eb: lambda e: e.matmul(
                        p_y[:, :], us[:, i, c * 128:(c + 1) * 128],
                        w2b[:, eb, i, dh * 512:(dh + 1) * 512],
                        start=(i == 0), stop=(i == NI - 1)))(),
                        waits=[(dv, dv_h), (gq, l2)] if (c == 0 and dh == 0 and i == 0) else [])
                pe_y = emit_drain(S, bufs, att_junk)
                S.op("vector", (lambda dh=dh: lambda e: e.tensor_copy(
                    ybt[:, dh, :], p_y[:, :]))(),
                    waits=[(pe, pe_y), (st, st.v)], incs=[(dv, 1)])
                S.op("tensor", lambda e: None, waits=[(dv, dv.v)])
            S.dma("sync", yb[e_i * 512 + c * 128:e_i * 512 + (c + 1) * 128, :],
                  ybt[:, :, :].rearrange("p a b -> p (a b)"), waits=[(dv, dv.v)], sem=st)

    S.op("sync", lambda e: None, waits=[(st, st.v)])
    S.build(nc, ctx)
    ctx.close()
    return nc



def build_stage6(serialize=False, chained=False):
    nc = bass.Bass("TRN2", target_bir_lowering=False, debug=False)
    x3T = nc.dram_tensor("x3T", [D, TOK], F32, kind="ExternalInput").ap()
    xfT_in = nc.dram_tensor("xfT", [D, TOK], BF16, kind="ExternalInput").ap()
    yb_full = nc.dram_tensor("yb_full", [B * T, D], BF16, kind="ExternalInput").ap()
    sinfo_in = nc.dram_tensor("sinfo", [128, 32, 12], F32, kind="ExternalInput").ap()
    ws1T = nc.dram_tensor("ws1T", [D, SI], F32, kind="ExternalInput").ap()
    ws3T = nc.dram_tensor("ws3T", [D, SI], F32, kind="ExternalInput").ap()
    ws2T = nc.dram_tensor("ws2T", [SI, D], F32, kind="ExternalInput").ap()
    idf_in = nc.dram_tensor("idf", [128, 128], F32, kind="ExternalInput").ap()
    outr = nc.dram_tensor("outr", [TOK, D], F32, kind="ExternalOutput").ap()

    S = Sched(serialize=serialize, chained=chained)
    ctx = ExitStack()
    bufs, sb, ps = alloc_common(S, nc, ctx)
    pe, dv, sc = bufs["pe"], bufs["dv"], bufs["sc"]
    ld, gq, st = bufs["ld"], bufs["gq"], bufs["st"]

    NS = SI // 128
    X = sb("X", [128, 8, T])
    xf = sb("xf", [128, 8, T], BF16)
    w1b = sb("w1b", [128, 8, SI], BF16)
    w3b = sb("w3b", [128, 8, SI], BF16)
    w2b = sb("w2b", [128, NS, D], BF16)
    us = sb("us", [128, NS, 512], BF16)
    sinfo = sb("sinfo_sb", [128, 32, 12])
    sidx = sb("sidx", [128, 1], I32)
    gbuf = sb("gbuf", [128, 1, D], BF16)
    acc = sb("acc", [128, D])
    idf = sb("idf_sb", [128, 128])
    jnk = sb("jnk", [128, 256])

    P = [ps(f"P{i}", [128, 512]) for i in range(6)]
    p_u = P[0]
    p_v = P[1]
    p_z = P[2]
    p_x = P[3]
    att_junk = P[4][0:1, 0:256]

    S.dma("sync", sinfo[:, :, :], sinfo_in[:, :, :], sem=ld)
    S.dma("sync", idf[:, :], idf_in[:, :], sem=ld)
    l1 = S.dma("gpsimd", w1b[:, :, :], ws1T.rearrange("(c p) f -> p c f", p=128), sem=gq)
    l3 = S.dma("gpsimd", w3b[:, :, :], ws3T.rearrange("(c p) f -> p c f", p=128), sem=gq)
    l2 = S.dma("gpsimd", w2b[:, :, :], ws2T.rearrange("(c p) f -> p c f", p=128), sem=gq)
    ld_c = ld.v
    S.op("vector", lambda e: e.memset(gbuf[:, 0, :], 0.0), incs=[(dv, 1)])
    init_dv = dv.v
    _bh = {}

    def _breg(e):
        if "r" not in _bh:
            _bh["r"] = e.alloc_register("bnd")
            e.reg_mov(_bh["r"], B * T - 1)
        return _bh["r"]


    for s in range(SEQ_PER_CORE):
        t0 = s * T
        S.dma("sync", X[:, :, :], x3T[:, t0:t0 + T].rearrange("(c p) t -> p c t", p=128),
              waits=[(st, st.v)], sem=ld)
        S.dma("sync", xf[:, :, :], xfT_in[:, t0:t0 + T].rearrange("(c p) t -> p c t", p=128),
              sem=ld)
        ld_x = ld.v

        for tc in range(4):
            tsl = slice(tc * 512, (tc + 1) * 512)
            for i in range(NS):
                for k in range(8):
                    S.op("tensor", (lambda i=i, k=k, tsl=tsl: lambda e: e.matmul(
                        p_u[:, :], w1b[:, k, i * 128:(i + 1) * 128], xf[:, k, tsl],
                        start=(k == 0), stop=(k == 7)))(),
                        waits=[(ld, ld_x), (gq, l3), (dv, init_dv)] if (i == 0 and k == 0 and tc == 0 and s == 0) else
                        ([(ld, ld_x), (dv, dv.v)] if (i == 0 and k == 0) else []))
                    S.op("tensor", (lambda i=i, k=k, tsl=tsl: lambda e: e.matmul(
                        p_v[:, :], w3b[:, k, i * 128:(i + 1) * 128], xf[:, k, tsl],
                        start=(k == 0), stop=(k == 7)))())
                pe_uv = emit_drain(S, bufs, att_junk)
                S.op("scalar", (lambda i=i: lambda e: e.activation(
                    out=us[:, i, :], in_=p_u[:, :], func=AF.Silu))(),
                    waits=[(pe, pe_uv)], incs=[(sc, 1)])
                S.op("vector", (lambda i=i: lambda e: e.tensor_mul(
                    us[:, i, :], us[:, i, :], p_v[:, :]))(),
                    waits=[(sc, sc.v)], incs=[(dv, 1)])
                S.op("tensor", lambda e: None, waits=[(dv, dv.v)])
            dv_h = dv.v

            for ttl in range(4):
                gt = s * 16 + tc * 4 + ttl
                for dh in range(2):
                    for i in range(NS):
                        S.op("tensor", (lambda ttl=ttl, dh=dh, i=i: lambda e: e.matmul(
                            p_z[:, :], us[:, i, ttl * 128:(ttl + 1) * 128],
                            w2b[:, i, dh * 512:(dh + 1) * 512],
                            start=(i == 0), stop=(i == NS - 1)))(),
                            waits=[(dv, dv_h), (gq, l2)] if (ttl == 0 and dh == 0 and i == 0) else [])
                    pe_z = emit_drain(S, bufs, att_junk)
                    S.op("vector", (lambda dh=dh: lambda e: e.tensor_copy(
                        acc[:, dh * 512:(dh + 1) * 512], p_z[:, :]))(),
                        waits=[(pe, pe_z)], incs=[(dv, 1)])
                    S.op("tensor", lambda e: None, waits=[(dv, dv.v)])
                for k in range(8):
                    tgt = p_x if k < 4 else p_u
                    S.op("tensor", (lambda k=k, tc=tc, ttl=ttl, tgt=tgt: lambda e: e.transpose(
                        tgt[:, (k % 4) * 128:(k % 4 + 1) * 128],
                        X[:, k, tc * 512 + ttl * 128:tc * 512 + (ttl + 1) * 128],
                        idf[:, :]))(),
                        waits=[(ld, ld_c)] if k == 0 else [])
                pe_x = emit_drain(S, bufs, att_junk)
                S.op("vector", lambda e: e.tensor_add(acc[:, 0:512], acc[:, 0:512],
                                                      p_x[:, :]),
                     waits=[(pe, pe_x)])
                S.op("vector", lambda e: e.tensor_add(acc[:, 512:1024], acc[:, 512:1024],
                                                      p_u[:, :]), incs=[(dv, 1)])
                S.op("tensor", lambda e: None, waits=[(dv, dv.v)])
                for k6 in range(TOPK):
                    S.op("vector", (lambda gt=gt, k6=k6: lambda e: e.tensor_copy(
                        sidx[:, :], sinfo[:, gt, 6 + k6:7 + k6]))())
                    S.op("vector", lambda e: e.tensor_copy(jnk[:, 0:128], jnk[:, 128:256]))
                    S.op("vector", lambda e: e.tensor_copy(jnk[:, 0:128], jnk[:, 128:256]))
                    S.op("vector", lambda e: e.tensor_copy(jnk[:, 0:128], jnk[:, 128:256]),
                         incs=[(dv, 1)])
                    S.op("gpsimd", (lambda: lambda e: e.indirect_dma_start(
                        out=gbuf[:, 0, :], out_offset=None,
                        in_=yb_full[:, :], in_offset=IndirectOffsetOnAxis(sidx[:, :], 0),
                        bounds_check=_breg(e), oob_is_err=False))(),
                        waits=[(dv, dv.v)], incs=[(gq, 16)])
                    S.op("vector", (lambda gt=gt, k6=k6: lambda e: e.scalar_tensor_tensor(
                        acc[:, :], gbuf[:, 0, :], sinfo[:, gt, k6:k6 + 1], acc[:, :],
                        ALU.mult, ALU.add))(),
                        waits=[(gq, gq.v)], incs=[(dv, 1)])
                S.dma("sync", outr[t0 + (tc * 4 + ttl) * 128:t0 + (tc * 4 + ttl + 1) * 128, :],
                      acc[:, :], waits=[(dv, dv.v)], sem=st)
                S.op("vector", lambda e: None, waits=[(st, st.v)])

    S.op("sync", lambda e: None, waits=[(st, st.v)])
    S.build(nc, ctx)
    ctx.close()
    return nc



def host_consts():
    idhi = np.concatenate([np.eye(64, dtype=np.float32)] * 2, 0)
    idf = np.eye(128, dtype=np.float32)
    idb = np.eye(128).astype(BF)
    ic64 = np.broadcast_to(np.arange(E, dtype=np.float32), (128, E)).copy()
    i512 = np.broadcast_to(np.arange(512, dtype=np.float32), (128, 512)).copy()
    ut = (np.arange(128)[:, None] < np.arange(128)[None, :]).astype(np.float32)
    kk = np.arange(128)[:, None]
    m = np.arange(896)[None, :]
    cmask = np.where(m - 384 >= kk, 0.0, -1e9).astype(np.float32)
    return dict(idhi=idhi, idf=idf, idb=idb, ic64=ic64, i512=i512, ut128=ut, cmask=cmask)


def prep_stage3_weights(Wq, Wk, Wv, Wo, nw, bq, bk, bv, bo):
    qk, bqk = [], []
    for g in range(H // 2):
        for h in (2 * g, 2 * g + 1):
            qk.append(Wq[h * DH:(h + 1) * DH, :])
        for h in (2 * g, 2 * g + 1):
            qk.append(Wk[h * DH:(h + 1) * DH, :])
        bqk.extend([bq[2 * g * DH:(2 * g + 1) * DH], bq[(2 * g + 1) * DH:(2 * g + 2) * DH],
                    bk[2 * g * DH:(2 * g + 1) * DH], bk[(2 * g + 1) * DH:(2 * g + 2) * DH]])
    return dict(
        wqkT3=np.ascontiguousarray(np.concatenate(qk, 0).T),
        wvT3=np.ascontiguousarray(Wv.T),
        woT=np.ascontiguousarray(Wo.T),
        nw=np.ascontiguousarray(nw.reshape(8, 128).T),
        bqk=np.ascontiguousarray(np.concatenate(bqk)[None, :]),
        bv=np.ascontiguousarray(bv[None, :]),
        bo=np.ascontiguousarray(bo[None, :]),
    )


_BUILT = {}


def _get(name, builder):
    if name not in _BUILT:
        _BUILT[name] = builder()
    return _BUILT[name]


def kernel(L0_attn_z, L0_attn_W_qkv, L0_ffn_weight_q, L0_ffn_weight_scale,
           L0_ffn_bias, L1_attn_norm_w, L1_attn_Wq, L1_attn_bq, L1_attn_Wk,
           L1_attn_bk, L1_attn_Wv, L1_attn_bv, L1_attn_Wo, L1_attn_bo,
           L1_ffn_gate_w, L1_ffn_gate_b, L1_ffn_w1, L1_ffn_w2, L1_ffn_w3,
           L1_ffn_ws1, L1_ffn_ws2, L1_ffn_ws3):
    f32 = lambda a: np.asarray(a, dtype=np.float32)
    C = host_consts()
    cores = list(range(N_CORES))

    shards = shard_tokens_fm(f32(L0_attn_z))
    wqkT, wvT = prep_wqkv(f32(L0_attn_W_qkv))
    nc1 = _get("s1", build_stage1)
    r1 = run_bass_kernel_spmd(nc1, [
        {"x0T": shards[c], "wqkT": wqkT, "wvT": wvT, "idhi": C["idhi"]}
        for c in cores], core_ids=cores)
    x1 = [r1.results[c]["x1T"] for c in cores]

    wqTb = np.ascontiguousarray(np.asarray(L0_ffn_weight_q).T.astype(BF))
    wsc = np.ascontiguousarray(f32(L0_ffn_weight_scale).reshape(8, 128).T)
    fbc = np.ascontiguousarray(f32(L0_ffn_bias).reshape(8, 128).T)
    nc2 = _get("s2", build_stage2)
    r2 = run_bass_kernel_spmd(nc2, [
        {"x1T": x1[c], "wqTb": wqTb, "wscale": wsc, "fbias": fbc, "idf": C["idf"]}
        for c in cores], core_ids=cores)
    x2 = [r2.results[c]["x2T"] for c in cores]

    w3d = prep_stage3_weights(f32(L1_attn_Wq), f32(L1_attn_Wk), f32(L1_attn_Wv),
                              f32(L1_attn_Wo), f32(L1_attn_norm_w), f32(L1_attn_bq),
                              f32(L1_attn_bk), f32(L1_attn_bv), f32(L1_attn_bo))
    w3d["idhi"] = C["idhi"]
    w3d["cmask"] = C["cmask"]
    nc3 = _get("s3", build_stage3)
    r3 = run_bass_kernel_spmd(nc3, [dict(w3d, x2T=x2[c]) for c in cores], core_ids=cores)
    x3 = [r3.results[c]["x3T"] for c in cores]

    gwT = np.ascontiguousarray(f32(L1_ffn_gate_w).T)
    gbrow = np.ascontiguousarray(f32(L1_ffn_gate_b)[None, :])
    nc4 = _get("s4", build_stage4)
    r4 = run_bass_kernel_spmd(nc4, [
        {"x3T": x3[c], "gwT": gwT, "gbrow": gbrow, "ic64": C["ic64"],
         "ut128": C["ut128"], "idb": C["idb"]} for c in cores], core_ids=cores)
    xfT = [r4.results[c]["xfT"] for c in cores]
    xfr = [r4.results[c]["xfr"] for c in cores]
    cnts = np.concatenate([r4.results[c]["cnt"] for c in cores], 0)
    rinfo = [r4.results[c]["rinfo"] for c in cores]

    nc4b = _get("s4b", build_stage4b)
    in4b = []
    for c in cores:
        csel = (np.arange(8) < c).astype(np.float32)[:, None]
        tbase = (c * TOK + np.arange(128) + 1).astype(np.float32)[:, None]
        in4b.append({"cnts": cnts, "rinfo": rinfo[c], "csel": np.ascontiguousarray(csel),
                     "i512": C["i512"], "tbase": np.ascontiguousarray(tbase),
                     "ic64": C["ic64"]})
    r4b = run_bass_kernel_spmd(nc4b, in4b, core_ids=cores)
    mparts = [r4b.results[c]["mpart"] for c in cores]
    sinfo = [r4b.results[c]["sinfo"] for c in cores]

    xf_full = np.ascontiguousarray(np.concatenate(xfr, 0))
    w1 = f32(L1_ffn_w1); w2 = f32(L1_ffn_w2); w3 = f32(L1_ffn_w3)
    nc5 = _get("s5", build_stage5)
    in5 = []
    for c in cores:
        mloc = np.ascontiguousarray(
            np.stack([mp[c * 8:(c + 1) * 8] for mp in mparts], -1))
        w1T = np.zeros((8, D, I_PAD), np.float32)
        w1T[:, :, :I_EXP] = w1[c * 8:(c + 1) * 8].transpose(0, 2, 1)
        w3T = np.zeros((8, D, I_PAD), np.float32)
        w3T[:, :, :I_EXP] = w3[c * 8:(c + 1) * 8].transpose(0, 2, 1)
        w2T = np.zeros((8, I_PAD, D), np.float32)
        w2T[:, :I_EXP, :] = w2[c * 8:(c + 1) * 8].transpose(0, 2, 1)
        in5.append({"mloc": mloc, "xf_full": xf_full, "w1T": np.ascontiguousarray(w1T),
                    "w3T": np.ascontiguousarray(w3T), "w2T": np.ascontiguousarray(w2T),
                    "idb": C["idb"]})
    r5 = run_bass_kernel_spmd(nc5, in5, core_ids=cores)
    yb_full = np.ascontiguousarray(np.concatenate([r5.results[c]["yb"] for c in cores], 0))

    ws1T = np.ascontiguousarray(f32(L1_ffn_ws1).T)
    ws3T = np.ascontiguousarray(f32(L1_ffn_ws3).T)
    ws2T = np.ascontiguousarray(f32(L1_ffn_ws2).T)
    nc6 = _get("s6", build_stage6)
    r6 = run_bass_kernel_spmd(nc6, [
        {"x3T": x3[c], "xfT": xfT[c], "yb_full": yb_full, "sinfo": sinfo[c],
         "ws1T": ws1T, "ws3T": ws3T, "ws2T": ws2T, "idf": C["idf"]}
        for c in cores], core_ids=cores)
    out = np.concatenate([r6.results[c]["outr"].reshape(SEQ_PER_CORE, T, D)
                          for c in cores], 0)
    return out


# version 04 (final)
# speedup vs baseline: 1.1868x; 1.0142x over previous
import numpy as np
import ml_dtypes
import concourse.bass as bass
import concourse.mybir as mybir
from concourse.bass_utils import run_bass_kernel_spmd
from concourse.bass_core_rust import IndirectOffsetOnAxis
from contextlib import ExitStack

F32 = mybir.dt.float32
BF16 = mybir.dt.bfloat16
I32 = mybir.dt.int32
AF = mybir.ActivationFunctionType
ALU = mybir.AluOpType
AX = mybir.AxisListType

N_CORES = 8
B, T, D = 16, 2048, 1024
SEQ_PER_CORE = B // N_CORES
TOK = SEQ_PER_CORE * T
H, DH = 16, 64
E, TOPK, CAP = 64, 6, 512
I_EXP = 704
SI = 1408
EPS_LN = 1e-5
EPS_RMS = 1e-6
SHIFT = 12.0

BF = ml_dtypes.bfloat16


ENGINES = ("sync", "tensor", "vector", "scalar", "gpsimd")


class PySem:
    __slots__ = ("name", "v", "h")

    def __init__(self, name):
        self.name = name
        self.v = 0
        self.h = None


class Sched:

    def __init__(self, serialize=False, chained=False, ser_set=None):
        self.ser_set = set(ser_set) if ser_set else None
        self.items = {e: [] for e in ENGINES}
        self.sems = []
        self.fence_buf = None
        self.chained = chained
        self.chain_sems = {}
        self.serialize = serialize
        self.eng_sems = {}

    def sem(self, name):
        s = PySem(name)
        self.sems.append(s)
        return s

    def op(self, eng, fn, waits=(), incs=(), isdma=False):
        if self.chained:
            if eng not in self.chain_sems:
                self.chain_sems[eng] = self.sem(f"ch_{eng}")
                self.chain_last = getattr(self, "chain_last", {})
                self.chain_last[eng] = None
            last = self.chain_last[eng]
            if last is not None:
                waits = list(waits) + [last]
            if incs:
                s0, n0 = incs[0]
                self.chain_last[eng] = (s0, s0.v + n0)
            else:
                cs = self.chain_sems[eng]
                incs = list(incs) + [(cs, 16 if isdma else 1)]
                self.chain_last[eng] = (cs, cs.v + (16 if isdma else 1))
        if self.serialize:
            if not self.eng_sems:
                for e2 in ENGINES:
                    self.eng_sems[e2] = self.sem(f"ser_{e2}")
                self.last = {e2: None for e2 in ENGINES}
            if self.ser_set and "PEWAIT_ONLY" in self.ser_set:
                if eng in ("vector", "scalar"):
                    pesem = next((x for x in self.sems if x.name == "pe"), None)
                    if pesem is not None and pesem.v:
                        waits = list(waits) + [(pesem, pesem.v)]
                for (s_, n) in incs:
                    s_.v += n
                if eng in ("vector", "scalar") and incs and not isdma:
                    self.items[eng].append((fn, tuple(waits), ()))
                    self.items[eng].append((self._fence_fn(eng), (), tuple(incs)))
                else:
                    self.items[eng].append((fn, tuple(waits), tuple(incs)))
                return
            if self.ser_set and "DIR_PE_WAITS" in self.ser_set:
                extra = [w for e2, w in self.last.items()
                         if e2 != eng and w] if eng == "tensor" else []
            elif self.ser_set and "DIR_WAITS_PE" in self.ser_set:
                extra = [self.last["tensor"]] if (eng != "tensor" and self.last.get("tensor")) else []
            else:
                extra = [w for e2, w in self.last.items()
                         if e2 != eng and w and (self.ser_set is None or e2 in self.ser_set
                                                 or eng in self.ser_set)]
            waits = list(waits) + extra
        for (s, n) in incs:
            s.v += n
        if eng in ("vector", "scalar") and incs and not isdma:
            self.items[eng].append((fn, tuple(waits), ()))
            self.items[eng].append((self._fence_fn(eng), (), tuple(incs)))
        else:
            self.items[eng].append((fn, tuple(waits), tuple(incs)))
        if self.serialize:
            if isdma and incs:
                self.last[eng] = (incs[0][0], incs[0][0].v)
            else:
                ser = self.eng_sems[eng]
                ser.v += 1
                self.items[eng].append((self._ser_fence_fn(eng), (), ((ser, 1),)))
                self.last[eng] = (ser, ser.v)

    def _ser_fence_fn(self, eng):
        if eng in ("vector", "scalar"):
            return self._fence_fn(eng)
        if eng == "tensor":
            return lambda e: e.matmul(self.ser_junk, self.ser_ones1, self.ser_onest,
                                      start=True, stop=True)
        return lambda e: e.nop()

    def _fence_fn(self, eng):
        if self.fence_buf is None:
            raise RuntimeError("fence buffer not set")
        fb = self.fence_buf
        if eng == "vector":
            return lambda e: e.memset(fb[:, 0:1], 0.0)
        return lambda e: e.activation(out=fb[:, 1:2], in_=fb[:, 1:2], func=AF.Identity)

    def dma(self, eng, out, in_, waits=(), sem=None, **kw):
        incs = [(sem, 16)] if sem is not None else []
        self.op(eng, lambda e: e.dma_start(out=out, in_=in_, **kw), waits, incs, isdma=True)
        return sem.v if sem is not None else None

    def build(self, nc, ctx):
        for s in self.sems:
            s.h = ctx.enter_context(nc.semaphore(s.name))
        block = ctx.enter_context(nc.Block())
        for eng_name in ENGINES:
            items = self.items[eng_name]
            if not items:
                continue

            def make_body(items):
                def body(eng):
                    for fn, waits, incs in items:
                        for (s, v) in waits:
                            eng.wait_ge(s.h, v)
                        inst = fn(eng)
                        for (s, n) in incs:
                            if inst is None:
                                inst = eng.nop()
                            inst.then_inc(s.h, n)
                return body

            getattr(block, eng_name)(make_body(items))


def ceil_div(a, b):
    return (a + b - 1) // b



def emit_ln_chunk(S, bufs, X, tsl, ld_wait, out_ap, eps=EPS_LN, rms=False, nw=None):
    (pe, dv, sc) = bufs["pe"], bufs["dv"], bufs["sc"]
    ones1, ones128 = bufs["ones1"], bufs["ones128"]
    sq, row = bufs["sq"], bufs["row"]
    p_m = bufs["p_m"][0:1, :]
    p_sq = bufs["p_sq"][0:1, :]
    p_rb, p_mrb = bufs["p_rb"], bufs["p_mrb"]

    for k in range(8):
        pk = k % 2
        waits = list(ld_wait)
        if k >= 2:
            waits.append((pe, pe.v))
        S.op("scalar", (lambda k=k, pk=pk, tsl=tsl: lambda e: e.activation(
            out=sq[:, pk, :], in_=X[:, k, tsl], func=AF.Square))(),
            waits=waits, incs=[(sc, 1)])
        S.op("tensor", (lambda k=k, tsl=tsl: lambda e: e.matmul(
            p_m[:, :], ones1[:, :], X[:, k, tsl],
            start=(k == 0), stop=(k == 7)))(),
            waits=list(ld_wait) if k == 0 else [])
        S.op("tensor", (lambda k=k, pk=pk: lambda e: e.matmul(
            p_sq[:, :], ones1[:, :], sq[:, pk, :],
            start=(k == 0), stop=(k == 7)))(),
            waits=[(sc, sc.v)], incs=[(pe, 1)])
    pe_stats = emit_drain(S, bufs, bufs["ln_junk"])
    m_r = row[:, 0:512]; ms_r = row[:, 512:1024]; var_r = row[:, 1024:1536]
    r_r = row[:, 1536:2048]; mr_r = row[:, 512:1024]
    S.op("vector", lambda e: e.tensor_single_scalar(m_r, p_m[:, :], 1.0 / D, ALU.mult),
         waits=[(pe, pe_stats)])
    S.op("vector", lambda e: e.tensor_single_scalar(ms_r, p_sq[:, :], 1.0 / D, ALU.mult))
    S.op("vector", lambda e: e.tensor_mul(var_r, m_r, m_r))
    S.op("vector", lambda e: e.tensor_sub(var_r, ms_r, var_r), incs=[(dv, 1)])
    eps_ap = bufs["cb"][0:1, 0:1] if eps == EPS_LN else bufs["cb"][0:1, 2:3]
    S.op("scalar", (lambda var_r=var_r, eps_ap=eps_ap: lambda e: e.activation(
        out=var_r, in_=var_r, func=AF.Ln, bias=eps_ap))(),
        waits=[(dv, dv.v)])
    S.op("scalar", lambda e: e.activation(out=r_r, in_=var_r, func=AF.Exp, scale=-0.5),
         incs=[(sc, 1)])
    if rms:
        S.op("scalar", lambda e: e.activation(out=bufs["tmprow"][0:1, 0:512], in_=var_r,
                                              func=AF.Exp, scale=-1.0), incs=[(sc, 1)])
        S.op("vector", lambda e: e.tensor_single_scalar(
            bufs["tmprow"][0:1, 512:1024], bufs["tmprow"][0:1, 0:512], -EPS_LN, ALU.mult),
            waits=[(sc, sc.v)])
        S.op("vector", lambda e: e.tensor_single_scalar(
            bufs["tmprow"][0:1, 512:1024], bufs["tmprow"][0:1, 512:1024], 1.0, ALU.add),
            incs=[(dv, 1)])
        S.op("scalar", lambda e: e.activation(out=bufs["tmprow"][0:1, 512:1024],
                                              in_=bufs["tmprow"][0:1, 512:1024],
                                              func=AF.Ln, bias=bufs["cb"][0:1, 2:3]),
             waits=[(dv, dv.v)])
        S.op("scalar", lambda e: e.activation(out=bufs["tmprow"][0:1, 512:1024],
                                              in_=bufs["tmprow"][0:1, 512:1024],
                                              func=AF.Exp, scale=-0.5), incs=[(sc, 1)])
        S.op("vector", lambda e: e.tensor_mul(r_r, r_r, bufs["tmprow"][0:1, 512:1024]),
             waits=[(sc, sc.v)])
    S.op("vector", lambda e: e.tensor_mul(mr_r, m_r, r_r),
         waits=[(sc, sc.v)], incs=[(dv, 1)])
    S.op("tensor", lambda e: e.matmul(p_rb[:, :], ones128[:, :], r_r, start=True, stop=True),
         waits=[(dv, dv.v)])
    S.op("tensor", lambda e: e.matmul(p_mrb[:, :], ones128[:, :], mr_r, start=True, stop=True))
    pe_bc = emit_drain(S, bufs, bufs["ln_junk"])
    for k in range(8):
        S.op("vector", (lambda k=k, tsl=tsl: lambda e: e.tensor_mul(
            bufs["tmp8"][:, :], X[:, k, tsl], p_rb[:, :]))(),
            waits=[(pe, pe_bc)] if k == 0 else [])
        if nw is None:
            S.op("vector", (lambda k=k: lambda e: e.tensor_sub(
                out_ap(k), bufs["tmp8"][:, :], p_mrb[:, :]))(),
                incs=[(dv, 1)] if k == 7 else [])
        else:
            S.op("vector", (lambda k=k: lambda e: e.tensor_sub(
                bufs["tmp8"][:, :], bufs["tmp8"][:, :], p_mrb[:, :]))())
            S.op("vector", (lambda k=k: lambda e: e.tensor_scalar(
                out_ap(k), bufs["tmp8"][:, :], nw[:, k:k + 1], None, ALU.mult))(),
                incs=[(dv, 1)] if k == 7 else [])
    S.op("tensor", lambda e: None, waits=[(dv, dv.v)])
    return dv.v


def emit_layernorm(S, nc, bufs, X, xn, seq_len, ld_wait, eps=EPS_LN):
    for tc in range(seq_len // 512):
        tsl = slice(tc * 512, (tc + 1) * 512)
        emit_ln_chunk(S, bufs, X, tsl, ld_wait,
                      (lambda tsl=tsl: lambda k: xn[:, k, tsl])(), eps=eps)
    return bufs["dv"].v


def alloc_common(S, nc, ctx):
    sb = lambda name, shape, dt=F32: ctx.enter_context(nc.sbuf_tensor(name, shape, dt))
    ps = lambda name, shape, dt=F32: ctx.enter_context(nc.psum_tensor(name, shape, dt))
    bufs = {
        "pe": S.sem("pe"), "dv": S.sem("dv"), "sc": S.sem("sc"),
        "ld": S.sem("ld"), "gq": S.sem("gq"), "st": S.sem("st"),
        "ones1": sb("ones1", [128, 1]),
        "ones128": sb("ones128", [1, 128]),
        "onest": sb("onest", [128, 256]),
        "cb": sb("cb", [128, 8]),
        "sq": sb("sq", [128, 2, 512]),
        "row": sb("row", [1, 4 * 512]),
        "tmprow": sb("tmprow", [1, 2 * 512]),
        "tmp8": sb("tmp8", [128, 512]),
    }
    S.fence_buf = sb("fence", [128, 2])
    dv = bufs["dv"]
    S.op("vector", lambda e: e.memset(bufs["ones1"][:, :], 1.0), incs=[(dv, 1)])
    S.op("vector", lambda e: e.memset(bufs["ones128"][:, :], 1.0), incs=[(dv, 1)])
    S.op("vector", lambda e: e.memset(bufs["onest"][:, :], 1.0), incs=[(dv, 1)])
    S.op("vector", lambda e: e.memset(bufs["cb"][:, 0:1], EPS_LN), incs=[(dv, 1)])
    S.op("vector", lambda e: e.memset(bufs["cb"][:, 1:2], -SHIFT), incs=[(dv, 1)])
    S.op("vector", lambda e: e.memset(bufs["cb"][:, 2:3], EPS_RMS), incs=[(dv, 1)])
    bufs["init_dv"] = dv.v
    return bufs, sb, ps



def emit_drain(S, bufs, junk_view):
    pe = bufs["pe"]
    S.op("tensor", lambda e: e.matmul(junk_view, bufs["ones1"][0:1, 0:1],
                                      bufs["onest"][0:1, 0:256], start=True, stop=True),
         incs=[(pe, 1)])
    return pe.v


def emit_attn_head(S, bufs, *, qS, kS, vaug_v, X_add, h, qsl, kt_lo, kt_hi,
                   pt, p1buf, p_scA, p_scB, p_sums, p_o, p_rb2, p_sh,
                   osb, rro, t1, id64, att_junk, mask_sb=None, mask_tiles=(),
                   scale=0.125, c_pass1=8.0, nq=512):
    pe, dv, sc = bufs["pe"], bufs["dv"], bufs["sc"]
    nkt = kt_hi - kt_lo
    dv_floor = bufs.get("dv_p2", None)
    if dv_floor is None:
        dv_floor = dv.v
    for i, kt in enumerate(range(kt_lo, kt_hi)):
        bank = p_scA if i % 2 == 0 else p_scB
        w = [(sc, sc.v)] if i >= 2 else [(sc, sc.v), (dv, dv_floor)]
        S.op("tensor", (lambda kt=kt, bank=bank, i=i: lambda e: e.matmul(
            bank[:, 0:nq], kS[0:64, kt * 128:(kt + 1) * 128], qS[0:64, qsl],
            start=True, stop=True))(), waits=w, incs=[(pe, 1)])
        if kt in mask_tiles:
            S.op("vector", (lambda kt=kt, bank=bank: lambda e: e.tensor_add(
                bank[:, 0:nq], bank[:, 0:nq], mask_tiles[kt]))(),
                waits=[(pe, pe.v)], incs=[(dv, 1)])
            S.op("scalar", (lambda i=i, bank=bank: lambda e: e.activation(
                out=p1buf[:, (i % 2) * nq:(i % 2) * nq + nq], in_=bank[:, 0:nq],
                func=AF.Exp, scale=scale / c_pass1))(),
                waits=[(dv, dv.v)], incs=[(sc, 1)])
        else:
            S.op("scalar", (lambda i=i, bank=bank: lambda e: e.activation(
                out=p1buf[:, (i % 2) * nq:(i % 2) * nq + nq], in_=bank[:, 0:nq],
                func=AF.Exp, scale=scale / c_pass1))(),
                waits=[(pe, pe.v)], incs=[(sc, 1)])
        sacc = bufs["sacc"]
        if i == 0:
            S.op("vector", lambda e: e.tensor_copy(sacc[:, 0:nq], p1buf[:, 0:nq]),
                 waits=[(sc, sc.v)])
        else:
            S.op("vector", (lambda i=i: lambda e: e.tensor_add(
                sacc[:, 0:nq], sacc[:, 0:nq],
                p1buf[:, (i % 2) * nq:(i % 2) * nq + nq]))(),
                waits=[(sc, sc.v)], incs=[(dv, 1)] if i == nkt - 1 else [])
    S.op("tensor", lambda e: e.matmul(p_sums[0:1, 0:nq], bufs["onesc_bf"][:, 0:1],
                                      sacc[:, 0:nq], start=True, stop=True),
         waits=[(dv, dv.v)])
    pe_s1 = emit_drain(S, bufs, att_junk)
    S.op("scalar", lambda e: e.activation(out=rro[0:1, 0:nq], in_=p_sums[0:1, 0:nq],
                                          func=AF.Ln),
         waits=[(pe, pe_s1)], incs=[(sc, 1)])
    S.op("vector", (lambda: lambda e: e.tensor_single_scalar(
        rro[0:1, 0:nq], rro[0:1, 0:nq], -c_pass1 / scale, ALU.mult))(),
        waits=[(sc, sc.v)], incs=[(dv, 1)])
    S.op("tensor", lambda e: e.matmul(p_sh[64:65, 0:nq], bufs["ones1"][0:1, 0:1],
                                      rro[0:1, 0:nq], start=True, stop=True),
         waits=[(dv, dv.v)])
    pe_m = emit_drain(S, bufs, att_junk)
    S.op("vector", (lambda: lambda e: e.tensor_copy(qS[64:65, qsl], p_sh[64:65, 0:nq]))(),
         waits=[(pe, pe_m)], incs=[(dv, 1)])
    dv_m = dv.v
    sc_tiles = []
    for i, kt in enumerate(range(kt_lo, kt_hi)):
        bank = p_scA if i % 2 == 0 else p_scB
        w = [(dv, dv_m)] if i < 2 else [(sc, sc.v)]
        S.op("tensor", (lambda kt=kt, bank=bank: lambda e: e.matmul(
            bank[:, 0:nq], kS[0:65, kt * 128:(kt + 1) * 128], qS[0:65, qsl],
            start=True, stop=True))(), waits=w, incs=[(pe, 1)])
        if kt in mask_tiles:
            S.op("vector", (lambda kt=kt, bank=bank: lambda e: e.tensor_add(
                bank[:, 0:nq], bank[:, 0:nq], mask_tiles[kt]))(),
                waits=[(pe, pe.v)], incs=[(dv, 1)])
            S.op("scalar", (lambda i=i, bank=bank: lambda e: e.activation(
                out=pt[:, i * nq:(i + 1) * nq], in_=bank[:, 0:nq],
                func=AF.Exp, scale=scale))(),
                waits=[(dv, dv.v)], incs=[(sc, 1)])
        else:
            S.op("scalar", (lambda i=i, bank=bank: lambda e: e.activation(
                out=pt[:, i * nq:(i + 1) * nq], in_=bank[:, 0:nq],
                func=AF.Exp, scale=scale))(),
                waits=[(pe, pe.v)], incs=[(sc, 1)])
        sc_tiles.append(sc.v)
    bufs["dv_p2"] = dv.v
    for i, kt in enumerate(range(kt_lo, kt_hi)):
        S.op("tensor", (lambda kt=kt, i=i, nkt=nkt: lambda e: e.matmul(
            p_o[0:65, 0:nq], vaug_v[:, kt, :], pt[:, i * nq:(i + 1) * nq],
            start=(i == 0), stop=(i == nkt - 1)))(),
            waits=[(sc, sc_tiles[min(i + 2, nkt - 1)])])
    pe_av = emit_drain(S, bufs, att_junk)
    S.op("vector", lambda e: e.tensor_copy(osb[:, 0:nq], p_o[0:65, 0:nq]),
         waits=[(pe, pe_av)])
    S.op("vector", lambda e: e.tensor_copy(rro[64:65, 0:nq], osb[64:65, 0:nq]))
    S.op("vector", lambda e: e.reciprocal(rro[64:65, 0:nq], rro[64:65, 0:nq]),
         incs=[(dv, 1)])
    S.op("tensor", lambda e: e.matmul(p_rb2[0:64, 0:nq], bufs["onest"][64:65, 0:64],
                                      rro[64:65, 0:nq], start=True, stop=True),
         waits=[(dv, dv.v)])
    pe_rb2 = emit_drain(S, bufs, att_junk)
    S.op("vector", lambda e: e.tensor_mul(t1[:, 0:nq], osb[0:64, 0:nq], p_rb2[0:64, 0:nq]),
         waits=[(pe, pe_rb2)], incs=[(dv, 1)])
    X_add(t1, p_sh, id64)
    S.op("tensor", lambda e: None, waits=[(dv, dv.v)])


def build_stage1(serialize=False, chained=False):
    nc = bass.Bass("TRN2", target_bir_lowering=False, debug=False)
    x0T = nc.dram_tensor("x0T", [D, TOK], F32, kind="ExternalInput").ap()
    wqkT = nc.dram_tensor("wqkT", [D, 2 * D], F32, kind="ExternalInput").ap()
    wvT = nc.dram_tensor("wvT", [D, D], F32, kind="ExternalInput").ap()
    idhi = nc.dram_tensor("idhi", [128, 64], F32, kind="ExternalInput").ap()
    x1T = nc.dram_tensor("x1T", [D, TOK], F32, kind="ExternalOutput").ap()

    S = Sched(serialize=serialize, chained=chained)
    ctx = ExitStack()
    bufs, sb, ps = alloc_common(S, nc, ctx)
    pe, dv, sc = bufs["pe"], bufs["dv"], bufs["sc"]
    ld, gq, st = bufs["ld"], bufs["gq"], bufs["st"]

    X = sb("X", [128, 8, T])
    xn = sb("xn", [128, 8, T], BF16)
    vaug = sb("vaug", [128, H * 16 * 65], BF16)
    qS = sb("qS", [65, 2 * T], BF16)
    kS = sb("kS", [65, 2 * T], BF16)
    pt = sb("pt", [128, 16 * 512], BF16)
    p1buf = sb("p1buf", [128, 2 * 512], BF16)
    wv = sb("wv", [128, 8, 512], BF16)
    wqk = sb("wqk", [128, 2, 8, 256], BF16)
    id64 = sb("id64", [128, 64])
    osb = sb("osb", [65, 512])
    rro = sb("rro", [128, 512])
    bufs["onesc_bf"] = sb("onesc_bf", [128, 1], BF16)
    bufs["sacc"] = pt[:, 15 * 512:16 * 512]
    t1 = bufs["tmp8"][0:64, :]

    P = [ps(f"P{i}", [128, 512]) for i in range(8)]
    bufs["p_m"], bufs["p_sq"], bufs["p_rb"], bufs["p_mrb"] = P[0], P[1], P[2], P[3]
    bufs["ln_junk"] = P[5][0:1, 0:256]
    p_qk = P[4]
    p_scA, p_scB = P[2], P[3]
    p_sums = P[1]
    p_o = P[4]
    p_rb2 = P[6]
    p_sh = P[5]
    att_junk = P[0][0:1, 0:256]

    vaug_vh = vaug.rearrange("p (h t c) -> p h t c", h=H, c=65)
    S.op("vector", lambda e: e.memset(vaug_vh[:, :, :, 64:65], 1.0), incs=[(dv, 1)])
    S.op("vector", lambda e: e.memset(kS[64:65, :], 1.0), incs=[(dv, 1)])
    S.op("vector", lambda e: e.memset(bufs["onesc_bf"][:, :], 1.0), incs=[(dv, 1)])
    init_dv = dv.v

    S.dma("sync", id64[:, :], idhi[:, :], sem=ld)
    ld_id = ld.v


    for s in range(SEQ_PER_CORE):
        t0 = s * T
        x_src = x0T[:, t0:t0 + T].rearrange("(c p) t -> p c t", p=128)
        S.dma("sync", X[:, :, :], x_src, waits=[(st, st.v)], sem=ld)
        ld_x = ld.v

        dv_xn = emit_layernorm(S, nc, bufs, X, xn, T,
                               [(ld, ld_x)] + ([(bufs["dv"], bufs["init_dv"])] if s == 0 else []))

        for half in range(2):
            S.op("gpsimd", lambda e: None, waits=[(pe, pe.v)])
            ld_wv = S.dma("gpsimd", wv[:, :, :],
                          wvT[:, half * 512:(half + 1) * 512].rearrange("(c p) f -> p c f", p=128),
                          sem=gq)
            for tt in range(16):
                for k in range(8):
                    S.op("tensor", (lambda tt=tt, k=k: lambda e: e.matmul(
                        p_qk[:, :], xn[:, k, tt * 128:(tt + 1) * 128],
                        wv[:, k, :],
                        start=(k == 0), stop=(k == 7)))(),
                        waits=([(dv, dv_xn), (gq, ld_wv)] if (tt == 0 and k == 0) else []))
                pe_v = emit_drain(S, bufs, att_junk)
                for hh in range(8):
                    h = half * 8 + hh
                    S.op("vector", (lambda tt=tt, h=h, hh=hh: lambda e: e.tensor_copy(
                        vaug_vh[:, h, tt, 0:64], p_qk[:, hh * 64:(hh + 1) * 64]))(),
                        waits=[(pe, pe_v)] if hh == 0 else [],
                        incs=[(dv, 1)] if hh == 7 else [])
                S.op("tensor", lambda e: None, waits=[(dv, dv.v)])

        for g in range(H // 2):
            gb = g % 2
            wsrc = wqkT[:, g * 256:(g + 1) * 256].rearrange("(c p) f -> p c f", p=128)
            if s > 0 or g >= 2:
                S.op("gpsimd", lambda e: None, waits=[(pe, pe.v)])
            ld_w = S.dma("gpsimd", wqk[:, gb, :, :], wsrc, sem=gq)

            for dst, c0 in ((qS, 0), (kS, 128)):
                for ncx in range(4):
                    for k in range(8):
                        S.op("tensor", (lambda gb=gb, ncx=ncx, k=k, c0=c0: lambda e: e.matmul(
                            p_qk[:, :], wqk[:, gb, k, c0:c0 + 128],
                            xn[:, k, ncx * 512:(ncx + 1) * 512],
                            start=(k == 0), stop=(k == 7)))(),
                            waits=[(gq, ld_w), (ld, ld_id)] if (k == 0 and ncx == 0 and c0 == 0) else [])
                    pe_qk = emit_drain(S, bufs, att_junk)
                    S.op("vector", (lambda dst=dst, ncx=ncx: lambda e: e.tensor_copy(
                        dst[0:64, ncx * 512:(ncx + 1) * 512], p_qk[0:64, :]))(),
                        waits=[(pe, pe_qk)])
                    S.op("vector", (lambda ncx=ncx: lambda e: e.tensor_copy(
                        bufs["tmp8"][64:128, :], p_qk[64:128, :]))(),
                        incs=[(dv, 1)])
                    S.op("tensor", lambda e: e.matmul(
                        p_sh[0:64, :], id64[64:128, :], bufs["tmp8"][64:128, :],
                        start=True, stop=True),
                        waits=[(dv, dv.v)])
                    pe_sh2 = emit_drain(S, bufs, att_junk)
                    S.op("vector", (lambda dst=dst, ncx=ncx: lambda e: e.tensor_copy(
                        dst[0:64, T + ncx * 512:T + (ncx + 1) * 512], p_sh[0:64, :]))(),
                        waits=[(pe, pe_sh2)], incs=[(dv, 1)])
                    S.op("tensor", lambda e: None, waits=[(dv, dv.v)])
            dv_qk = dv.v

            for sub in range(2):
                h = 2 * g + sub
                qSh = qS[:, sub * T:(sub + 1) * T]
                kSh = kS[:, sub * T:(sub + 1) * T]
                c = h // 2
                for qc in range(4):
                    qsl = slice(qc * 512, (qc + 1) * 512)

                    def X_add(t1v, p_shv, id64v, c=c, qsl=qsl, sub=sub):
                        if sub == 0:
                            S.op("vector", (lambda: lambda e: e.tensor_add(
                                X[0:64, c, qsl], X[0:64, c, qsl], t1v[:, 0:512]))(),
                                incs=[(dv, 1)])
                        else:
                            S.op("tensor", (lambda: lambda e: e.matmul(
                                p_shv[64:128, :], id64v[0:64, :],
                                t1v[:, 0:512], start=True, stop=True))(),
                                waits=[(dv, dv.v)])
                            pe_sh = emit_drain(S, bufs, att_junk)
                            S.op("vector", (lambda: lambda e: e.tensor_add(
                                X[64:128, c, qsl], X[64:128, c, qsl], p_shv[64:128, :]))(),
                                waits=[(pe, pe_sh)], incs=[(dv, 1)])

                    emit_attn_head(S, bufs, qS=qSh, kS=kSh, vaug_v=vaug_vh[:, h, :, :],
                                   X_add=X_add, h=h, qsl=qsl, kt_lo=0, kt_hi=16,
                                   pt=pt, p1buf=p1buf, p_scA=p_scA, p_scB=p_scB,
                                   p_sums=p_sums, p_o=p_o, p_rb2=p_rb2, p_sh=p_sh,
                                   osb=osb, rro=rro, t1=t1, id64=id64,
                                   att_junk=att_junk)

        x_dst = x1T[:, t0:t0 + T].rearrange("(c p) t -> p c t", p=128)
        S.dma("sync", x_dst, X[:, :, :], waits=[(dv, dv.v)], sem=st)

    S.op("sync", lambda e: None, waits=[(st, st.v)])
    S.build(nc, ctx)
    ctx.close()
    return nc


def shard_tokens_fm(x_full):
    out = []
    for c in range(N_CORES):
        sl = x_full[c * SEQ_PER_CORE:(c + 1) * SEQ_PER_CORE].reshape(TOK, D)
        out.append(np.ascontiguousarray(np.asarray(sl, dtype=np.float32).T))
    return out


def prep_wqkv(W_qkv):
    W = np.asarray(W_qkv, dtype=np.float32).reshape(H, 3, DH, D)
    pairs = []
    for g in range(H // 2):
        pairs.append(W[2 * g, 0]); pairs.append(W[2 * g + 1, 0])
        pairs.append(W[2 * g, 1]); pairs.append(W[2 * g + 1, 1])
    qk = np.concatenate(pairs, 0)
    v = W[:, 2].reshape(H * DH, D)
    return np.ascontiguousarray(qk.T), np.ascontiguousarray(v.T)






def build_stage2(serialize=False, chained=False):
    nc = bass.Bass("TRN2", target_bir_lowering=False, debug=False)
    x1T = nc.dram_tensor("x1T", [D, TOK], F32, kind="ExternalInput").ap()
    wqTb = nc.dram_tensor("wqTb", [D, D], BF16, kind="ExternalInput").ap()
    wscale = nc.dram_tensor("wscale", [128, 8], F32, kind="ExternalInput").ap()
    fbias = nc.dram_tensor("fbias", [128, 8], F32, kind="ExternalInput").ap()
    idf_in = nc.dram_tensor("idf", [128, 128], F32, kind="ExternalInput").ap()
    x2T = nc.dram_tensor("x2T", [D, TOK], F32, kind="ExternalOutput").ap()

    S = Sched(serialize=serialize, chained=chained)
    ctx = ExitStack()
    bufs, sb, ps = alloc_common(S, nc, ctx)
    pe, dv, sc = bufs["pe"], bufs["dv"], bufs["sc"]
    ld, gq, st = bufs["ld"], bufs["gq"], bufs["st"]

    X = sb("X", [128, 8, T])
    xnf = sb("xnf", [128, 8, 512])
    xq = sb("xq", [128, 8, 512], BF16)
    wq = sb("wq", [128, 8, D], BF16)
    wsc = sb("wsc", [128, 8])
    fbc = sb("fbc", [128, 8])
    bufs["idf"] = sb("idf_sb", [128, 128])
    amaxc = sb("amaxc", [128, 2])
    rrow = sb("rrow", [1, 3 * 512])
    ti = sb("ti", [128, 512], I32)
    tf = sb("tf", [128, 512])
    td = sb("td", [128, 512])
    tg = sb("tg", [128, 512])
    t2 = sb("t2", [128, 512])

    P = [ps(f"P{i}", [128, 512]) for i in range(8)]
    bufs["p_m"], bufs["p_sq"], bufs["p_rb"], bufs["p_mrb"] = P[0], P[1], P[2], P[3]
    bufs["ln_junk"] = P[5][0:1, 0:256]
    p_tr = P[4]
    p_out = P[5]
    p_xs = P[6]
    p_junk = P[7][0:1, 0:256]

    ld_w = S.dma("gpsimd", wq[:, :, :], wqTb.rearrange("(c p) f -> p c f", p=128), sem=gq)
    S.dma("sync", wsc[:, :], wscale[:, :], sem=ld)
    S.dma("sync", fbc[:, :], fbias[:, :], sem=ld)
    S.dma("sync", bufs["idf"][:, :], idf_in[:, :], sem=ld)
    ld_const = ld.v

    for s in range(SEQ_PER_CORE):
        t0 = s * T
        x_src = x1T[:, t0:t0 + T].rearrange("(c p) t -> p c t", p=128)
        S.dma("sync", X[:, :, :], x_src, waits=[(st, st.v)], sem=ld)
        ld_x = ld.v

        for tc in range(4):
            tsl = slice(tc * 512, (tc + 1) * 512)
            dv_ln = emit_ln_chunk(
                S, bufs, X, tsl,
                [(ld, ld_x)] + ([(dv, bufs["init_dv"]), (ld, ld_const)] if (s == 0 and tc == 0) else []),
                lambda k: xnf[:, k, :])
            for tt in range(4):
                for k in range(8):
                    S.op("tensor", (lambda k=k, tt=tt: lambda e: e.transpose(
                        p_tr[:, 0:128], xnf[:, k, tt * 128:(tt + 1) * 128], bufs["idf"][:, :]))(),
                        waits=[(dv, dv_ln)] if (tt == 0 and k == 0) else [])
                    pe_t = emit_drain(S, bufs, p_junk)
                    S.op("vector", (lambda k=k: lambda e: e.tensor_reduce(
                        amaxc[:, 1:2], p_tr[:, 0:128], AX.X, ALU.max,
                        apply_absolute_value=True))(),
                        waits=[(pe, pe_t)])
                    if k == 0:
                        S.op("vector", lambda e: e.tensor_copy(amaxc[:, 0:1], amaxc[:, 1:2]),
                             waits=[(pe, pe.v)], incs=[(dv, 1)])
                    else:
                        S.op("vector", lambda e: e.tensor_tensor(
                            amaxc[:, 0:1], amaxc[:, 0:1], amaxc[:, 1:2], ALU.max),
                            incs=[(dv, 1)])
                    S.op("tensor", lambda e: None, waits=[(dv, dv.v)])
                S.op("tensor", (lambda tt=tt: lambda e: e.transpose(
                    p_tr[0:1, 0:128], amaxc[:, 0:1], bufs["idf"][:, :]))())
                pe_t2 = emit_drain(S, bufs, p_junk)
                S.op("vector", (lambda tt=tt: lambda e: e.tensor_copy(
                    rrow[0:1, tt * 128:(tt + 1) * 128], p_tr[0:1, 0:128]))(),
                    waits=[(pe, pe_t2)], incs=[(dv, 1)])
                S.op("tensor", lambda e: None, waits=[(dv, dv.v)])
            S.op("vector", lambda e: e.tensor_single_scalar(
                rrow[0:1, 0:512], rrow[0:1, 0:512], 1e-6, ALU.max))
            S.op("vector", lambda e: e.tensor_single_scalar(
                rrow[0:1, 512:1024], rrow[0:1, 0:512], 1.0 / 127.0, ALU.mult))
            S.op("vector", lambda e: e.reciprocal(rrow[0:1, 1024:1536], rrow[0:1, 512:1024]),
                 incs=[(dv, 1)])
            S.op("tensor", lambda e: e.matmul(p_tr[:, :], bufs["ones128"][:, :],
                                              rrow[0:1, 1024:1536], start=True, stop=True),
                 waits=[(dv, dv.v)])
            S.op("tensor", lambda e: e.matmul(p_xs[:, :], bufs["ones128"][:, :],
                                              rrow[0:1, 512:1024], start=True, stop=True))
            pe_bc = emit_drain(S, bufs, p_junk)
            for k in range(8):
                S.op("vector", (lambda k=k: lambda e: e.tensor_mul(
                    tf[:, :], xnf[:, k, :], p_tr[:, :]))(),
                    waits=[(pe, pe_bc)] if k == 0 else [])
                S.op("vector", lambda e: e.tensor_copy(ti[:, :], tf[:, :]))
                S.op("vector", lambda e: e.tensor_copy(td[:, :], ti[:, :]))
                S.op("vector", lambda e: e.tensor_sub(tg[:, :], tf[:, :], td[:, :]))
                S.op("vector", lambda e: e.scalar_tensor_tensor(
                    t2[:, :], tg[:, :], 0.5, td[:, :], ALU.is_ge, ALU.add))
                S.op("vector", lambda e: e.tensor_single_scalar(
                    tg[:, :], tg[:, :], -0.5, ALU.is_le))
                S.op("vector", (lambda k=k: lambda e: e.tensor_sub(
                    xq[:, k, :], t2[:, :], tg[:, :]))(),
                    incs=[(dv, 1)] if k == 7 else [])
            dv_q = dv.v
            for o in range(8):
                for k in range(8):
                    S.op("tensor", (lambda o=o, k=k: lambda e: e.matmul(
                        p_out[:, :], wq[:, k, o * 128:(o + 1) * 128], xq[:, k, :],
                        start=(k == 0), stop=(k == 7)))(),
                        waits=[(dv, dv_q), (gq, ld_w)] if (o == 0 and k == 0) else
                              ([(dv, dv.v)] if k == 0 else []))
                pe_o = emit_drain(S, bufs, p_junk)
                S.op("scalar", (lambda o=o: lambda e: e.activation(
                    out=t2[:, :], in_=p_out[:, :], func=AF.Copy, scale=wsc[:, o:o + 1]))(),
                    waits=[(pe, pe_o)], incs=[(sc, 1)])
                S.op("vector", lambda e: e.tensor_mul(t2[:, :], t2[:, :], p_xs[:, :]),
                     waits=[(sc, sc.v)])
                S.op("vector", (lambda o=o: lambda e: e.tensor_scalar(
                    t2[:, :], t2[:, :], fbc[:, o:o + 1], None, ALU.add))(),)
                S.op("vector", (lambda o=o, tsl=tsl: lambda e: e.tensor_add(
                    X[:, o, tsl], X[:, o, tsl], t2[:, :]))(),
                    incs=[(dv, 1)])
            S.op("tensor", lambda e: None, waits=[(dv, dv.v)])

        x_dst = x2T[:, t0:t0 + T].rearrange("(c p) t -> p c t", p=128)
        S.dma("sync", x_dst, X[:, :, :], waits=[(dv, dv.v)], sem=st)

    S.op("sync", lambda e: None, waits=[(st, st.v)])
    S.build(nc, ctx)
    ctx.close()
    return nc



def build_stage3(serialize=False, chained=False):
    nc = bass.Bass("TRN2", target_bir_lowering=False, debug=False)
    x2T = nc.dram_tensor("x2T", [D, TOK], F32, kind="ExternalInput").ap()
    wqkT3 = nc.dram_tensor("wqkT3", [D, 2 * D], F32, kind="ExternalInput").ap()
    wvT3 = nc.dram_tensor("wvT3", [D, D], F32, kind="ExternalInput").ap()
    woT = nc.dram_tensor("woT", [D, D], F32, kind="ExternalInput").ap()
    nw_in = nc.dram_tensor("nw", [128, 8], F32, kind="ExternalInput").ap()
    bqk_in = nc.dram_tensor("bqk", [1, 2 * D], F32, kind="ExternalInput").ap()
    bv_in = nc.dram_tensor("bv", [1, D], F32, kind="ExternalInput").ap()
    bo_in = nc.dram_tensor("bo", [1, D], F32, kind="ExternalInput").ap()
    idhi = nc.dram_tensor("idhi", [128, 64], F32, kind="ExternalInput").ap()
    cmask_in = nc.dram_tensor("cmask", [128, 896], F32, kind="ExternalInput").ap()
    x3T = nc.dram_tensor("x3T", [D, TOK], F32, kind="ExternalOutput").ap()

    S = Sched(serialize=serialize, chained=chained)
    ctx = ExitStack()
    bufs, sb, ps = alloc_common(S, nc, ctx)
    pe, dv, sc = bufs["pe"], bufs["dv"], bufs["sc"]
    ld, gq, st = bufs["ld"], bufs["gq"], bufs["st"]

    X = sb("X", [128, 8, T])
    xn = sb("xn", [128, 8, T], BF16)
    vaug = sb("vaug", [128, 8 * 16 * 65], BF16)
    qS = sb("qS", [65, 2 * T], BF16)
    kS = sb("kS", [65, 2 * T], BF16)
    Ob = sb("Ob", [128, 4, T], BF16)
    pt = sb("pt", [128, 16 * 512], BF16)
    p1buf = pt[:, 0:2 * 512]
    wshare = sb("wshare", [128, 4096], BF16)
    wv = wshare.rearrange("p (a b) -> p a b", a=8)
    wo = wshare.rearrange("p (a b) -> p a b", a=4)
    wqk3 = sb("wqk", [128, 1, 8, 256], BF16)
    id64 = sb("id64", [128, 64])
    osb = sb("osb", [65, 512])
    rro = sb("rro", [128, 512])
    nw = sb("nw_sb", [128, 8])
    brow = sb("brow", [1, 2 * D], BF16)
    bvrow = sb("bvrow", [1, D], BF16)
    borow = sb("borow", [1, D], BF16)
    onesr = sb("onesr", [1, 512], BF16)
    cmask = sb("cmask_sb", [128, 896], BF16)
    bufs["onesc_bf"] = sb("onesc_bf", [128, 1], BF16)
    bufs["sacc"] = pt[:, 15 * 512:16 * 512]
    t1 = bufs["tmp8"][0:64, :]

    P = [ps(f"P{i}", [128, 512]) for i in range(8)]
    bufs["p_m"], bufs["p_sq"], bufs["p_rb"], bufs["p_mrb"] = P[0], P[1], P[2], P[3]
    bufs["ln_junk"] = P[5][0:1, 0:256]
    p_qk = P[4]
    p_scA, p_scB = P[2], P[3]
    p_sums = P[1]
    p_o = P[4]
    p_rb2 = P[6]
    p_sh = P[5]
    att_junk = P[0][0:1, 0:256]
    p_wo = P[7]

    vaug_vh = vaug.rearrange("p (h t c) -> p h t c", h=8, c=65)
    S.op("vector", lambda e: e.memset(vaug_vh[:, :, :, 64:65], 1.0), incs=[(dv, 1)])
    S.op("vector", lambda e: e.memset(kS[64:65, :], 1.0), incs=[(dv, 1)])
    S.op("vector", lambda e: e.memset(bufs["onesc_bf"][:, :], 1.0), incs=[(dv, 1)])
    S.op("vector", lambda e: e.memset(onesr[:, :], 1.0), incs=[(dv, 1)])
    init_dv = dv.v

    S.dma("sync", id64[:, :], idhi[:, :], sem=ld)
    S.dma("sync", nw[:, :], nw_in[:, :], sem=ld)
    S.dma("gpsimd", cmask[:, :], cmask_in[:, :], sem=gq)
    S.dma("gpsimd", brow[:, :], bqk_in[:, :], sem=gq)
    S.dma("gpsimd", bvrow[:, :], bv_in[:, :], sem=gq)
    S.dma("gpsimd", borow[:, :], bo_in[:, :], sem=gq)
    ld_const = ld.v
    gq_const = gq.v

    for s in range(SEQ_PER_CORE):
        t0 = s * T
        x_src = x2T[:, t0:t0 + T].rearrange("(c p) t -> p c t", p=128)
        S.dma("sync", X[:, :, :], x_src, waits=[(st, st.v)], sem=ld)
        ld_x = ld.v

        dv_xn = 0
        for tc in range(4):
            tsl = slice(tc * 512, (tc + 1) * 512)
            w0 = [(ld, ld_x)]
            if s == 0 and tc == 0:
                w0 += [(dv, init_dv), (ld, ld_const), (gq, gq_const)]
            dv_xn = emit_ln_chunk(S, bufs, X, tsl, w0,
                                  (lambda tsl=tsl: lambda k: xn[:, k, tsl])(),
                                  rms=True, nw=nw)

        for half in range(2):
            S.op("gpsimd", lambda e: None, waits=[(pe, pe.v)])
            ld_wv = S.dma("gpsimd", wv[:, :, :],
                          wvT3[:, half * 512:(half + 1) * 512].rearrange("(c p) f -> p c f", p=128),
                          sem=gq)
            for tt in range(16):
                for k in range(8):
                    S.op("tensor", (lambda tt=tt, k=k: lambda e: e.matmul(
                        p_qk[:, :], xn[:, k, tt * 128:(tt + 1) * 128], wv[:, k, :],
                        start=(k == 0), stop=False))(),
                        waits=([(dv, dv_xn), (gq, ld_wv)] if (tt == 0 and k == 0) else []))
                S.op("tensor", (lambda tt=tt, half=half: lambda e: e.matmul(
                    p_qk[:, :], onesr[0:1, 0:128], bvrow[0:1, half * 512:(half + 1) * 512],
                    start=False, stop=True))())
                pe_v = emit_drain(S, bufs, att_junk)
                for hh in range(8):
                    S.op("vector", (lambda tt=tt, hh=hh: lambda e: e.tensor_copy(
                        vaug_vh[:, hh, tt, 0:64], p_qk[:, hh * 64:(hh + 1) * 64]))(),
                        waits=[(pe, pe_v)] if hh == 0 else [],
                        incs=[(dv, 1)] if hh == 7 else [])
                S.op("tensor", lambda e: None, waits=[(dv, dv.v)])

            for gl in range(4):
                g = half * 4 + gl
                gb = 0
                wsrc = wqkT3[:, g * 256:(g + 1) * 256].rearrange("(c p) f -> p c f", p=128)
                S.op("gpsimd", lambda e: None, waits=[(pe, pe.v)])
                ld_w = S.dma("gpsimd", wqk3[:, 0, :, :], wsrc, sem=gq)

                for dst, c0 in ((qS, 0), (kS, 128)):
                    for ncx in range(4):
                        for k in range(8):
                            S.op("tensor", (lambda ncx=ncx, k=k, c0=c0: lambda e: e.matmul(
                                p_qk[:, :], wqk3[:, 0, k, c0:c0 + 128],
                                xn[:, k, ncx * 512:(ncx + 1) * 512],
                                start=(k == 0), stop=False))(),
                                waits=[(gq, ld_w)] if (k == 0 and ncx == 0 and c0 == 0) else [])
                        S.op("tensor", (lambda g=g, ncx=ncx, c0=c0: lambda e: e.matmul(
                            p_qk[:, :], brow[0:1, g * 256 + c0:g * 256 + c0 + 128],
                            onesr[0:1, :],
                            start=False, stop=True))())
                        pe_qk = emit_drain(S, bufs, att_junk)
                        S.op("vector", (lambda dst=dst, ncx=ncx: lambda e: e.tensor_copy(
                            dst[0:64, ncx * 512:(ncx + 1) * 512], p_qk[0:64, :]))(),
                            waits=[(pe, pe_qk)])
                        S.op("vector", (lambda ncx=ncx: lambda e: e.tensor_copy(
                            bufs["tmp8"][64:128, :], p_qk[64:128, :]))(),
                            incs=[(dv, 1)])
                        S.op("tensor", lambda e: e.matmul(
                            p_sh[0:64, :], id64[64:128, :], bufs["tmp8"][64:128, :],
                            start=True, stop=True),
                            waits=[(dv, dv.v)])
                        pe_sh2 = emit_drain(S, bufs, att_junk)
                        S.op("vector", (lambda dst=dst, ncx=ncx: lambda e: e.tensor_copy(
                            dst[0:64, T + ncx * 512:T + (ncx + 1) * 512], p_sh[0:64, :]))(),
                            waits=[(pe, pe_sh2)], incs=[(dv, 1)])
                        S.op("tensor", lambda e: None, waits=[(dv, dv.v)])

                for sub in range(2):
                    hl = 2 * gl + sub
                    qSh = qS[:, sub * T:(sub + 1) * T]
                    kSh = kS[:, sub * T:(sub + 1) * T]
                    c = hl // 2
                    for qc in range(4):
                        qsl = slice(qc * 512, (qc + 1) * 512)
                        kt_hi = 4 * qc + 4
                        mt = {}
                        for j in range(4):
                            kt = 4 * qc + j
                            cc = kt * 128 - qc * 512
                            mt[kt] = cmask[:, 384 - cc:384 - cc + 512]

                        def X_add(t1v, p_shv, id64v, c=c, qsl=qsl, sub=sub):
                            if sub == 0:
                                S.op("vector", (lambda: lambda e: e.tensor_copy(
                                    Ob[0:64, c, qsl], t1v[:, 0:512]))(),
                                    incs=[(dv, 1)])
                            else:
                                S.op("tensor", (lambda: lambda e: e.matmul(
                                    p_shv[64:128, :], id64v[0:64, :],
                                    t1v[:, 0:512], start=True, stop=True))(),
                                    waits=[(dv, dv.v)])
                                pe_sh = emit_drain(S, bufs, att_junk)
                                S.op("vector", (lambda: lambda e: e.tensor_copy(
                                    Ob[64:128, c, qsl], p_shv[64:128, :]))(),
                                    waits=[(pe, pe_sh)], incs=[(dv, 1)])

                        emit_attn_head(S, bufs, qS=qSh, kS=kSh,
                                       vaug_v=vaug_vh[:, hl, :, :],
                                       X_add=X_add, h=hl, qsl=qsl, kt_lo=0, kt_hi=kt_hi,
                                       pt=pt, p1buf=p1buf, p_scA=p_scA, p_scB=p_scB,
                                       p_sums=p_sums, p_o=p_o, p_rb2=p_rb2, p_sh=p_sh,
                                       osb=osb, rro=rro, t1=t1, id64=id64,
                                       att_junk=att_junk, mask_tiles=mt)

            S.op("gpsimd", lambda e: None, waits=[(pe, pe.v)])
            ld_wo = S.dma("gpsimd", wo[:, :, :],
                          woT[half * 512:(half + 1) * 512, :].rearrange("(c p) f -> p c f", p=128),
                          sem=gq)
            dv_ob = dv.v
            for qc in range(4):
                qsl = slice(qc * 512, (qc + 1) * 512)
                for o in range(8):
                    for k in range(4):
                        S.op("tensor", (lambda o=o, k=k, qsl=qsl: lambda e: e.matmul(
                            p_wo[:, :], wo[:, k, o * 128:(o + 1) * 128], Ob[:, k, qsl],
                            start=(k == 0), stop=(half == 1 or k < 3)))(),
                            waits=[(gq, ld_wo), (dv, dv_ob)] if (qc == 0 and o == 0 and k == 0) else [])
                    if half == 0:
                        S.op("tensor", (lambda o=o: lambda e: e.matmul(
                            p_wo[:, :], borow[0:1, o * 128:(o + 1) * 128], onesr[0:1, :],
                            start=False, stop=True))())
                    pe_wo = emit_drain(S, bufs, att_junk)
                    if half == 0:
                        S.op("vector", (lambda o=o, qsl=qsl: lambda e: e.tensor_add(
                            bufs["tmp8"][:, :], p_wo[:, :], xn[:, o, qsl]))(),
                            waits=[(pe, pe_wo)])
                    else:
                        S.op("vector", (lambda o=o, qsl=qsl: lambda e: e.tensor_copy(
                            bufs["tmp8"][:, :], p_wo[:, :]))(),
                            waits=[(pe, pe_wo)])
                    S.op("vector", (lambda o=o, qsl=qsl: lambda e: e.tensor_add(
                        X[:, o, qsl], X[:, o, qsl], bufs["tmp8"][:, :]))(),
                        incs=[(dv, 1)])
                    S.op("tensor", lambda e: None, waits=[(dv, dv.v)])

        x_dst = x3T[:, t0:t0 + T].rearrange("(c p) t -> p c t", p=128)
        S.dma("sync", x_dst, X[:, :, :], waits=[(dv, dv.v)], sem=st)

    S.op("sync", lambda e: None, waits=[(st, st.v)])
    S.build(nc, ctx)
    ctx.close()
    return nc



def build_stage4(serialize=False, chained=False, ser_set=None):
    nc = bass.Bass("TRN2", target_bir_lowering=False, debug=False)
    x3T = nc.dram_tensor("x3T", [D, TOK], F32, kind="ExternalInput").ap()
    gwT = nc.dram_tensor("gwT", [D, E], F32, kind="ExternalInput").ap()
    gbrow = nc.dram_tensor("gbrow", [1, E], F32, kind="ExternalInput").ap()
    ic64_in = nc.dram_tensor("ic64", [128, E], F32, kind="ExternalInput").ap()
    ut128_in = nc.dram_tensor("ut128", [128, 128], F32, kind="ExternalInput").ap()
    idb_in = nc.dram_tensor("idb", [128, 128], BF16, kind="ExternalInput").ap()
    xfT = nc.dram_tensor("xfT", [D, TOK], BF16, kind="ExternalOutput").ap()
    xfr = nc.dram_tensor("xfr", [TOK, D], BF16, kind="ExternalOutput").ap()
    cnt = nc.dram_tensor("cnt", [1, E], F32, kind="ExternalOutput").ap()
    rinfo = nc.dram_tensor("rinfo", [128, 32, 18], F32, kind="ExternalOutput").ap()

    S = Sched(serialize=serialize, chained=chained, ser_set=ser_set)
    ctx = ExitStack()
    bufs, sb, ps = alloc_common(S, nc, ctx)
    pe, dv, sc = bufs["pe"], bufs["dv"], bufs["sc"]
    ld, gq, st = bufs["ld"], bufs["gq"], bufs["st"]

    X = sb("X", [128, 8, T])
    xf = sb("xf", [128, 8, T], BF16)
    gw = sb("gw", [128, 8, E], BF16)
    gbb = sb("gbb", [128, E])
    ic64 = sb("ic64_sb", [128, E])
    ut128 = sb("ut128_sb", [128, 128])
    idb = sb("idb_sb", [128, 128], BF16)
    bufs["tmp8b"] = sb("tmp8b", [128, 1024], BF16)
    sco = sb("sco", [128, E])
    sp = sb("sp", [128, E])
    oh = sb("oh", [128, E])
    cmat = sb("cmat", [128, E])
    pplus = sb("pplus", [128, E])
    top8 = sb("top8", [128, 8])
    run = sb("run", [1, E])
    info = sb("info", [128, 32, 18])
    junkrow = sb("junkrow", [128, E])
    rowmax = sb("rowmax", [128, 2])
    jnkw = sb("jnkw", [128, 256])

    def _spw(inc=False):
        S.op("vector", lambda e: e.tensor_copy(jnkw[:, 0:128], jnkw[:, 128:256]),
             incs=[(dv, 1)] if inc else [])

    P = [ps(f"P{i}", [128, 512]) for i in range(7)]
    bufs["p_m"], bufs["p_sq"], bufs["p_rb"], bufs["p_mrb"] = P[0], P[1], P[2], P[3]
    bufs["ln_junk"] = P[5][0:1, 0:256]
    p_g = P[4]
    p_off = P[5]
    p_tot = P[6]
    att_junk = P[0][0:1, 0:256]
    S.ser_junk = P[6][0:1, 256:512]
    S.ser_ones1 = bufs["ones1"][0:1, 0:1]
    S.ser_onest = bufs["onest"][0:1, 0:256]
    p_trb = ctx.enter_context(nc.psum_tensor("p_trb", [128, 1024], BF16))

    S.op("vector", lambda e: e.memset(run[:, :], 0.0), incs=[(dv, 1)])
    init_dv0 = dv.v

    S.dma("gpsimd", gw[:, :, :], gwT.rearrange("(c p) f -> p c f", p=128), sem=gq)
    S.dma("sync", ic64[:, :], ic64_in[:, :], sem=ld)
    S.dma("sync", ut128[:, :], ut128_in[:, :], sem=ld)
    S.dma("sync", idb[:, :], idb_in[:, :], sem=ld)
    ld_const = ld.v
    gq_const = gq.v
    gbr = sb("gbr", [1, E])
    S.dma("sync", gbr[:, :], gbrow[:, :], sem=ld)
    ld_gb = ld.v
    S.op("tensor", lambda e: e.matmul(p_g[:, 0:E], bufs["ones128"][:, :], gbr[:, :],
                                      start=True, stop=True),
         waits=[(ld, ld_gb), (dv, init_dv0)])
    pe_gb = emit_drain(S, bufs, att_junk)
    S.op("vector", lambda e: e.tensor_copy(gbb[:, :], p_g[:, 0:E]),
         waits=[(pe, pe_gb)], incs=[(dv, 1)])
    init_dv = dv.v

    for s in range(SEQ_PER_CORE):
        t0 = s * T
        x_src = x3T[:, t0:t0 + T].rearrange("(c p) t -> p c t", p=128)
        S.dma("sync", X[:, :, :], x_src, waits=[(st, st.v)], sem=ld)
        ld_x = ld.v

        dv_xf = emit_layernorm(S, nc, bufs, X, xf, T,
                               [(ld, ld_x)] + ([(dv, init_dv), (ld, ld_const)] if s == 0 else []))
        S.dma("sync", xfT[:, t0:t0 + T].rearrange("(c p) t -> p c t", p=128),
              xf[:, :, :], waits=[(dv, dv_xf)], sem=st)

        for tt in range(16):
            gt = s * 16 + tt
            for k in range(8):
                S.op("tensor", (lambda k=k, tt=tt: lambda e: e.transpose(
                    p_trb[:, k * 128:(k + 1) * 128], xf[:, k, tt * 128:(tt + 1) * 128],
                    idb[:, :]))(),
                    waits=[(dv, dv_xf)] if (k == 0) else [])
            pe_tr = emit_drain(S, bufs, att_junk)
            S.op("vector", lambda e: e.tensor_copy(bufs["tmp8b"][:, :], p_trb[:, :]),
                 waits=[(pe, pe_tr), (st, st.v)], incs=[(dv, 1)])
            S.dma("sync", xfr[t0 + tt * 128:t0 + (tt + 1) * 128, :],
                  bufs["tmp8b"][:, :], waits=[(dv, dv.v)], sem=st)
            dv_tr = dv.v

            for k in range(8):
                S.op("tensor", (lambda k=k, tt=tt: lambda e: e.matmul(
                    p_g[:, 0:E], xf[:, k, tt * 128:(tt + 1) * 128], gw[:, k, :],
                    start=(k == 0), stop=(k == 7)))(),
                    waits=([(dv, dv.v)] + ([(gq, gq_const)] if (s == 0 and tt == 0) else [])) if k == 0 else [])
            pe_g = emit_drain(S, bufs, att_junk)
            S.op("vector", lambda e: e.tensor_copy(junkrow[:, 0:8], p_g[:, 0:8]),
                 waits=[(pe, pe_g)])
            S.op("vector", lambda e: e.tensor_reduce(rowmax[:, 0:1], p_g[:, 0:E],
                                                     AX.X, ALU.max))
            _spw(); _spw()
            S.op("vector", lambda e: e.tensor_single_scalar(rowmax[:, 1:2], rowmax[:, 0:1],
                                                            -1.0, ALU.mult),
                 incs=[(dv, 1)])
            _spw(inc=True)
            _spw(inc=True)
            S.op("scalar", lambda e: e.activation(out=sco[:, :], in_=p_g[:, 0:E],
                                                  func=AF.Exp, bias=rowmax[:, 1:2],
                                                  accum_out=rowmax[:, 0:1]),
                 waits=[(dv, dv.v)])
            S.op("scalar", lambda e: e.activation(out=jnkw[:, 0:64], in_=jnkw[:, 64:128],
                                                  func=AF.Identity))
            S.op("scalar", lambda e: e.activation(out=jnkw[:, 0:64], in_=jnkw[:, 64:128],
                                                  func=AF.Identity), incs=[(sc, 1)])
            S.op("vector", lambda e: e.reciprocal(rowmax[:, 1:2], rowmax[:, 0:1]),
                 waits=[(sc, sc.v)])
            _spw(); _spw()
            S.op("vector", lambda e: e.tensor_scalar(sco[:, :], sco[:, :],
                                                     rowmax[:, 1:2], None, ALU.mult))
            _spw(); _spw()
            S.op("vector", lambda e: e.tensor_add(sp[:, :], sco[:, :], gbb[:, :]),
                 incs=[(dv, 1)])
            _spw(); _spw()
            S.op("vector", lambda e: e.max(out=top8[:, :], in_=sp[:, :]))
            _spw(); _spw(); _spw()
            for k6 in range(TOPK):
                S.op("vector", (lambda k6=k6: lambda e: e.tensor_scalar(
                    oh[:, :], sp[:, :], top8[:, k6:k6 + 1], None, ALU.is_equal))())
                _spw(); _spw()
                S.op("vector", (lambda k6=k6, gt=gt: lambda e: e.scalar_tensor_tensor(
                    junkrow[:, :], sco[:, :], 1.0, oh[:, :], ALU.mult, ALU.mult,
                    accum_out=info[:, gt, 6 + k6:7 + k6]))())
                S.op("vector", (lambda k6=k6, gt=gt: lambda e: e.scalar_tensor_tensor(
                    junkrow[:, :], ic64[:, :], 1.0, oh[:, :], ALU.mult, ALU.mult,
                    accum_out=info[:, gt, k6:k6 + 1]))())
                if k6 == 0:
                    S.op("vector", lambda e: e.tensor_copy(cmat[:, :], oh[:, :]))
                else:
                    S.op("vector", lambda e: e.tensor_add(cmat[:, :], cmat[:, :], oh[:, :]),
                         incs=[(dv, 1)] if k6 == TOPK - 1 else [])
            dv_c = dv.v
            S.op("tensor", lambda e: e.matmul(p_g[:, 0:E], ut128[:, :], cmat[:, :],
                                              start=True, stop=True),
                 waits=[(dv, dv_c)])
            S.op("tensor", lambda e: e.matmul(p_tot[0:1, 0:E], bufs["ones1"][:, :],
                                              cmat[:, :], start=True, stop=True))
            S.op("tensor", lambda e: e.matmul(p_off[:, 0:E], bufs["ones128"][:, :],
                                              run[:, :], start=True, stop=True))
            pe_pr = emit_drain(S, bufs, att_junk)
            S.op("vector", lambda e: e.tensor_copy(pplus[:, :], p_g[:, 0:E]),
                 waits=[(pe, pe_pr)])
            _spw(); _spw()
            S.op("vector", lambda e: e.tensor_add(pplus[:, :], pplus[:, :], p_off[:, 0:E]))
            S.op("vector", lambda e: e.tensor_add(run[:, :], run[:, :], p_tot[0:1, 0:E]),
                 incs=[(dv, 1)])
            _spw(); _spw()
            for k6 in range(TOPK):
                S.op("vector", (lambda k6=k6: lambda e: e.tensor_scalar(
                    oh[:, :], sp[:, :], top8[:, k6:k6 + 1], None, ALU.is_equal))())
                _spw(); _spw()
                S.op("vector", (lambda k6=k6, gt=gt: lambda e: e.scalar_tensor_tensor(
                    junkrow[:, :], pplus[:, :], 1.0, oh[:, :], ALU.mult, ALU.mult,
                    accum_out=info[:, gt, 12 + k6:13 + k6]))(),
                    incs=[(dv, 1)] if k6 == TOPK - 1 else [])
            S.op("tensor", lambda e: None, waits=[(dv, dv.v)])

    _spw(); _spw(); _spw(inc=True)
    S.dma("sync", cnt[:, :], run[:, :], waits=[(bufs["dv"], bufs["dv"].v)], sem=st)
    S.dma("sync", rinfo[:, :, :], info[:, :, :], sem=st)
    S.op("sync", lambda e: None, waits=[(st, st.v)])
    S.build(nc, ctx)
    ctx.close()
    return nc



def build_stage4b(serialize=False, chained=False):
    nc = bass.Bass("TRN2", target_bir_lowering=False, debug=False)
    cnts = nc.dram_tensor("cnts", [8, E], F32, kind="ExternalInput").ap()
    rinfo = nc.dram_tensor("rinfo", [128, 32, 18], F32, kind="ExternalInput").ap()
    csel = nc.dram_tensor("csel", [8, 1], F32, kind="ExternalInput").ap()
    i512_in = nc.dram_tensor("i512", [128, 512], F32, kind="ExternalInput").ap()
    tbase_in = nc.dram_tensor("tbase", [128, 1], F32, kind="ExternalInput").ap()
    ic64_in = nc.dram_tensor("ic64", [128, E], F32, kind="ExternalInput").ap()
    mpart = nc.dram_tensor("mpart", [E, 512], F32, kind="ExternalOutput").ap()
    sinfo = nc.dram_tensor("sinfo", [128, 32, 12], F32, kind="ExternalOutput").ap()

    S = Sched(serialize=serialize, chained=chained)
    ctx = ExitStack()
    bufs, sb, ps = alloc_common(S, nc, ctx)
    pe, dv, sc = bufs["pe"], bufs["dv"], bufs["sc"]
    ld, gq, st = bufs["ld"], bufs["gq"], bufs["st"]

    cntb = sb("cntb", [8, E])
    cselb = sb("cselb", [8, 1])
    i512 = sb("i512_sb", [128, 512])
    tbase = sb("tbase_sb", [128, 1])
    ic64 = sb("ic64_sb", [128, E])
    info = sb("info", [128, 32, 18])
    sout = sb("sout", [128, 32, 12])
    offb = sb("offb", [128, E])
    oh = sb("oh", [128, E])
    ohp = sb("ohp", [128, 2, 512])
    oht = sb("oht", [128, 2, E])
    tg = sb("tg", [128, 1])
    posg = sb("posg", [128, 1])
    lt = sb("lt", [128, 1])
    junkrow = sb("junkrow", [128, E])
    msb = sb("msb", [64, 512])
    jnkw = sb("jnkw", [128, 256])

    def _spw(inc=False):
        S.op("vector", lambda e: e.tensor_copy(jnkw[:, 0:128], jnkw[:, 128:256]),
             incs=[(dv, 1)] if inc else [])

    P = [ps(f"P{i}", [128, 512]) for i in range(4)]
    p_off = P[0]
    p_m = P[1]
    att_junk = P[2][0:1, 0:256]
    S.ser_junk = P[3][0:1, 0:256]
    S.ser_ones1 = bufs["ones1"][0:1, 0:1]
    S.ser_onest = bufs["onest"][0:1, 0:256]

    S.dma("sync", cntb[:, :], cnts[:, :], sem=ld)
    S.dma("sync", cselb[:, :], csel[:, :], sem=ld)
    S.dma("sync", i512[:, :], i512_in[:, :], sem=ld)
    S.dma("sync", tbase[:, :], tbase_in[:, :], sem=ld)
    S.dma("sync", ic64[:, :], ic64_in[:, :], sem=ld)
    S.dma("sync", info[:, :, :], rinfo[:, :, :], sem=ld)
    ld_c = ld.v

    S.op("tensor", lambda e: e.matmul(p_off[0:1, 0:E], cselb[:, :], cntb[:, :],
                                      start=True, stop=True),
         waits=[(ld, ld_c)])
    pe1 = emit_drain(S, bufs, att_junk)
    S.op("vector", lambda e: e.tensor_copy(bufs["tmp8"][0:1, 0:E], p_off[0:1, 0:E]),
         waits=[(pe, pe1)], incs=[(dv, 1)])
    S.op("tensor", lambda e: e.matmul(p_off[:, 0:E], bufs["ones128"][:, :],
                                      bufs["tmp8"][0:1, 0:E], start=True, stop=True),
         waits=[(dv, dv.v)])
    pe2 = emit_drain(S, bufs, att_junk)
    S.op("vector", lambda e: e.tensor_copy(offb[:, :], p_off[:, 0:E]),
         waits=[(pe, pe2)], incs=[(dv, 1)])
    dv_off = dv.v

    first = True
    pecnt = []
    j = 0
    for gt in range(32):
        S.op("vector", (lambda gt=gt: lambda e: e.tensor_single_scalar(
            tg[:, :], tbase[:, :], float(gt * 128), ALU.add))(),
            waits=[(dv, dv_off)] if first else [])
        _spw(); _spw()
        for k6 in range(TOPK):
            jb = j % 2
            S.op("vector", (lambda gt=gt, k6=k6: lambda e: e.tensor_scalar(
                oh[:, :], ic64[:, :], info[:, gt, k6:k6 + 1], None, ALU.is_equal))())
            _spw(); _spw()
            S.op("vector", (lambda: lambda e: e.scalar_tensor_tensor(
                junkrow[:, :], offb[:, :], 1.0, oh[:, :], ALU.mult, ALU.mult,
                accum_out=posg[:, :]))())
            _spw(); _spw()
            S.op("vector", (lambda gt=gt, k6=k6: lambda e: e.tensor_scalar(
                posg[:, :], posg[:, :], info[:, gt, 12 + k6:13 + k6], None, ALU.add))())
            _spw(); _spw()
            S.op("vector", lambda e: e.tensor_single_scalar(lt[:, :], posg[:, :], 512.0, ALU.is_lt))
            _spw(); _spw()
            S.op("vector", (lambda gt=gt, k6=k6: lambda e: e.tensor_scalar(
                sout[:, gt, k6:k6 + 1], lt[:, :], info[:, gt, 6 + k6:7 + k6], None, ALU.mult))())
            S.op("vector", (lambda gt=gt, k6=k6: lambda e: e.tensor_single_scalar(
                bufs["tmp8"][:, 0:1], info[:, gt, k6:k6 + 1], 512.0, ALU.mult))())
            _spw(); _spw()
            S.op("vector", lambda e: e.tensor_add(bufs["tmp8"][:, 0:1], bufs["tmp8"][:, 0:1], posg[:, :]))
            S.op("vector", lambda e: e.tensor_single_scalar(lt[:, :], lt[:, :], -1.0, ALU.add))
            _spw(); _spw()
            S.op("vector", lambda e: e.tensor_single_scalar(lt[:, :], lt[:, :], -1e7, ALU.mult))
            _spw(); _spw()
            S.op("vector", (lambda gt=gt, k6=k6: lambda e: e.tensor_add(
                sout[:, gt, 6 + k6:7 + k6], bufs["tmp8"][:, 0:1], lt[:, :]))())
            war = [(pe, pecnt[j - 2])] if j >= 2 else []
            S.op("vector", (lambda jb=jb: lambda e: e.tensor_scalar(
                oht[:, jb, :], oh[:, :], tg[:, :], None, ALU.mult))(), waits=war)
            S.op("vector", (lambda jb=jb: lambda e: e.tensor_scalar(
                ohp[:, jb, :], i512[:, :], posg[:, :], None, ALU.is_equal))(),
                incs=[(dv, 1)])
            S.op("tensor", (lambda gt=gt, k6=k6, jb=jb: lambda e: e.matmul(
                p_m[0:64, :], oht[:, jb, :], ohp[:, jb, :],
                start=(gt == 0 and k6 == 0), stop=(gt == 31 and k6 == TOPK - 1)))(),
                waits=[(dv, dv.v)], incs=[(pe, 1)])
            pecnt.append(pe.v)
            j += 1
        first = False
    pe_m = emit_drain(S, bufs, att_junk)
    S.op("vector", lambda e: e.tensor_copy(msb[:, :], p_m[0:64, :]),
         waits=[(pe, pe_m)], incs=[(dv, 1)])
    _spw(); _spw(inc=True)
    S.dma("sync", mpart[:, :], msb[:, :], waits=[(dv, dv.v)], sem=st)
    S.dma("sync", sinfo[:, :, :], sout[:, :, :], waits=[(dv, dv.v)], sem=st)
    S.op("sync", lambda e: None, waits=[(st, st.v)])
    S.build(nc, ctx)
    ctx.close()
    return nc


I_PAD = 768


def build_stage5(serialize=False, chained=False):
    nc = bass.Bass("TRN2", target_bir_lowering=False, debug=False)
    mloc = nc.dram_tensor("mloc", [8, 512, 8], F32, kind="ExternalInput").ap()
    xf_full = nc.dram_tensor("xf_full", [B * T, D], BF16, kind="ExternalInput").ap()
    w1T = nc.dram_tensor("w1T", [8, D, I_PAD], F32, kind="ExternalInput").ap()
    w3T = nc.dram_tensor("w3T", [8, D, I_PAD], F32, kind="ExternalInput").ap()
    w2T = nc.dram_tensor("w2T", [8, I_PAD, D], F32, kind="ExternalInput").ap()
    idb_in = nc.dram_tensor("idb", [128, 128], BF16, kind="ExternalInput").ap()
    yb = nc.dram_tensor("yb", [8 * 512, D], BF16, kind="ExternalOutput").ap()

    S = Sched(serialize=serialize, chained=chained)
    ctx = ExitStack()
    bufs, sb, ps = alloc_common(S, nc, ctx)
    pe, dv, sc = bufs["pe"], bufs["dv"], bufs["sc"]
    ld, gq, st = bufs["ld"], bufs["gq"], bufs["st"]

    NI = I_PAD // 128
    marr = sb("marr", [128, 4, 8])
    midx = sb("midx", [128, 4], I32)
    mfl = sb("mfl", [128, 4])
    xb = sb("xb", [128, 4, D], BF16)
    xbT = sb("xbT", [128, 8, 512], BF16)
    w1b = sb("w1b", [128, 2, 8, I_PAD], BF16)
    w3b = sb("w3b", [128, 2, 8, I_PAD], BF16)
    w2b = sb("w2b", [128, 2, NI, D], BF16)
    us = sb("us", [128, NI, 512], BF16)
    vs = sb("vs", [128, NI, 512], BF16)
    idb = sb("idb_sb", [128, 128], BF16)
    ybt = sb("ybt", [128, 2, 512], BF16)
    jnk5 = sb("jnk5", [128, 256])

    P = [ps(f"P{i}", [128, 512]) for i in range(6)]
    p_u = P[0]
    p_v = P[1]
    p_y = P[2]
    att_junk = P[3][0:1, 0:256]
    p_trb = ctx.enter_context(nc.psum_tensor("p_trb", [128, 1024], BF16))

    S.dma("sync", idb[:, :], idb_in[:, :], sem=ld)
    ld_id = ld.v

    prev_gather = 0
    for e_i in range(8):
        eb = e_i % 2
        if e_i >= 2:
            S.op("gpsimd", lambda e: None, waits=[(pe, pe.v)])
        l1 = S.dma("gpsimd", w1b[:, eb, :, :],
                   w1T[e_i, :, :].rearrange("(c p) f -> p c f", p=128), sem=gq)
        l3 = S.dma("gpsimd", w3b[:, eb, :, :],
                   w3T[e_i, :, :].rearrange("(c p) f -> p c f", p=128), sem=gq)
        l2 = S.dma("gpsimd", w2b[:, eb, :, :],
                   w2T[e_i, :, :].rearrange("(c p) f -> p c f", p=128), sem=gq)

        S.dma("sync", marr[:, :, :],
              mloc[e_i, :, :].rearrange("(c p) k -> p c k", p=128),
              waits=[(dv, dv.v)], sem=ld)
        def _sp():
            S.op("vector", lambda e: e.tensor_copy(jnk5[:, 0:128], jnk5[:, 128:256]))
        S.op("vector", lambda e: e.tensor_reduce(mfl[:, :], marr[:, :, :], AX.X, ALU.add),
             waits=[(ld, ld.v)])
        _sp(); _sp()
        S.op("vector", lambda e: e.tensor_single_scalar(mfl[:, :], mfl[:, :], -1.0, ALU.add))
        _sp(); _sp()
        S.op("vector", lambda e: e.tensor_scalar_max(mfl[:, :], mfl[:, :], 0.0))
        _sp(); _sp()
        S.op("vector", lambda e: e.tensor_copy(midx[:, :], mfl[:, :]),
             waits=[(gq, prev_gather)] if prev_gather else [])
        _sp(); _sp()
        S.op("vector", lambda e: e.tensor_copy(jnk5[:, 0:128], jnk5[:, 128:256]),
             incs=[(dv, 1)])
        dv_idx = dv.v
        for c in range(4):
            S.op("gpsimd", (lambda c=c: lambda e: e.indirect_dma_start(
                out=xb[:, c, :], out_offset=None,
                in_=xf_full[:, :], in_offset=IndirectOffsetOnAxis(midx[:, c:c + 1], 0),
                bounds_check=B * T - 1, oob_is_err=False))(),
                waits=[(dv, dv_idx)] if c == 0 else [], incs=[(gq, 16)])
        gq_gather = gq.v
        prev_gather = gq_gather
        for c in range(4):
            for k in range(8):
                S.op("tensor", (lambda c=c, k=k: lambda e: e.transpose(
                    p_trb[:, k * 128:(k + 1) * 128], xb[:, c, k * 128:(k + 1) * 128],
                    idb[:, :]))(),
                    waits=[(gq, gq_gather), (ld, ld_id)] if (c == 0 and k == 0) else [])
            pe_tr = emit_drain(S, bufs, att_junk)
            S.op("vector", (lambda c=c: lambda e: e.tensor_copy(
                xbT[:, :, c * 128:(c + 1) * 128],
                p_trb[:, :].rearrange("p (a b) -> p a b", a=8)))(),
                waits=[(pe, pe_tr)], incs=[(dv, 1)])
            S.op("tensor", lambda e: None, waits=[(dv, dv.v)])
        dv_xbT = dv.v

        for i in range(NI):
            for k in range(8):
                S.op("tensor", (lambda i=i, k=k, eb=eb: lambda e: e.matmul(
                    p_u[:, :], w1b[:, eb, k, i * 128:(i + 1) * 128], xbT[:, k, :],
                    start=(k == 0), stop=(k == 7)))(),
                    waits=[(dv, dv_xbT), (gq, l3)] if (i == 0 and k == 0) else [])
                S.op("tensor", (lambda i=i, k=k, eb=eb: lambda e: e.matmul(
                    p_v[:, :], w3b[:, eb, k, i * 128:(i + 1) * 128], xbT[:, k, :],
                    start=(k == 0), stop=(k == 7)))())
            pe_uv = emit_drain(S, bufs, att_junk)
            S.op("scalar", (lambda i=i: lambda e: e.activation(
                out=us[:, i, :], in_=p_u[:, :], func=AF.Silu))(),
                waits=[(pe, pe_uv)], incs=[(sc, 1)])
            S.op("vector", (lambda i=i: lambda e: e.tensor_copy(vs[:, i, :], p_v[:, :]))(),
                 waits=[(pe, pe_uv)])
            S.op("vector", (lambda i=i: lambda e: e.tensor_mul(
                us[:, i, :], us[:, i, :], vs[:, i, :]))(),
                waits=[(sc, sc.v)], incs=[(dv, 1)])
            S.op("tensor", lambda e: None, waits=[(dv, dv.v)])
        dv_h = dv.v

        for c in range(4):
            for dh in range(2):
                for i in range(NI):
                    S.op("tensor", (lambda c=c, dh=dh, i=i, eb=eb: lambda e: e.matmul(
                        p_y[:, :], us[:, i, c * 128:(c + 1) * 128],
                        w2b[:, eb, i, dh * 512:(dh + 1) * 512],
                        start=(i == 0), stop=(i == NI - 1)))(),
                        waits=[(dv, dv_h), (gq, l2)] if (c == 0 and dh == 0 and i == 0) else [])
                pe_y = emit_drain(S, bufs, att_junk)
                S.op("vector", (lambda dh=dh: lambda e: e.tensor_copy(
                    ybt[:, dh, :], p_y[:, :]))(),
                    waits=[(pe, pe_y), (st, st.v)], incs=[(dv, 1)])
                S.op("tensor", lambda e: None, waits=[(dv, dv.v)])
            S.dma("sync", yb[e_i * 512 + c * 128:e_i * 512 + (c + 1) * 128, :],
                  ybt[:, :, :].rearrange("p a b -> p (a b)"), waits=[(dv, dv.v)], sem=st)

    S.op("sync", lambda e: None, waits=[(st, st.v)])
    S.build(nc, ctx)
    ctx.close()
    return nc



def build_stage6(serialize=False, chained=False):
    nc = bass.Bass("TRN2", target_bir_lowering=False, debug=False)
    x3T = nc.dram_tensor("x3T", [D, TOK], F32, kind="ExternalInput").ap()
    xfT_in = nc.dram_tensor("xfT", [D, TOK], BF16, kind="ExternalInput").ap()
    yb_full = nc.dram_tensor("yb_full", [B * T, D], BF16, kind="ExternalInput").ap()
    sinfo_in = nc.dram_tensor("sinfo", [128, 32, 12], F32, kind="ExternalInput").ap()
    ws1T = nc.dram_tensor("ws1T", [D, SI], F32, kind="ExternalInput").ap()
    ws3T = nc.dram_tensor("ws3T", [D, SI], F32, kind="ExternalInput").ap()
    ws2T = nc.dram_tensor("ws2T", [SI, D], F32, kind="ExternalInput").ap()
    idf_in = nc.dram_tensor("idf", [128, 128], F32, kind="ExternalInput").ap()
    outr = nc.dram_tensor("outr", [TOK, D], F32, kind="ExternalOutput").ap()

    S = Sched(serialize=serialize, chained=chained)
    ctx = ExitStack()
    bufs, sb, ps = alloc_common(S, nc, ctx)
    pe, dv, sc = bufs["pe"], bufs["dv"], bufs["sc"]
    ld, gq, st = bufs["ld"], bufs["gq"], bufs["st"]

    NS = SI // 128
    X = sb("X", [128, 8, T])
    xf = sb("xf", [128, 8, T], BF16)
    w1b = sb("w1b", [128, 8, SI], BF16)
    w3b = sb("w3b", [128, 8, SI], BF16)
    w2b = sb("w2b", [128, NS, D], BF16)
    us = sb("us", [128, NS, 512], BF16)
    sinfo = sb("sinfo_sb", [128, 32, 12])
    sidx = sb("sidx", [128, 8], I32)
    gbuf = sb("gbuf", [128, 2, D], BF16)
    acc = sb("acc", [128, D])
    idf = sb("idf_sb", [128, 128])
    jnk = sb("jnk", [128, 256])

    P = [ps(f"P{i}", [128, 512]) for i in range(6)]
    p_u = P[0]
    p_v = P[1]
    p_z = P[2]
    p_x = P[3]
    att_junk = P[4][0:1, 0:256]

    S.dma("sync", sinfo[:, :, :], sinfo_in[:, :, :], sem=ld)
    S.dma("sync", idf[:, :], idf_in[:, :], sem=ld)
    l1 = S.dma("gpsimd", w1b[:, :, :], ws1T.rearrange("(c p) f -> p c f", p=128), sem=gq)
    l3 = S.dma("gpsimd", w3b[:, :, :], ws3T.rearrange("(c p) f -> p c f", p=128), sem=gq)
    l2 = S.dma("gpsimd", w2b[:, :, :], ws2T.rearrange("(c p) f -> p c f", p=128), sem=gq)
    ld_c = ld.v
    S.op("vector", lambda e: e.memset(gbuf[:, :, :], 0.0), incs=[(dv, 1)])
    init_dv = dv.v
    _bh = {}

    def _breg(e):
        if "r" not in _bh:
            _bh["r"] = e.alloc_register("bnd")
            e.reg_mov(_bh["r"], B * T - 1)
        return _bh["r"]


    for s in range(SEQ_PER_CORE):
        t0 = s * T
        S.dma("sync", X[:, :, :], x3T[:, t0:t0 + T].rearrange("(c p) t -> p c t", p=128),
              waits=[(st, st.v)], sem=ld)
        S.dma("sync", xf[:, :, :], xfT_in[:, t0:t0 + T].rearrange("(c p) t -> p c t", p=128),
              sem=ld)
        ld_x = ld.v

        for tc in range(4):
            tsl = slice(tc * 512, (tc + 1) * 512)
            for i in range(NS):
                for k in range(8):
                    S.op("tensor", (lambda i=i, k=k, tsl=tsl: lambda e: e.matmul(
                        p_u[:, :], w1b[:, k, i * 128:(i + 1) * 128], xf[:, k, tsl],
                        start=(k == 0), stop=(k == 7)))(),
                        waits=[(ld, ld_x), (gq, l3), (dv, init_dv)] if (i == 0 and k == 0 and tc == 0 and s == 0) else
                        ([(ld, ld_x), (dv, dv.v)] if (i == 0 and k == 0) else []))
                    S.op("tensor", (lambda i=i, k=k, tsl=tsl: lambda e: e.matmul(
                        p_v[:, :], w3b[:, k, i * 128:(i + 1) * 128], xf[:, k, tsl],
                        start=(k == 0), stop=(k == 7)))())
                pe_uv = emit_drain(S, bufs, att_junk)
                S.op("scalar", (lambda i=i: lambda e: e.activation(
                    out=us[:, i, :], in_=p_u[:, :], func=AF.Silu))(),
                    waits=[(pe, pe_uv)], incs=[(sc, 1)])
                S.op("vector", (lambda i=i: lambda e: e.tensor_mul(
                    us[:, i, :], us[:, i, :], p_v[:, :]))(),
                    waits=[(sc, sc.v)], incs=[(dv, 1)])
                S.op("tensor", lambda e: None, waits=[(dv, dv.v)])
            dv_h = dv.v

            for ttl in range(4):
                gt = s * 16 + tc * 4 + ttl
                for dh in range(2):
                    for i in range(NS):
                        S.op("tensor", (lambda ttl=ttl, dh=dh, i=i: lambda e: e.matmul(
                            p_z[:, :], us[:, i, ttl * 128:(ttl + 1) * 128],
                            w2b[:, i, dh * 512:(dh + 1) * 512],
                            start=(i == 0), stop=(i == NS - 1)))(),
                            waits=[(dv, dv_h), (gq, l2)] if (ttl == 0 and dh == 0 and i == 0) else [])
                    pe_z = emit_drain(S, bufs, att_junk)
                    S.op("vector", (lambda dh=dh: lambda e: e.tensor_copy(
                        acc[:, dh * 512:(dh + 1) * 512], p_z[:, :]))(),
                        waits=[(pe, pe_z)], incs=[(dv, 1)])
                    S.op("tensor", lambda e: None, waits=[(dv, dv.v)])
                for k in range(8):
                    tgt = p_x if k < 4 else p_u
                    S.op("tensor", (lambda k=k, tc=tc, ttl=ttl, tgt=tgt: lambda e: e.transpose(
                        tgt[:, (k % 4) * 128:(k % 4 + 1) * 128],
                        X[:, k, tc * 512 + ttl * 128:tc * 512 + (ttl + 1) * 128],
                        idf[:, :]))(),
                        waits=[(ld, ld_c)] if k == 0 else [])
                pe_x = emit_drain(S, bufs, att_junk)
                S.op("vector", lambda e: e.tensor_add(acc[:, 0:512], acc[:, 0:512],
                                                      p_x[:, :]),
                     waits=[(pe, pe_x)])
                S.op("vector", lambda e: e.tensor_add(acc[:, 512:1024], acc[:, 512:1024],
                                                      p_u[:, :]), incs=[(dv, 1)])
                S.op("tensor", lambda e: None, waits=[(dv, dv.v)])
                for k6 in range(TOPK):
                    S.op("vector", (lambda gt=gt, k6=k6: lambda e: e.tensor_copy(
                        sidx[:, k6:k6 + 1], sinfo[:, gt, 6 + k6:7 + k6]))())
                S.op("vector", lambda e: e.tensor_copy(jnk[:, 0:128], jnk[:, 128:256]))
                S.op("vector", lambda e: e.tensor_copy(jnk[:, 0:128], jnk[:, 128:256]))
                S.op("vector", lambda e: e.tensor_copy(jnk[:, 0:128], jnk[:, 128:256]),
                     incs=[(dv, 1)])
                dv_idx = dv.v
                dv_stt = {}
                for k6 in range(TOPK):
                    b = k6 % 2
                    war = [(dv, dv_stt[k6 - 2])] if k6 >= 2 else []
                    S.op("gpsimd", (lambda k6=k6, b=b: lambda e: e.indirect_dma_start(
                        out=gbuf[:, b, :], out_offset=None,
                        in_=yb_full[:, :],
                        in_offset=IndirectOffsetOnAxis(sidx[:, k6:k6 + 1], 0),
                        bounds_check=_breg(e), oob_is_err=False))(),
                        waits=([(dv, dv_idx)] if k6 == 0 else []) + war,
                        incs=[(gq, 16)])
                    S.op("vector", (lambda gt=gt, k6=k6, b=b: lambda e: e.scalar_tensor_tensor(
                        acc[:, :], gbuf[:, b, :], sinfo[:, gt, k6:k6 + 1], acc[:, :],
                        ALU.mult, ALU.add))(),
                        waits=[(gq, gq.v)], incs=[(dv, 1)])
                    dv_stt[k6] = dv.v
                S.dma("sync", outr[t0 + (tc * 4 + ttl) * 128:t0 + (tc * 4 + ttl + 1) * 128, :],
                      acc[:, :], waits=[(dv, dv.v)], sem=st)
                S.op("vector", lambda e: None, waits=[(st, st.v)])

    S.op("sync", lambda e: None, waits=[(st, st.v)])
    S.build(nc, ctx)
    ctx.close()
    return nc



def host_consts():
    idhi = np.concatenate([np.eye(64, dtype=np.float32)] * 2, 0)
    idf = np.eye(128, dtype=np.float32)
    idb = np.eye(128).astype(BF)
    ic64 = np.broadcast_to(np.arange(E, dtype=np.float32), (128, E)).copy()
    i512 = np.broadcast_to(np.arange(512, dtype=np.float32), (128, 512)).copy()
    ut = (np.arange(128)[:, None] < np.arange(128)[None, :]).astype(np.float32)
    kk = np.arange(128)[:, None]
    m = np.arange(896)[None, :]
    cmask = np.where(m - 384 >= kk, 0.0, -1e9).astype(np.float32)
    return dict(idhi=idhi, idf=idf, idb=idb, ic64=ic64, i512=i512, ut128=ut, cmask=cmask)


def prep_stage3_weights(Wq, Wk, Wv, Wo, nw, bq, bk, bv, bo):
    qk, bqk = [], []
    for g in range(H // 2):
        for h in (2 * g, 2 * g + 1):
            qk.append(Wq[h * DH:(h + 1) * DH, :])
        for h in (2 * g, 2 * g + 1):
            qk.append(Wk[h * DH:(h + 1) * DH, :])
        bqk.extend([bq[2 * g * DH:(2 * g + 1) * DH], bq[(2 * g + 1) * DH:(2 * g + 2) * DH],
                    bk[2 * g * DH:(2 * g + 1) * DH], bk[(2 * g + 1) * DH:(2 * g + 2) * DH]])
    return dict(
        wqkT3=np.ascontiguousarray(np.concatenate(qk, 0).T),
        wvT3=np.ascontiguousarray(Wv.T),
        woT=np.ascontiguousarray(Wo.T),
        nw=np.ascontiguousarray(nw.reshape(8, 128).T),
        bqk=np.ascontiguousarray(np.concatenate(bqk)[None, :]),
        bv=np.ascontiguousarray(bv[None, :]),
        bo=np.ascontiguousarray(bo[None, :]),
    )


_BUILT = {}


def _get(name, builder):
    if name not in _BUILT:
        _BUILT[name] = builder()
    return _BUILT[name]


def kernel(L0_attn_z, L0_attn_W_qkv, L0_ffn_weight_q, L0_ffn_weight_scale,
           L0_ffn_bias, L1_attn_norm_w, L1_attn_Wq, L1_attn_bq, L1_attn_Wk,
           L1_attn_bk, L1_attn_Wv, L1_attn_bv, L1_attn_Wo, L1_attn_bo,
           L1_ffn_gate_w, L1_ffn_gate_b, L1_ffn_w1, L1_ffn_w2, L1_ffn_w3,
           L1_ffn_ws1, L1_ffn_ws2, L1_ffn_ws3):
    f32 = lambda a: np.asarray(a, dtype=np.float32)
    C = host_consts()
    cores = list(range(N_CORES))

    shards = shard_tokens_fm(f32(L0_attn_z))
    wqkT, wvT = prep_wqkv(f32(L0_attn_W_qkv))
    nc1 = _get("s1", build_stage1)
    r1 = run_bass_kernel_spmd(nc1, [
        {"x0T": shards[c], "wqkT": wqkT, "wvT": wvT, "idhi": C["idhi"]}
        for c in cores], core_ids=cores)
    x1 = [r1.results[c]["x1T"] for c in cores]

    wqTb = np.ascontiguousarray(np.asarray(L0_ffn_weight_q).T.astype(BF))
    wsc = np.ascontiguousarray(f32(L0_ffn_weight_scale).reshape(8, 128).T)
    fbc = np.ascontiguousarray(f32(L0_ffn_bias).reshape(8, 128).T)
    nc2 = _get("s2", build_stage2)
    r2 = run_bass_kernel_spmd(nc2, [
        {"x1T": x1[c], "wqTb": wqTb, "wscale": wsc, "fbias": fbc, "idf": C["idf"]}
        for c in cores], core_ids=cores)
    x2 = [r2.results[c]["x2T"] for c in cores]

    w3d = prep_stage3_weights(f32(L1_attn_Wq), f32(L1_attn_Wk), f32(L1_attn_Wv),
                              f32(L1_attn_Wo), f32(L1_attn_norm_w), f32(L1_attn_bq),
                              f32(L1_attn_bk), f32(L1_attn_bv), f32(L1_attn_bo))
    w3d["idhi"] = C["idhi"]
    w3d["cmask"] = C["cmask"]
    nc3 = _get("s3", build_stage3)
    r3 = run_bass_kernel_spmd(nc3, [dict(w3d, x2T=x2[c]) for c in cores], core_ids=cores)
    x3 = [r3.results[c]["x3T"] for c in cores]

    gwT = np.ascontiguousarray(f32(L1_ffn_gate_w).T)
    gbrow = np.ascontiguousarray(f32(L1_ffn_gate_b)[None, :])
    nc4 = _get("s4", build_stage4)
    r4 = run_bass_kernel_spmd(nc4, [
        {"x3T": x3[c], "gwT": gwT, "gbrow": gbrow, "ic64": C["ic64"],
         "ut128": C["ut128"], "idb": C["idb"]} for c in cores], core_ids=cores)
    xfT = [r4.results[c]["xfT"] for c in cores]
    xfr = [r4.results[c]["xfr"] for c in cores]
    cnts = np.concatenate([r4.results[c]["cnt"] for c in cores], 0)
    rinfo = [r4.results[c]["rinfo"] for c in cores]

    nc4b = _get("s4b", build_stage4b)
    in4b = []
    for c in cores:
        csel = (np.arange(8) < c).astype(np.float32)[:, None]
        tbase = (c * TOK + np.arange(128) + 1).astype(np.float32)[:, None]
        in4b.append({"cnts": cnts, "rinfo": rinfo[c], "csel": np.ascontiguousarray(csel),
                     "i512": C["i512"], "tbase": np.ascontiguousarray(tbase),
                     "ic64": C["ic64"]})
    r4b = run_bass_kernel_spmd(nc4b, in4b, core_ids=cores)
    mparts = [r4b.results[c]["mpart"] for c in cores]
    sinfo = [r4b.results[c]["sinfo"] for c in cores]

    xf_full = np.ascontiguousarray(np.concatenate(xfr, 0))
    w1 = f32(L1_ffn_w1); w2 = f32(L1_ffn_w2); w3 = f32(L1_ffn_w3)
    nc5 = _get("s5", build_stage5)
    in5 = []
    for c in cores:
        mloc = np.ascontiguousarray(
            np.stack([mp[c * 8:(c + 1) * 8] for mp in mparts], -1))
        w1T = np.zeros((8, D, I_PAD), np.float32)
        w1T[:, :, :I_EXP] = w1[c * 8:(c + 1) * 8].transpose(0, 2, 1)
        w3T = np.zeros((8, D, I_PAD), np.float32)
        w3T[:, :, :I_EXP] = w3[c * 8:(c + 1) * 8].transpose(0, 2, 1)
        w2T = np.zeros((8, I_PAD, D), np.float32)
        w2T[:, :I_EXP, :] = w2[c * 8:(c + 1) * 8].transpose(0, 2, 1)
        in5.append({"mloc": mloc, "xf_full": xf_full, "w1T": np.ascontiguousarray(w1T),
                    "w3T": np.ascontiguousarray(w3T), "w2T": np.ascontiguousarray(w2T),
                    "idb": C["idb"]})
    r5 = run_bass_kernel_spmd(nc5, in5, core_ids=cores)
    yb_full = np.ascontiguousarray(np.concatenate([r5.results[c]["yb"] for c in cores], 0))

    ws1T = np.ascontiguousarray(f32(L1_ffn_ws1).T)
    ws3T = np.ascontiguousarray(f32(L1_ffn_ws3).T)
    ws2T = np.ascontiguousarray(f32(L1_ffn_ws2).T)
    nc6 = _get("s6", build_stage6)
    r6 = run_bass_kernel_spmd(nc6, [
        {"x3T": x3[c], "xfT": xfT[c], "yb_full": yb_full, "sinfo": sinfo[c],
         "ws1T": ws1T, "ws3T": ws3T, "ws2T": ws2T, "idf": C["idf"]}
        for c in cores], core_ids=cores)
    out = np.concatenate([r6.results[c]["outr"].reshape(SEQ_PER_CORE, T, D)
                          for c in cores], 0)
    return out
